# version 17
# baseline (speedup 1.0000x reference)
import numpy as np
from contextlib import ExitStack
import concourse.bass as bass
import concourse.mybir as mybir
from concourse.bass_utils import run_bass_kernel_spmd

F32 = mybir.dt.float32
BF16 = mybir.dt.bfloat16
AF = mybir.ActivationFunctionType
ALU = mybir.AluOpType

TOK = 4096
N_CORES = 8
A1_LEVEL = 5
A1_PHI = "full"
A1_VTM = True
A1_TP = True
D = 1024
TILE = 512
NT = TOK // TILE

ENGINES = ("tensor", "vector", "scalar", "gpsimd", "sync")


class Sched:
    def __init__(self, nc):
        self.nc = nc
        self.q = {e: [] for e in ENGINES}
        self.sems = {}
        self.cnt = {}
        self.stack = ExitStack()
        self.bank_free = {}
        self.pending = {}

    def barrier(self, fence=None):
        if fence is not None:
            fa, fb = fence
            self.do("vector", lambda e: e.memset(fa[0:1, 0:1], 0.0), inc="tV")
            self.do("scalar", lambda e: e.copy(fb[0:1, 0:1], fa[0:1, 0:1]),
                    wait=[("tV", self.cnt["tV"])], inc="tS")
        snap = [(n, c) for n, c in self.cnt.items()
                if c > 0 and not n.startswith("pv")]
        for e in ENGINES:
            self.pending[e] = list(snap)

    def sem(self, name):
        if name not in self.sems:
            self.sems[name] = self.stack.enter_context(self.nc.semaphore(name))
            self.cnt[name] = 0
        return self.sems[name]

    def do(self, eng, fn, wait=(), inc=None, inc_k=1):
        wait = list(wait)
        if self.pending.get(eng):
            wait = self.pending.pop(eng) + wait
        sem_objs = [(self.sem(n), v) for n, v in wait if v > 0]
        tgt = None
        if inc is not None:
            self.sem(inc)
            self.cnt[inc] += inc_k
            tgt = (self.sems[inc], inc_k)

        def runner(e, sem_objs=sem_objs, fn=fn, tgt=tgt):
            for s_, v in sem_objs:
                e.wait_ge(s_, v)
            ins = fn(e)
            if tgt is not None:
                ins.then_inc(tgt[0], tgt[1])
        self.q[eng].append(runner)
        return self.cnt.get(inc) if inc else None

    def wait(self, eng, name, value):
        if value <= 0:
            return
        sem = self.sem(name)
        self.q[eng].append(lambda e: e.wait_ge(sem, value))

    def run(self):
        nc = self.nc
        with nc.Block() as block:
            for ename in ENGINES:
                def make(ename):
                    def body(e):
                        for f in self.q[ename]:
                            f(e)
                    return body
                getattr(block, ename)(make(ename))


def fm(x):
    return np.ascontiguousarray(np.asarray(x, dtype=np.float32).T)


def pc(v):
    v = np.asarray(v, dtype=np.float32)
    return np.ascontiguousarray(v.reshape(-1, 128).T)


def wT(w):
    return np.ascontiguousarray(np.asarray(w, dtype=np.float32).T)


def emit_ln(s, C, x_ap, xln, t_tok, first_wait=()):
    nF = 8
    s.do("vector", lambda e: e.tensor_copy(C["xb"][:, 0:nF * t_tok], x_ap),
         wait=first_wait, inc="tV")
    vc = s.cnt["tV"]
    for fc in range(nF):
        s.do("tensor", lambda e, fc=fc: e.matmul(
            C["ps_a"][0:1, 0:t_tok], C["ones_bf"][:, 0:1], C["xb"][:, fc * t_tok:(fc + 1) * t_tok],
            start=(fc == 0), stop=(fc == nF - 1)),
            wait=[("tV", vc)] if fc == 0 else (),
            inc="tP" if fc == nF - 1 else None)
    pe0 = s.cnt["tP"]
    s.do("scalar", lambda e: e.activation(C["xb"][:, 0:nF * t_tok], C["xb"][:, 0:nF * t_tok], AF.Square),
         wait=[("tV", vc), ("tP", pe0)], inc="tS")
    sc = s.cnt["tS"]
    for fc in range(nF):
        s.do("tensor", lambda e, fc=fc: e.matmul(
            C["ps_b"][0:1, 0:t_tok], C["ones_bf"][:, 0:1], C["xb"][:, fc * t_tok:(fc + 1) * t_tok],
            start=(fc == 0), stop=(fc == nF - 1)),
            wait=[("tS", sc)] if fc == 0 else (),
            inc="tP" if fc == nF - 1 else None)
    pe1 = s.cnt["tP"]
    s.do("vector", lambda e: e.tensor_scalar_mul(C["m_row"][0:1, 0:t_tok], C["ps_a"][0:1, 0:t_tok], 1.0 / D),
         wait=[("tP", pe1)], inc="tV")
    s.do("vector", lambda e: e.tensor_tensor(C["v_row"][0:1, 0:t_tok], C["m_row"][0:1, 0:t_tok],
                                             C["m_row"][0:1, 0:t_tok], ALU.mult),
         wait=[("tV", s.cnt["tV"])], inc="tV")
    s.do("vector", lambda e: e.scalar_tensor_tensor(
        C["v_row"][0:1, 0:t_tok], C["ps_b"][0:1, 0:t_tok], 1.0 / D, C["v_row"][0:1, 0:t_tok],
        ALU.mult, ALU.subtract), wait=[("tV", s.cnt["tV"])], inc="tV")
    vc2 = s.cnt["tV"]
    s.do("scalar", lambda e: e.activation(C["r_row"][0:1, 0:t_tok], C["v_row"][0:1, 0:t_tok],
                                          AF.Ln, bias=C["eps_v"][0:1, 0:1]), wait=[("tV", vc2)], inc="tS")
    s.do("scalar", lambda e: e.activation(C["r_row"][0:1, 0:t_tok], C["r_row"][0:1, 0:t_tok],
                                          AF.Exp, scale=-0.5), wait=[("tS", s.cnt["tS"])], inc="tS")
    sc2 = s.cnt["tS"]
    s.do("vector", lambda e: e.scalar_tensor_tensor(
        C["b_row"][0:1, 0:t_tok], C["m_row"][0:1, 0:t_tok], -1.0, C["r_row"][0:1, 0:t_tok],
        ALU.mult, ALU.mult), wait=[("tS", sc2)], inc="tV")
    vc3 = s.cnt["tV"]
    s.do("tensor", lambda e: e.matmul(C["ps_a"][:, 0:t_tok], C["ones_f"][0:1, :],
                                      C["r_row"][0:1, 0:t_tok], start=True, stop=True),
         wait=[("tS", sc2)])
    s.do("tensor", lambda e: e.matmul(C["ps_b"][:, 0:t_tok], C["ones_f"][0:1, :],
                                      C["b_row"][0:1, 0:t_tok], start=True, stop=True),
         wait=[("tV", vc3)], inc="tP")
    pe2 = s.cnt["tP"]
    tmp_w = {}
    tmp_r = {}

    def emit_tmp(fc):
        w = ([("tP", pe2)] if fc == 0 else [])
        if fc - 2 in tmp_r:
            w.append(("tV", tmp_r[fc - 2]))
        s.do("vector", lambda e, fc=fc: e.tensor_tensor(
            C["tmp"][fc % 2][:, 0:t_tok], x_ap[:, fc * t_tok:(fc + 1) * t_tok],
            C["ps_a"][:, 0:t_tok], ALU.mult), wait=w, inc="tV")
        tmp_w[fc] = s.cnt["tV"]

    def emit_xln(fc):
        s.do("vector", lambda e, fc=fc: e.tensor_tensor(
            xln[:, fc * t_tok:(fc + 1) * t_tok], C["tmp"][fc % 2][:, 0:t_tok],
            C["ps_b"][:, 0:t_tok], ALU.add), wait=[("tV", tmp_w[fc])], inc="tV")
        tmp_r[fc] = s.cnt["tV"]
    emit_tmp(0)
    for fc in range(1, nF):
        emit_tmp(fc)
        emit_xln(fc - 1)
    emit_xln(nF - 1)
    return s.cnt["tV"]


def common_tiles(s, nc, t_tok=TILE):
    sb = lambda name, shape, dt=F32: s.stack.enter_context(nc.sbuf_tensor(name, shape, dt))
    ps = lambda name, shape, dt=F32: s.stack.enter_context(nc.psum_tensor(name, shape, dt))
    C = {}
    C["xb"] = sb("c_xb", [128, 8 * t_tok], BF16)
    C["tmp"] = [sb(f"c_tmp{i}", [128, t_tok]) for i in range(2)]
    C["m_row"] = sb("c_m", [1, t_tok])
    C["v_row"] = sb("c_v", [1, t_tok])
    C["r_row"] = sb("c_r", [1, t_tok])
    C["b_row"] = sb("c_b", [1, t_tok])
    C["ones_bf"] = sb("c_1bf", [128, 1], BF16)
    C["ones_f"] = sb("c_1f", [1, 128])
    C["eps_v"] = sb("c_eps", [1, 1])
    s.do("vector", lambda e: e.memset(C["ones_bf"][:, 0:1], 1.0))
    s.do("vector", lambda e: e.memset(C["eps_v"][0:1, 0:1], 1e-5))
    s.do("vector", lambda e: e.memset(C["ones_f"][0:1, :], 1.0), inc="tV")
    return C



def emit_tp8(s, pTs, pT_name, id_bf, srcs, dsts, first_wait):
    tpc = {}

    def emit_t(fc):
        slot = fc % 2
        w = list(first_wait) if fc == 0 else []
        if (pT_name, slot) in s.bank_free:
            w.extend(s.bank_free[(pT_name, slot)])
        s.do("tensor", lambda e, fc=fc, slot=slot: e.transpose(
            pTs[slot][:, 0:128], srcs[fc], id_bf[:, :]),
            wait=w, inc="tP")
        tpc[fc] = s.cnt["tP"]

    def emit_c(fc):
        slot = fc % 2
        s.do("vector", lambda e, fc=fc, slot=slot: e.tensor_copy(
            dsts[fc], pTs[slot][:, 0:128]),
            wait=[("tP", tpc[fc])], inc="tV")
        s.bank_free[(pT_name, slot)] = [("tV", s.cnt["tV"])]
    emit_t(0)
    emit_t(1)
    for fc in range(2, 8):
        emit_c(fc - 2)
        emit_t(fc)
    emit_c(6)
    emit_c(7)
    return s.cnt["tV"]


def emit_preconv(s, io, scr, groups, pace=(0, 0)):
    pv = {}
    n_conv = sum(n_oc for _, n_oc in groups)
    tp0, tp1 = pace
    i = 0
    for stage, n_oc in groups:
        src = io[stage]
        for oc in range(n_oc):
            thr = tp0 + ((tp1 - tp0) * i) // n_conv
            w = [("tP", thr)] if thr > 0 else []
            s.do("gpsimd", lambda e, oc=oc, src=src, stage=stage: e.dma_start(
                out=scr[stage][oc], in_=src[oc]), wait=w, inc="pvw", inc_k=16)
            pv[(stage, oc)] = ("pvw", s.cnt["pvw"])
            i += 1
    return pv


def panP(w, kc_n, oc_n):
    return np.ascontiguousarray(
        np.asarray(w, np.float32).reshape(kc_n, 128, oc_n, 128)
        .transpose(2, 1, 0, 3).reshape(oc_n, 128, kc_n * 128))


def emit_f1(s, nc, io, C, pre="f1_"):
    x_in = io["x"]; ws = [io["w0"], io["w1"], io["w2"]]; bias = io["bias"]
    out = io["out"]
    stack = ExitStack()
    sb = lambda name, shape, dt=F32: stack.enter_context(nc.sbuf_tensor(pre + name, shape, dt))
    ps = lambda name, shape, dt=F32: stack.enter_context(nc.psum_tensor(pre + name, shape, dt))
    C = dict(C)
    C["ps_a"] = ps("lnps_a", [128, TILE])
    C["ps_b"] = ps("lnps_b", [128, TILE])
    x_t = [sb(f"x{i}", [128, 8 * TILE]) for i in range(2)]
    xln = sb("xln", [128, 8 * TILE], BF16)
    h = [[sb(f"h{i}_{b}", [128, 8 * TILE], BF16) for b in range(2)] for i in range(2)]
    o_t = [sb(f"o{i}", [128, 8 * TILE]) for i in range(2)]
    t_act = [sb(f"t_act{i}", [128, TILE]) for i in range(2)]
    t_act_free = {}
    ps_h = [ps(f"ps_h{i}", [128, TILE]) for i in range(2)]

    if "preW" in io:
        scrF1, pre_w = io["preW"]
        w_bf = [sb(f"w_bf{i}", [128, 8 * D], BF16) for i in range(3)]
        bias_pc = sb("bias_pc", [128, 8])
        dws = []
        for i in range(3):
            s.do("gpsimd", lambda e, i=i: e.dma_start(
                out=w_bf[i][:, 0:8 * D], in_=scrF1[i]),
                wait=(list(pre_w) if i == 0 else []), inc="dW", inc_k=16)
            dws.append(("dW", s.cnt["dW"]))
        s.do("sync", lambda e: e.dma_start(out=bias_pc[:, :], in_=bias), inc="dC", inc_k=16)
        f1_first = [dws[0], ("dC", s.cnt["dC"])]
        stage_dw = [[], [dws[1]], [dws[2]]]
    else:
        w_bf = [sb(f"w_bf{i}", [128, 8 * D], BF16) for i in range(3)]
        bias_pc = sb("bias_pc", [128, 8])
        for i in range(3):
            s.do("gpsimd", lambda e, i=i: e.dma_start(
                out=w_bf[i].ap().rearrange("p (kc m) -> p kc m", m=D),
                in_=ws[i].rearrange("(kc kp) m -> kp kc m", kp=128)),
                inc="dW", inc_k=16)
        f1_dw = s.cnt["dW"]
        s.do("sync", lambda e: e.dma_start(out=bias_pc[:, :], in_=bias), inc="dC", inc_k=16)
        f1_dc = s.cnt["dC"]
        stage_dw = [[], [], []]
        f1_first = [("dW", f1_dw), ("dC", f1_dc)]

    x_fm = x_in.rearrange("(fc fp) t -> fp fc t", fp=128)
    out_fm = out.rearrange("(fc fp) t -> fp fc t", fp=128)

    tile_done_v = {}
    for ti in range(NT):
        buf = ti % 2
        xw = [("tV", tile_done_v.get(ti - 2, 0))]
        s.do("sync", lambda e, ti=ti, buf=buf: e.dma_start(
            out=x_t[buf].ap().rearrange("p (fc t) -> p fc t", t=TILE),
            in_=x_fm[:, :, ti * TILE:(ti + 1) * TILE]),
            wait=xw, inc=f"dX{buf}", inc_k=16)
        dx = s.cnt[f"dX{buf}"]
        first = [(f"dX{buf}", dx)] + (f1_first if ti == 0 else [])
        vln = emit_ln(s, C, x_t[buf][:, :], xln, TILE, first_wait=first)

        srcs = [xln, h[0][buf], h[1][buf]]
        for stage in range(3):
            src = srcs[stage]
            stage_in_ready = s.cnt["tV"]
            for oc in range(8):
                pb = ps_h[oc % 2]
                pbname = f"ps_h{oc % 2}"
                for kc in range(8):
                    w = []
                    if kc == 0:
                        w.append(("tV", stage_in_ready if stage > 0 else vln))
                        if ti == 0:
                            w.extend(stage_dw[stage])
                        if pbname in s.bank_free:
                            w.extend(s.bank_free[pbname])
                    s.do("tensor", lambda e, stage=stage, oc=oc, kc=kc, pb=pb, src=src: e.matmul(
                        pb[:, :], w_bf[stage][:, kc * D + oc * 128: kc * D + oc * 128 + 128],
                        src[:, kc * TILE:(kc + 1) * TILE],
                        start=(kc == 0), stop=(kc == 7)),
                        wait=w, inc="tP" if kc == 7 else None)
                pe = s.cnt["tP"]
                if stage < 2:
                    s.do("vector", lambda e, stage=stage, oc=oc, pb=pb, buf=buf: e.tensor_copy(
                        h[stage][buf][:, oc * TILE:(oc + 1) * TILE], pb[:, :]),
                        wait=[("tP", pe)], inc="tV")
                    s.bank_free[pbname] = [("tV", s.cnt["tV"])]
                else:
                    ta = t_act[oc % 2]
                    sw = [("tP", pe)]
                    if (oc % 2) in t_act_free:
                        sw.append(t_act_free[oc % 2])
                    s.do("scalar", lambda e, oc=oc, pb=pb, ta=ta: e.activation(
                        ta[:, :], pb[:, :], AF.Tanh, bias=bias_pc[:, oc:oc + 1]),
                        wait=sw, inc="tS")
                    s.bank_free[pbname] = [("tS", s.cnt["tS"])]
                    stc = s.cnt["tS"]
                    rw = [("tS", stc)]
                    if oc == 0 and ti >= 2:
                        rw.append((f"sO{buf}", s.cnt.get(f"sO{buf}", 0)))
                    s.do("vector", lambda e, oc=oc, buf=buf, ta=ta: e.tensor_tensor(
                        o_t[buf][:, oc * TILE:(oc + 1) * TILE], ta[:, :],
                        x_t[buf][:, oc * TILE:(oc + 1) * TILE], ALU.add),
                        wait=rw, inc="tV")
                    t_act_free[oc % 2] = ("tV", s.cnt["tV"])
        tile_done_v[ti] = s.cnt["tV"]
        s.do("scalar", lambda e, ti=ti, buf=buf: e.dma_start(
            out=out_fm[:, :, ti * TILE:(ti + 1) * TILE],
            in_=o_t[buf].ap().rearrange("p (fc t) -> p fc t", t=TILE)),
            wait=[("tV", tile_done_v[ti])], inc=f"sO{buf}", inc_k=16)

    s.wait("scalar", "sO0", s.cnt["sO0"])
    s.wait("scalar", "sO1", s.cnt["sO1"])
    stack.close()


def build_f1():
    nc = bass.Bass("TRN2", target_bir_lowering=False)
    io = {}
    for nm, shape, dt, kind in (
            ("x", [D, TOK], F32, "ExternalInput"), ("w0", [D, D], F32, "ExternalInput"),
            ("w1", [D, D], F32, "ExternalInput"), ("w2", [D, D], F32, "ExternalInput"),
            ("bias", [128, 8], F32, "ExternalInput"), ("epsc", [1, 1], F32, "ExternalInput"),
            ("out", [D, TOK], F32, "ExternalOutput")):
        io[nm] = nc.dram_tensor(nm, shape, dt, kind=kind).ap()
    s = Sched(nc)
    C = common_tiles(s, nc)
    emit_f1(s, nc, io, C)
    s.run()
    s.stack.close()
    return nc


def f1_host(x3_fm_percore, weight, aux1, aux2, bias_):
    nc = build_f1()
    epsc = np.full((1, 1), 1e-5, np.float32)
    in_maps = [{
        "x": x3_fm_percore[c],
        "w0": np.ascontiguousarray(weight), "w1": np.ascontiguousarray(aux1),
        "w2": np.ascontiguousarray(aux2),
        "bias": pc(bias_), "epsc": epsc,
    } for c in range(N_CORES)]
    res = run_bass_kernel_spmd(nc, in_maps, core_ids=list(range(N_CORES)))
    return [res.results[c]["out"] for c in range(N_CORES)]


def emit_f0(s, nc, io, C, pre="f0_"):
    x_in = io["x"]; w1T = io.get("w1T"); w2T = io.get("w2T"); h1T = io.get("h1T")
    h2T = io.get("h2T"); w3T = io.get("w3T"); s12 = io["s12"]; sh12 = io["sh12"]
    s3 = io["s3"]; out = io["out"]
    pv = io.get("pv")
    scr = io.get("scr") or {
        "A": nc.dram_tensor(pre + "scrA", [32, 128, 2048], BF16).ap(),
        "H1": nc.dram_tensor(pre + "scrH1", [32, 128, 4096], BF16).ap(),
        "H2": nc.dram_tensor(pre + "scrH2", [32, 128, 4096], BF16).ap(),
        "W3": nc.dram_tensor(pre + "scrW3", [8, 128, 4096], BF16).ap(),
    }
    stack = ExitStack()
    sb = lambda name, shape, dt=F32: stack.enter_context(nc.sbuf_tensor(pre + name, shape, dt))
    ps = lambda name, shape, dt=F32: stack.enter_context(nc.psum_tensor(pre + name, shape, dt))
    C = dict(C)
    C["ps_a"] = ps("lnps_a", [128, TILE])
    C["ps_b"] = ps("lnps_b", [128, TILE])
    s12_pc = sb("s12_pc", [128, 64])
    sh12_pc = sb("sh12_pc", [128, 64])
    s3_pc = sb("s3_pc", [128, 8])
    x_t = [sb(f"x{i}", [128, 8 * TILE]) for i in range(2)]
    xln = sb("xln", [128, 8 * TILE], BF16)
    g0 = sb("g0", [128, 32 * TILE], BF16)
    g1 = sb("g1", [128, 32 * TILE], BF16)
    o_t = [sb(f"o{i}", [128, 8 * TILE]) for i in range(2)]
    t_act = [sb(f"t_act{i}", [128, TILE]) for i in range(2)]
    pA = [sb(f"pA{i}", [128, 2048], BF16) for i in range(2)]
    pH = [sb(f"pH{i}", [128, 4096], BF16) for i in range(2)]
    ps1 = [ps(f"f0ps1_{i}", [128, TILE]) for i in range(2)]
    ps2 = [ps(f"f0ps2_{i}", [128, TILE]) for i in range(2)]

    for dst, src in ((s12_pc[:, :], s12), (sh12_pc[:, :], sh12), (s3_pc[:, :], s3)):
        s.do("sync", lambda e, dst=dst, src=src: e.dma_start(out=dst, in_=src), inc="dC", inc_k=16)
    f0_dc = s.cnt["dC"]

    x_fm = x_in.rearrange("(fc fp) t -> fp fc t", fp=128)
    out_fm = out.rearrange("(fc fp) t -> fp fc t", fp=128)
    if pv is None:
        fsrc = {
            "A": None,
            "H1": h1T.rearrange("(kc kp) m -> kp kc m", kp=128),
            "H2": h2T.rearrange("(kc kp) m -> kp kc m", kp=128),
            "W3": w3T.rearrange("(kc kp) m -> kp kc m", kp=128),
        }
        w1f = w1T.rearrange("(kc kp) m -> kp kc m", kp=128)
        w2f = w2T.rearrange("(kc kp) m -> kp kc m", kp=128)

    plan = ([("A", oc) for oc in range(32)] + [("H1", oc) for oc in range(32)]
            + [("H2", oc) for oc in range(32)] + [("W3", oc) for oc in range(8)])

    panel_written = {}
    pan_free = {}
    scratch_dep = {}
    loads_dw = {}

    def issue_load(it, k):
        stage, oc = plan[k]
        bufs, nb = (pA, 2048) if stage == "A" else (pH, 4096)
        i = (k % 2)
        bid = ("A" if stage == "A" else "H") + str(i)
        sem = ("dA" if stage == "A" else "dH") + str(i) + ("c" if it == 0 else "")
        w = list(pan_free.get(bid, []))
        if pv is not None:
            if it == 0:
                w.append(pv[(stage, oc)])
            s.do("sync", lambda e, oc=oc, i=i, sc_=scr[stage], bufs=bufs, nb=nb: e.dma_start(
                out=bufs[i][:, 0:nb], in_=sc_[oc]), wait=w, inc=sem, inc_k=16)
            loads_dw[(it, stage, oc)] = (sem, s.cnt[sem])
            return
        if it == 0:
            if stage == "A":
                s.do("gpsimd", lambda e, oc=oc, i=i: e.dma_start(
                    out=pA[i].ap().rearrange("p (w kc m) -> p w kc m", w=2, m=128)[:, 0, :, :],
                    in_=w1f[:, :, oc * 128:(oc + 1) * 128]), wait=w, inc=sem, inc_k=16)
                s.do("gpsimd", lambda e, oc=oc, i=i: e.dma_start(
                    out=pA[i].ap().rearrange("p (w kc m) -> p w kc m", w=2, m=128)[:, 1, :, :],
                    in_=w2f[:, :, oc * 128:(oc + 1) * 128]), inc=sem, inc_k=16)
            else:
                s.do("gpsimd", lambda e, oc=oc, i=i, sf=fsrc[stage], bufs=bufs: e.dma_start(
                    out=bufs[i].ap().rearrange("p (kc m) -> p kc m", m=128),
                    in_=sf[:, :, oc * 128:(oc + 1) * 128]), wait=w, inc=sem, inc_k=16)
            dw = s.cnt[sem]
            dsem = "dS" + bid
            s.do("sync", lambda e, oc=oc, i=i, sc_=scr[stage], bufs=bufs, nb=nb: e.dma_start(
                out=sc_[oc], in_=bufs[i][:, 0:nb]),
                wait=[(sem, dw)], inc=dsem, inc_k=16)
            panel_written[(stage, oc)] = s.cnt[dsem]
            loads_dw[(it, stage, oc)] = (sem, dw)
            scratch_dep[bid] = (dsem, s.cnt[dsem])
        else:
            if (it, "first") not in loads_dw:
                w = w + [(nm, s.cnt.get(nm, 0)) for nm in
                         ("dSA0", "dSA1", "dSH0", "dSH1")]
                loads_dw[(it, "first")] = True
            s.do("sync", lambda e, oc=oc, i=i, sc_=scr[stage], bufs=bufs, nb=nb: e.dma_start(
                out=bufs[i][:, 0:nb], in_=sc_[oc]),
                wait=w, inc=sem, inc_k=16)
            loads_dw[(it, stage, oc)] = (sem, s.cnt[sem])

    tile_done_v = {}
    tile_order = list(io.get("tile_order") or range(NT))
    for it, ti in enumerate(tile_order):
        buf = it % 2
        s.do("sync", lambda e, ti=ti, buf=buf: e.dma_start(
            out=x_t[buf].ap().rearrange("p (fc t) -> p fc t", t=TILE),
            in_=x_fm[:, :, ti * TILE:(ti + 1) * TILE]),
            wait=[("tV", tile_done_v.get(it - 2, 0))], inc=f"dX{buf}", inc_k=16)
        dx = s.cnt[f"dX{buf}"]
        first = [(f"dX{buf}", dx)] + ([("dC", f0_dc)] if it == 0 else [])
        vln = emit_ln(s, C, x_t[buf][:, :], xln, TILE, first_wait=first)

        issue_load(it, 0)
        issue_load(it, 1)
        t_act_free = {}
        for k, (stage, oc) in enumerate(plan):
            i = k % 2
            bid = ("A" if stage == "A" else "H") + str(i)
            pbname = f"f0ps1_{oc % 2}"
            pb = ps1[oc % 2]
            dwk = loads_dw[(it, stage, oc)]
            if stage == "A":
                pb2 = ps2[oc % 2]
                pb2name = f"f0ps2_{oc % 2}"
                for kc in range(8):
                    w = []
                    if kc == 0:
                        w = [dwk, ("tV", vln)]
                        if pbname in s.bank_free:
                            w.extend(s.bank_free[pbname])
                    s.do("tensor", lambda e, kc=kc, pb=pb, i=i: e.matmul(
                        pb[:, :], pA[i][:, kc * 128:(kc + 1) * 128],
                        xln[:, kc * TILE:(kc + 1) * TILE],
                        start=(kc == 0), stop=(kc == 7)), wait=w)
                for kc in range(8):
                    w = []
                    if kc == 0 and pb2name in s.bank_free:
                        w = list(s.bank_free[pb2name])
                    s.do("tensor", lambda e, kc=kc, pb2=pb2, i=i: e.matmul(
                        pb2[:, :], pA[i][:, 1024 + kc * 128:1024 + (kc + 1) * 128],
                        xln[:, kc * TILE:(kc + 1) * TILE],
                        start=(kc == 0), stop=(kc == 7)), wait=w,
                        inc="tP" if kc == 7 else None)
                pe = s.cnt["tP"]
                pan_free[bid] = [("tP", pe)] + ([scratch_dep[bid]] if bid in scratch_dep else [])
                ta = t_act[oc % 2]
                sw = [("tP", pe)]
                if (oc % 2) in t_act_free:
                    sw.append(t_act_free[oc % 2])
                s.do("scalar", lambda e, oc=oc, pb=pb, ta=ta: e.activation(
                    ta[:, :], pb[:, :], AF.Silu, scale=s12_pc[:, oc:oc + 1]),
                    wait=sw, inc="tS")
                s.bank_free[pbname] = [("tS", s.cnt["tS"])]
                stc = s.cnt["tS"]
                s.do("vector", lambda e, oc=oc, pb2=pb2, ta=ta: e.scalar_tensor_tensor(
                    g0[:, oc * TILE:(oc + 1) * TILE], pb2[:, :], s12_pc[:, 32 + oc:32 + oc + 1],
                    ta[:, :], ALU.mult, ALU.mult),
                    wait=[("tS", stc)], inc="tV")
                s.bank_free[pb2name] = [("tV", s.cnt["tV"])]
                t_act_free[oc % 2] = ("tV", s.cnt["tV"])
                if k == 31:
                    g0_ready = s.cnt["tV"]
            elif stage in ("H1", "H2"):
                src_g = g0 if stage == "H1" else g1
                dst_g = g1 if stage == "H1" else g0
                ready = g0_ready if stage == "H1" else g1_ready
                for kc in range(32):
                    w = []
                    if kc == 0:
                        w = [dwk]
                        w.append(("tV", ready) if stage == "H1" else ("tS", ready))
                        if pbname in s.bank_free:
                            w.extend(s.bank_free[pbname])
                    s.do("tensor", lambda e, kc=kc, pb=pb, i=i, src_g=src_g: e.matmul(
                        pb[:, :], pH[i][:, kc * 128:(kc + 1) * 128],
                        src_g[:, kc * TILE:(kc + 1) * TILE],
                        start=(kc == 0), stop=(kc == 31)), wait=w,
                        inc="tP" if kc == 31 else None)
                pe = s.cnt["tP"]
                pan_free[bid] = [("tP", pe)] + ([scratch_dep[bid]] if bid in scratch_dep else [])
                s.do("scalar", lambda e, oc=oc, pb=pb, dst_g=dst_g, off=(0 if stage == "H1" else 32): e.activation(
                    dst_g[:, oc * TILE:(oc + 1) * TILE], pb[:, :], AF.Gelu,
                    scale=sh12_pc[:, off + oc:off + oc + 1]),
                    wait=[("tP", pe)], inc="tS")
                s.bank_free[pbname] = [("tS", s.cnt["tS"])]
                if k == 63:
                    g1_ready = s.cnt["tS"]
                if k == 95:
                    g2_ready = s.cnt["tS"]
            else:
                for kc in range(32):
                    w = []
                    if kc == 0:
                        w = [dwk, ("tS", g2_ready)]
                        if pbname in s.bank_free:
                            w.extend(s.bank_free[pbname])
                    s.do("tensor", lambda e, kc=kc, pb=pb, i=i: e.matmul(
                        pb[:, :], pH[i][:, kc * 128:(kc + 1) * 128],
                        g0[:, kc * TILE:(kc + 1) * TILE],
                        start=(kc == 0), stop=(kc == 31)), wait=w,
                        inc="tP" if kc == 31 else None)
                pe = s.cnt["tP"]
                pan_free[bid] = [("tP", pe)] + ([scratch_dep[bid]] if bid in scratch_dep else [])
                rw = [("tP", pe)]
                if oc == 0 and it >= 2:
                    rw.append((f"sO{buf}", s.cnt.get(f"sO{buf}", 0)))
                s.do("vector", lambda e, oc=oc, pb=pb, buf=buf: e.scalar_tensor_tensor(
                    o_t[buf][:, oc * TILE:(oc + 1) * TILE], pb[:, :], s3_pc[:, oc:oc + 1],
                    x_t[buf][:, oc * TILE:(oc + 1) * TILE], ALU.mult, ALU.add),
                    wait=rw, inc="tV")
                s.bank_free[pbname] = [("tV", s.cnt["tV"])]
            if k + 2 < len(plan):
                issue_load(it, k + 2)
        tile_done_v[it] = s.cnt["tV"]
        s.do("scalar", lambda e, ti=ti, buf=buf: e.dma_start(
            out=out_fm[:, :, ti * TILE:(ti + 1) * TILE],
            in_=o_t[buf].ap().rearrange("p (fc t) -> p fc t", t=TILE)),
            wait=[("tV", tile_done_v[it])], inc=f"sO{buf}", inc_k=16)
        if it == 0 and "halo_cb" in io:
            io["halo_cb"]([(f"sO{buf}", s.cnt[f"sO{buf}"])])

    s.wait("scalar", "sO0", s.cnt["sO0"])
    s.wait("scalar", "sO1", s.cnt["sO1"])
    stack.close()


def build_f0():
    nc = bass.Bass("TRN2", target_bir_lowering=False)
    io = {}
    for nm, shape, dt, kind in (
            ("x", [D, TOK], F32, "ExternalInput"), ("w1T", [D, 4096], F32, "ExternalInput"),
            ("w2T", [D, 4096], F32, "ExternalInput"), ("h1T", [4096, 4096], F32, "ExternalInput"),
            ("h2T", [4096, 4096], F32, "ExternalInput"), ("w3T", [4096, D], F32, "ExternalInput"),
            ("s12", [128, 64], F32, "ExternalInput"), ("sh12", [128, 64], F32, "ExternalInput"),
            ("s3", [128, 8], F32, "ExternalInput"), ("epsc", [1, 1], F32, "ExternalInput"),
            ("out", [D, TOK], F32, "ExternalOutput")):
        io[nm] = nc.dram_tensor(nm, shape, dt, kind=kind).ap()
    s = Sched(nc)
    C = common_tiles(s, nc)
    emit_f0(s, nc, io, C)
    s.run()
    s.stack.close()
    return nc


def f0_host(x1_fm_percore, w1_w, w1_s, w2_w, w2_s, h1_w, h1_s, h2_w, h2_s, w3_w, w3_s):
    nc = build_f0()
    epsc = np.full((1, 1), 1e-5, np.float32)
    s12_ = np.concatenate([pc(w1_s), pc(w2_s)], axis=1)
    sh12_ = np.concatenate([pc(h1_s), pc(h2_s)], axis=1)
    in_maps = [{
        "x": x1_fm_percore[c],
        "w1T": wT(w1_w), "w2T": wT(w2_w), "h1T": wT(h1_w), "h2T": wT(h2_w),
        "w3T": wT(w3_w),
        "s12": s12_, "sh12": sh12_, "s3": pc(w3_s), "epsc": epsc,
    } for c in range(N_CORES)]
    res = run_bass_kernel_spmd(nc, in_maps, core_ids=list(range(N_CORES)))
    return [res.results[c]["out"] for c in range(N_CORES)]


def emit_a1(s, nc, io, C, pre="a1_"):
    x_in = io["x"]; WqT = io["WqT"]; WkT = io["WkT"]; WvT = io["WvT"]
    bq = io["bq"]; bk = io["bk"]; bv_bc = io["bv_bc"]
    triu = io["triu"]; ident = io["ident"]
    qp_out = io["qp"]; num_out = io["num"]; den_out = io["den"]
    kv_out = io["kv"]; z_out = io["z"]
    stack = ExitStack()
    sb = lambda name, shape, dt=F32: stack.enter_context(nc.sbuf_tensor(pre + name, shape, dt))
    ps = lambda name, shape, dt=F32: stack.enter_context(nc.psum_tensor(pre + name, shape, dt))
    C = dict(C)
    C["ps_a"] = ps("lnps_a", [128, TILE])
    C["ps_b"] = ps("lnps_b", [128, TILE])
    w_bf = {k: sb(f"w_{k}", [128, 8 * D], BF16) for k in ("q", "k", "v")}
    bq_pc = sb("bq_pc", [128, 8])
    bk_pc = sb("bk_pc", [128, 8])
    bv_row = sb("bv_row", [128, 1024])
    triu_sb = sb("triu_sb", [128, 128])
    id_bf = sb("id_bf", [128, 128], BF16)
    x_t = [sb("x0", [128, 8 * TILE])]
    xln = sb("xln", [128, 8 * TILE], BF16)
    qp_sb = sb("qp_sb", [128, 8 * TILE], BF16)
    kp_sb = sb("kp_sb", [128, 8 * TILE], BF16)
    kp_tm = sb("kp_tm", [128, 4 * 1024], BF16)
    v_tm = sb("v_tm", [128, 4 * 1024], BF16)
    asb = [sb(f"asb{i}", [128, TILE], BF16) for i in range(4)]
    s_f32 = sb("s_f32", [128, 8 * 1024])
    s_bf = sb("s_bf", [128, 8 * 1024], BF16)
    z_f32 = sb("z_f32", [128, 8])
    z_bf = sb("z_bf", [128, 8], BF16)
    zt8 = sb("zt8", [128, 8])
    num_sb = [sb(f"num_sb{i}", [128, 1024], BF16) for i in range(2)]
    den_sb = sb("den_sb", [128, 32])
    t_r = [sb(f"t_r{i}", [128, TILE]) for i in range(2)]
    t_z = [sb(f"t_z{i}", [128, TILE]) for i in range(2)]
    t_e = [sb(f"t_e{i}", [128, TILE]) for i in range(2)]

    p = [ps(f"a1p{i}", [128, TILE]) for i in range(4)]
    pT = [ps(f"a1pT{i}", [128, 1024], BF16) for i in range(2)]

    for k, src in (("q", WqT), ("k", WkT), ("v", WvT)):
        s.do("gpsimd", lambda e, k=k, src=src: e.dma_start(
            out=w_bf[k].ap().rearrange("p (kc m) -> p kc m", m=D),
            in_=src.rearrange("(kc kp) m -> kp kc m", kp=128)), inc="dW", inc_k=16)
    s.do("gpsimd", lambda e: e.dma_start(out=id_bf[:, :], in_=ident), inc="dW", inc_k=16)
    a1_dw = s.cnt["dW"]
    for dst, src in ((bq_pc[:, :], bq), (bk_pc[:, :], bk), (bv_row[:, :], bv_bc),
                     (triu_sb[:, :], triu)):
        s.do("sync", lambda e, dst=dst, src=src: e.dma_start(out=dst, in_=src), inc="dC", inc_k=16)
    a1_dc = s.cnt["dC"]
    s.do("vector", lambda e: e.memset(s_f32[:, :], 0.0))
    s.do("vector", lambda e: e.memset(s_bf[:, :], 0.0))
    s.do("vector", lambda e: e.memset(z_f32[:, :], 0.0))
    s.do("vector", lambda e: e.memset(z_bf[:, :], 0.0))
    s.do("vector", lambda e: e.memset(den_sb[:, :], 0.0), inc="tV")

    x_fm = x_in.rearrange("(fc fp) t -> fp fc t", fp=128)
    qp_fm = qp_out.rearrange("(fc fp) t -> fp fc t", fp=128)

    def phi_proj(which, dst, b_pc, free_guard):
        for oc in range(8):
            pb = p[oc % 2]
            pbn = f"a1p{oc % 2}"
            for kc in range(8):
                w = []
                if kc == 0:
                    w = list(free_guard)
                    if pbn in s.bank_free:
                        w.extend(s.bank_free[pbn])
                s.do("tensor", lambda e, oc=oc, kc=kc, pb=pb, which=which: e.matmul(
                    pb[:, :], w_bf[which][:, kc * D + oc * 128: kc * D + oc * 128 + 128],
                    xln[:, kc * TILE:(kc + 1) * TILE],
                    start=(kc == 0), stop=(kc == 7)), wait=w,
                    inc="tP" if kc == 7 else None)
            pe = s.cnt["tP"]
            i = oc % 2
            if A1_PHI == "copy":
                s.do("vector", lambda e, pb=pb, oc=oc, dst=dst: e.tensor_copy(
                    dst[:, oc * TILE:(oc + 1) * TILE], pb[:, :]),
                    wait=[("tP", pe)], inc="tV")
                s.bank_free[pbn] = [("tV", s.cnt["tV"])]
                continue
            sw = [("tP", pe)]
            if ("t_r", i) in s.bank_free:
                sw.extend(s.bank_free[("t_r", i)])
            s.do("scalar", lambda e, pb=pb, oc=oc, i=i, b_pc=b_pc: e.activation(
                t_r[i][:, :], pb[:, :], AF.Relu, bias=b_pc[:, oc:oc + 1]),
                wait=sw, inc="tS")
            relu_c = s.cnt["tS"]
            vw = [("tP", pe), ("tS", relu_c)]
            if ("t_z", i) in s.bank_free:
                vw.extend(s.bank_free[("t_z", i)])
            s.do("vector", lambda e, pb=pb, oc=oc, i=i, b_pc=b_pc: e.tensor_scalar(
                t_z[i][:, :], pb[:, :], b_pc[:, oc:oc + 1], 0.0, ALU.add, ALU.min),
                wait=vw, inc="tV")
            s.bank_free[pbn] = [("tS", relu_c), ("tV", s.cnt["tV"])]
            ew = [("tV", s.cnt["tV"])]
            if ("t_e", i) in s.bank_free:
                ew.extend(s.bank_free[("t_e", i)])
            s.do("scalar", lambda e, i=i: e.activation(t_e[i][:, :], t_z[i][:, :], AF.Exp),
                 wait=ew, inc="tS")
            s.bank_free[("t_z", i)] = [("tS", s.cnt["tS"])]
            exp_c = s.cnt["tS"]
            s.do("vector", lambda e, i=i, oc=oc, dst=dst: e.tensor_tensor(
                dst[:, oc * TILE:(oc + 1) * TILE], t_e[i][:, :], t_r[i][:, :], ALU.add),
                wait=[("tS", exp_c)], inc="tV")
            s.bank_free[("t_r", i)] = [("tV", s.cnt["tV"])]
            s.bank_free[("t_e", i)] = [("tV", s.cnt["tV"])]
        return s.cnt["tV"]

    tile_done_v = {}
    for sc_i in range(NT):
        buf = 0
        s.do("sync", lambda e, sc_i=sc_i, buf=buf: e.dma_start(
            out=x_t[buf].ap().rearrange("p (fc t) -> p fc t", t=TILE),
            in_=x_fm[:, :, sc_i * TILE:(sc_i + 1) * TILE]),
            wait=[("tV", tile_done_v.get(sc_i - 1, 0))], inc="dX0", inc_k=16)
        dx = s.cnt["dX0"]
        first = [("dX0", dx)] + ([("dW", a1_dw), ("dC", a1_dc)] if sc_i == 0 else [])
        vln = emit_ln(s, C, x_t[buf][:, :], xln, TILE, first_wait=first)
        tile_done_v[sc_i] = vln

        if A1_LEVEL >= 1:
            qg = [("tV", vln)] + ([("sQ", 16 * sc_i)] if sc_i > 0 else [])
            vq = phi_proj("q", qp_sb, bq_pc, qg)
            vk = phi_proj("k", kp_sb, bk_pc, [("tV", vq)])
        else:
            vq = vk = s.cnt["tV"]

        if A1_LEVEL >= 1:
            s.do("scalar", lambda e, sc_i=sc_i: e.dma_start(
                out=qp_fm[:, :, sc_i * TILE:(sc_i + 1) * TILE],
                in_=qp_sb.ap().rearrange("p (fc t) -> p fc t", t=TILE)),
                wait=[("tV", vq)], inc="sQ", inc_k=16)

        for c in range(4 if A1_LEVEL >= 2 else 0):
            for half in range(2 if A1_VTM else 0):
                pb = p[2 + half]
                pbn = f"a1p{2 + half}"
                for kc in range(8):
                    w = []
                    if kc == 0:
                        w = [("tV", vk)]
                        if pbn in s.bank_free:
                            w.extend(s.bank_free[pbn])
                    s.do("tensor", lambda e, c=c, kc=kc, pb=pb, half=half: e.matmul(
                        pb[:, :], xln[:, kc * TILE + c * 128: kc * TILE + c * 128 + 128],
                        w_bf["v"][:, kc * D + half * TILE: kc * D + (half + 1) * TILE],
                        start=(kc == 0), stop=(kc == 7)), wait=w,
                        inc="tP" if kc == 7 else None)
                pe = s.cnt["tP"]
                s.do("vector", lambda e, c=c, half=half, pb=pb: e.tensor_tensor(
                    v_tm[:, c * 1024 + half * TILE: c * 1024 + (half + 1) * TILE],
                    pb[:, :], bv_row[:, half * TILE:(half + 1) * TILE], ALU.add),
                    wait=[("tP", pe)], inc="tV")
                s.bank_free[pbn] = [("tV", s.cnt["tV"])]
            if A1_TP:
                emit_tp8(
                    s, pT, "a1pT", id_bf,
                    [kp_sb[:, fc * TILE + c * 128: fc * TILE + c * 128 + 128] for fc in range(8)],
                    [kp_tm[:, c * 1024 + fc * 128: c * 1024 + (fc + 1) * 128] for fc in range(8)],
                    [("tV", vk)])
        v_ready = s.cnt["tV"]

        for sch in range(4 if A1_LEVEL >= 3 else 0):
            n = TILE - sch * 128
            pb = p[sch % 2]
            pbn = f"a1p{sch % 2}"
            for fc in range(8):
                w = []
                if fc == 0:
                    w = [("tV", v_ready)]
                    if pbn in s.bank_free:
                        w.extend(s.bank_free[pbn])
                s.do("tensor", lambda e, sch=sch, fc=fc, pb=pb, n=n: e.matmul(
                    pb[:, 0:n], kp_sb[:, fc * TILE + sch * 128: fc * TILE + (sch + 1) * 128],
                    qp_sb[:, fc * TILE + sch * 128: fc * TILE + TILE],
                    start=(fc == 0), stop=(fc == 7)), wait=w,
                    inc="tP" if fc == 7 else None)
            pe = s.cnt["tP"]
            s.do("vector", lambda e, sch=sch, pb=pb: e.tensor_tensor(
                asb[sch][:, sch * 128:(sch + 1) * 128], pb[:, 0:128], triu_sb[:, :], ALU.mult),
                wait=[("tP", pe)], inc="tV")
            if sch < 3:
                s.do("vector", lambda e, sch=sch, pb=pb, n=n: e.tensor_copy(
                    asb[sch][:, (sch + 1) * 128: TILE], pb[:, 128:n]), inc="tV")
            s.bank_free[pbn] = [("tV", s.cnt["tV"])]
        a_ready = s.cnt["tV"]

        for qc in range(4 if A1_LEVEL >= 4 else 0):
            g = sc_i * 4 + qc
            k = 0
            for sch in range(qc + 1):
                w = []
                if k == 0:
                    w = [("tV", a_ready)]
                    if "a1p0" in s.bank_free:
                        w.extend(s.bank_free["a1p0"])
                s.do("tensor", lambda e, sch=sch, qc=qc, k=k: e.matmul(
                    p[0][:, 0:1], asb[sch][:, qc * 128:(qc + 1) * 128], C["ones_bf"][:, 0:1],
                    start=(k == 0), stop=False), wait=w)
                k += 1
            for fc in range(8):
                s.do("tensor", lambda e, fc=fc, qc=qc: e.matmul(
                    p[0][:, 0:1], qp_sb[:, fc * TILE + qc * 128: fc * TILE + (qc + 1) * 128],
                    z_bf[:, fc:fc + 1], start=False, stop=(fc == 7)),
                    inc="tP" if fc == 7 else None)
            pe_den = s.cnt["tP"]
            s.do("vector", lambda e, g=g: e.tensor_copy(den_sb[:, g:g + 1], p[0][:, 0:1]),
                 wait=[("tP", pe_den)], inc="tV")
            s.bank_free["a1p0"] = [("tV", s.cnt["tV"])]
            nbuf = num_sb[qc % 2]
            for half in range(2):
                pb = p[2 + half]
                pbn = f"a1p{2 + half}"
                k = 0
                tot = (qc + 1) + 8
                for sch in range(qc + 1):
                    w = []
                    if k == 0:
                        w = []
                        if pbn in s.bank_free:
                            w.extend(s.bank_free[pbn])
                    s.do("tensor", lambda e, sch=sch, qc=qc, pb=pb, half=half, k=k: e.matmul(
                        pb[:, :], asb[sch][:, qc * 128:(qc + 1) * 128],
                        v_tm[:, sch * 1024 + half * TILE: sch * 1024 + (half + 1) * TILE],
                        start=(k == 0), stop=False), wait=w)
                    k += 1
                for fc in range(8):
                    s.do("tensor", lambda e, fc=fc, qc=qc, pb=pb, half=half: e.matmul(
                        pb[:, :], qp_sb[:, fc * TILE + qc * 128: fc * TILE + (qc + 1) * 128],
                        s_bf[:, fc * 1024 + half * TILE: fc * 1024 + (half + 1) * TILE],
                        start=False, stop=(fc == 7)),
                        inc="tP" if fc == 7 else None)
                pe_num = s.cnt["tP"]
                w = [("tP", pe_num)]
                if ("num", qc % 2) in s.bank_free:
                    w.extend(s.bank_free[("num", qc % 2)])
                s.do("vector", lambda e, pb=pb, half=half, nbuf=nbuf: e.tensor_copy(
                    nbuf[:, half * TILE:(half + 1) * TILE], pb[:, :]), wait=w, inc="tV")
                s.bank_free[pbn] = [("tV", s.cnt["tV"])]
            nv = s.cnt["tV"]
            nsem = f"sN{qc % 2}"
            s.do("scalar", lambda e, g=g, nbuf=nbuf: e.dma_start(
                out=num_out[g * 128:(g + 1) * 128, :], in_=nbuf[:, :]),
                wait=[("tV", nv)], inc=nsem, inc_k=16)
            s.bank_free[("num", qc % 2)] = [(nsem, s.cnt[nsem])]

        for dc in range(8 if A1_LEVEL >= 5 else 0):
            for half in range(2):
                pb = p[half]
                pbn = f"a1p{half}"
                for c in range(4):
                    w = []
                    if c == 0:
                        w = []
                        if pbn in s.bank_free:
                            w.extend(s.bank_free[pbn])
                    s.do("tensor", lambda e, dc=dc, c=c, pb=pb, half=half: e.matmul(
                        pb[:, :], kp_tm[:, c * 1024 + dc * 128:c * 1024 + (dc + 1) * 128],
                        v_tm[:, c * 1024 + half * TILE: c * 1024 + (half + 1) * TILE],
                        start=(c == 0), stop=(c == 3)), wait=w,
                        inc="tP" if c == 3 else None)
                pe = s.cnt["tP"]
                s.do("vector", lambda e, dc=dc, half=half, pb=pb: e.tensor_tensor(
                    s_f32[:, dc * 1024 + half * TILE: dc * 1024 + (half + 1) * TILE],
                    s_f32[:, dc * 1024 + half * TILE: dc * 1024 + (half + 1) * TILE],
                    pb[:, :], ALU.add), wait=[("tP", pe)], inc="tV")
                s.do("vector", lambda e, dc=dc, half=half: e.tensor_copy(
                    s_bf[:, dc * 1024 + half * TILE: dc * 1024 + (half + 1) * TILE],
                    s_f32[:, dc * 1024 + half * TILE: dc * 1024 + (half + 1) * TILE]),
                    wait=[("tV", s.cnt["tV"])], inc="tV")
                s.bank_free[pbn] = [("tV", s.cnt["tV"])]

        zw = [("tP", s.cnt["tP"])]
        if A1_LEVEL >= 5:
            for fc in range(8):
                s.do("vector", lambda e, fc=fc: e.reduce_sum(
                    zt8[:, fc:fc + 1], kp_sb[:, fc * TILE:(fc + 1) * TILE],
                    mybir.AxisListType.X),
                    wait=(zw if fc == 0 else []), inc="tV")
            s.do("vector", lambda e: e.tensor_tensor(
                z_f32[:, 0:8], z_f32[:, 0:8], zt8[:, 0:8], ALU.add),
                wait=[("tV", s.cnt["tV"])], inc="tV")
            s.do("vector", lambda e: e.tensor_copy(z_bf[:, 0:8], z_f32[:, 0:8]),
                 wait=[("tV", s.cnt["tV"])], inc="tV")

    vfin = s.cnt["tV"]
    s.do("scalar", lambda e: e.dma_start(out=den_out, in_=den_sb[:, :]),
         wait=[("tV", vfin)], inc="sO", inc_k=16)
    s.do("gpsimd" if kv_out.tensor.dtype == BF16 else "scalar", lambda e: e.dma_start(
        out=kv_out.rearrange("(dc dp) m -> dp dc m", dp=128),
        in_=s_f32.ap().rearrange("p (dc m) -> p dc m", m=1024)),
        wait=[("tV", vfin)], inc="sO", inc_k=16)
    s.do("scalar", lambda e: e.dma_start(out=z_out, in_=z_f32[:, :]), inc="sO", inc_k=16)
    s.wait("scalar", "sO", s.cnt["sO"])
    s.wait("scalar", "sQ", s.cnt.get("sQ", 0))
    s.wait("scalar", "sN0", s.cnt.get("sN0", 0))
    s.wait("scalar", "sN1", s.cnt.get("sN1", 0))
    stack.close()


def build_a1():
    nc = bass.Bass("TRN2", target_bir_lowering=False)
    io = {}
    for nm, shape, dt, kind in (
            ("x", [D, TOK], F32, "ExternalInput"), ("WqT", [D, D], F32, "ExternalInput"),
            ("WkT", [D, D], F32, "ExternalInput"), ("WvT", [D, D], F32, "ExternalInput"),
            ("bq", [128, 8], F32, "ExternalInput"), ("bk", [128, 8], F32, "ExternalInput"),
            ("bv_bc", [128, 1024], F32, "ExternalInput"),
            ("triu", [128, 128], F32, "ExternalInput"), ("ident", [128, 128], F32, "ExternalInput"),
            ("epsc", [1, 1], F32, "ExternalInput"),
            ("qp", [D, TOK], BF16, "ExternalOutput"), ("num", [TOK, D], BF16, "ExternalOutput"),
            ("den", [128, 32], F32, "ExternalOutput"), ("kv", [D, D], F32, "ExternalOutput"),
            ("z", [128, 8], F32, "ExternalOutput")):
        io[nm] = nc.dram_tensor(nm, shape, dt, kind=kind).ap()
    s = Sched(nc)
    C = common_tiles(s, nc)
    emit_a1(s, nc, io, C)
    s.run()
    s.stack.close()
    return nc


def emit_a2(s, nc, io, C, pre="a2_"):
    x_in = io["x"]; qp_in = io["qp"]; num_in = io["num"]; den_in = io["den"]
    kvp_in = io["kvp"]; zp_in = io["zp"]; WoT = io["WoT"]; bo = io["bo"]
    ident = io["ident"]; out = io["out"]
    stack = ExitStack()
    sb = lambda name, shape, dt=F32: stack.enter_context(nc.sbuf_tensor(pre + name, shape, dt))
    ps = lambda name, shape, dt=F32: stack.enter_context(nc.psum_tensor(pre + name, shape, dt))
    wo_bf = sb("wo_bf", [128, 8 * D], BF16)
    kvp_bf = sb("kvp_bf", [128, 8 * D], BF16)
    zp_bf = sb("zp_bf", [128, 8], BF16)
    bo_pc = sb("bo_pc", [128, 8])
    id_bf = sb("id_bf", [128, 128], BF16)
    x_t = [sb(f"x{i}", [128, 8 * TILE]) for i in range(2)]
    qp_t = [sb(f"qp{i}", [128, 8 * TILE], BF16) for i in range(2)]
    num_t = [sb(f"num{i}", [128, 1024], BF16) for i in range(2)]
    den_t = sb("den_t", [128, 32])
    dfull = sb("dfull", [128, 1])
    rden = sb("rden", [128, 1])
    t0 = sb("t0", [128, 1024])
    o_tm = sb("o_tm", [128, 1024], BF16)
    o_fm = sb("o_fm", [128, 8 * TILE], BF16)
    o_t = [sb(f"o{i}", [128, 8 * TILE]) for i in range(2)]

    p = [ps(f"a2p{i}", [128, TILE]) for i in range(6)]
    pT = [ps(f"a2pT{i}", [128, 1024], BF16) for i in range(2)]

    s.do("gpsimd", lambda e: e.dma_start(
        out=wo_bf.ap().rearrange("p (kc m) -> p kc m", m=D),
        in_=WoT.rearrange("(kc kp) m -> kp kc m", kp=128)), inc="dW", inc_k=16)
    s.do("gpsimd", lambda e: e.dma_start(
        out=kvp_bf.ap().rearrange("p (kc m) -> p kc m", m=D),
        in_=kvp_in.rearrange("(kc kp) m -> kp kc m", kp=128)), inc="dW", inc_k=16)
    s.do("gpsimd", lambda e: e.dma_start(out=zp_bf[:, :], in_=zp_in), inc="dW", inc_k=16)
    s.do("gpsimd", lambda e: e.dma_start(out=id_bf[:, :], in_=ident), inc="dW", inc_k=16)
    a2_dw = s.cnt["dW"]
    for dst, src in ((bo_pc[:, :], bo), (den_t[:, :], den_in)):
        s.do("sync", lambda e, dst=dst, src=src: e.dma_start(out=dst, in_=src), inc="dC", inc_k=16)
    a2_dc = s.cnt["dC"]
    if "mask" in io:
        mask_pc = sb("mask_pc", [128, 1])
        s.do("sync", lambda e: e.dma_start(out=mask_pc[:, 0:1], in_=io["mask"]), inc="dC", inc_k=16)
        a2_dc = s.cnt["dC"]
        s.do("vector", lambda e: e.tensor_scalar_mul(kvp_bf[:, :], kvp_bf[:, :], mask_pc[:, 0:1]),
             wait=[("dW", a2_dw), ("dC", a2_dc)], inc="tV")
        s.do("vector", lambda e: e.tensor_scalar_mul(zp_bf[:, :], zp_bf[:, :], mask_pc[:, 0:1]),
             wait=[("tV", s.cnt["tV"])], inc="tV")
        a2_mask_v = s.cnt["tV"]
    else:
        a2_mask_v = 0

    x_fm = x_in.rearrange("(fc fp) t -> fp fc t", fp=128)
    qp_fm = qp_in.rearrange("(fc fp) t -> fp fc t", fp=128)
    out_fm = out.rearrange("(fc fp) t -> fp fc t", fp=128)

    tile_done_v = {}
    for sc_i in range(NT):
        buf = sc_i % 2
        prev = [("tV", tile_done_v.get(sc_i - 2, 0))]
        s.do("sync", lambda e, sc_i=sc_i, buf=buf: e.dma_start(
            out=x_t[buf].ap().rearrange("p (fc t) -> p fc t", t=TILE),
            in_=x_fm[:, :, sc_i * TILE:(sc_i + 1) * TILE]), wait=prev, inc=f"dX{buf}", inc_k=16)
        s.do("sync", lambda e, sc_i=sc_i, buf=buf: e.dma_start(
            out=qp_t[buf].ap().rearrange("p (fc t) -> p fc t", t=TILE),
            in_=qp_fm[:, :, sc_i * TILE:(sc_i + 1) * TILE]), inc=f"dQ{buf}", inc_k=16)
        dxx = s.cnt[f"dX{buf}"]
        dqq = s.cnt[f"dQ{buf}"]
        for c in range(4):
            g = sc_i * 4 + c
            nb = num_t[c % 2]
            nw = [s.bank_free.get(("numt", c % 2), ("tV", 0))]
            s.do("sync", lambda e, g=g, nb=nb: e.dma_start(
                out=nb[:, :], in_=num_in[g * 128:(g + 1) * 128, :]),
                wait=[w for w in nw if w[1] > 0], inc=f"dN{c % 2}", inc_k=16)
            dxc = (f"dN{c % 2}", s.cnt[f"dN{c % 2}"])
            pp = (p[0], p[1]) if c % 2 == 0 else (p[2], p[3])
            ppn = ("a2p0", "a2p1") if c % 2 == 0 else ("a2p2", "a2p3")
            for half in range(2):
                pb = pp[half]
                for fc in range(8):
                    w = []
                    if fc == 0:
                        w = [(f"dQ{buf}", dqq), ("dW", a2_dw), ("tV", a2_mask_v)]
                        if ppn[half] in s.bank_free:
                            w.extend(s.bank_free[ppn[half]])
                    s.do("tensor", lambda e, c=c, fc=fc, pb=pb, half=half, buf=buf: e.matmul(
                        pb[:, :], qp_t[buf][:, fc * TILE + c * 128: fc * TILE + (c + 1) * 128],
                        kvp_bf[:, fc * D + half * TILE: fc * D + (half + 1) * TILE],
                        start=(fc == 0), stop=(fc == 7)), wait=w,
                        inc="tP" if fc == 7 else None)
            for fc in range(8):
                w = []
                if fc == 0 and "a2p4" in s.bank_free:
                    w = list(s.bank_free["a2p4"])
                s.do("tensor", lambda e, c=c, fc=fc, buf=buf: e.matmul(
                    p[4][:, 0:1], qp_t[buf][:, fc * TILE + c * 128: fc * TILE + (c + 1) * 128],
                    zp_bf[:, fc:fc + 1], start=(fc == 0), stop=(fc == 7)),
                    wait=w, inc="tP" if fc == 7 else None)
            pe = s.cnt["tP"]
            s.do("vector", lambda e, g=g: e.scalar_tensor_tensor(
                dfull[:, 0:1], p[4][:, 0:1], 1e-6, den_t[:, g:g + 1], ALU.add, ALU.add),
                wait=[("tP", pe), ("dC", a2_dc)], inc="tV")
            s.bank_free["a2p4"] = [("tV", s.cnt["tV"])]
            s.do("vector", lambda e: e.reciprocal(rden[:, 0:1], dfull[:, 0:1]),
                 wait=[("tV", s.cnt["tV"])], inc="tV")
            t0_w = {}
            for half in range(2):
                s.do("vector", lambda e, half=half, nb=nb, pp=pp: e.tensor_tensor(
                    t0[:, half * TILE:(half + 1) * TILE], pp[half][:, :],
                    nb[:, half * TILE:(half + 1) * TILE], ALU.add),
                    wait=[dxc] if half == 0 else (), inc="tV")
                t0_w[half] = s.cnt["tV"]
            for half in range(2):
                s.do("vector", lambda e, half=half: e.tensor_scalar_mul(
                    o_tm[:, half * TILE:(half + 1) * TILE],
                    t0[:, half * TILE:(half + 1) * TILE], rden[:, 0:1]),
                    wait=[("tV", t0_w[half])], inc="tV")
                s.bank_free[ppn[half]] = [("tV", s.cnt["tV"])]
            s.bank_free[("numt", c % 2)] = ("tV", s.cnt["tV"])
            otm_v = s.cnt["tV"]
            emit_tp8(
                s, pT, "a2pT", id_bf,
                [o_tm[:, fc * 128:(fc + 1) * 128] for fc in range(8)],
                [o_fm[:, fc * TILE + c * 128: fc * TILE + (c + 1) * 128] for fc in range(8)],
                [("tV", otm_v)])
        ofm_v = s.cnt["tV"]
        for oc in range(8):
            pb = p[4 + (oc % 2)]
            pbn = f"a2p{4 + (oc % 2)}"
            for kc in range(8):
                w = []
                if kc == 0:
                    w = [("tV", ofm_v)]
                    if pbn in s.bank_free:
                        w.extend(s.bank_free[pbn])
                s.do("tensor", lambda e, oc=oc, kc=kc, pb=pb: e.matmul(
                    pb[:, :], wo_bf[:, kc * D + oc * 128: kc * D + oc * 128 + 128],
                    o_fm[:, kc * TILE:(kc + 1) * TILE],
                    start=(kc == 0), stop=(kc == 7)), wait=w,
                    inc="tP" if kc == 7 else None)
            pe = s.cnt["tP"]
            rw = [("tP", pe), (f"dX{buf}", dxx)]
            if oc == 0 and sc_i >= 2:
                rw.append((f"sO{buf}", s.cnt.get(f"sO{buf}", 0)))
            s.do("vector", lambda e, oc=oc, pb=pb, buf=buf: e.scalar_tensor_tensor(
                o_t[buf][:, oc * TILE:(oc + 1) * TILE], pb[:, :], bo_pc[:, oc:oc + 1],
                x_t[buf][:, oc * TILE:(oc + 1) * TILE], ALU.add, ALU.add),
                wait=rw, inc="tV")
            s.bank_free[pbn] = [("tV", s.cnt["tV"])]
        tile_done_v[sc_i] = s.cnt["tV"]
        s.do("scalar", lambda e, sc_i=sc_i, buf=buf: e.dma_start(
            out=out_fm[:, :, sc_i * TILE:(sc_i + 1) * TILE],
            in_=o_t[buf].ap().rearrange("p (fc t) -> p fc t", t=TILE)),
            wait=[("tV", tile_done_v[sc_i])], inc=f"sO{buf}", inc_k=16)

    s.wait("scalar", "sO0", s.cnt["sO0"])
    s.wait("scalar", "sO1", s.cnt["sO1"])
    stack.close()


def build_a2():
    nc = bass.Bass("TRN2", target_bir_lowering=False)
    io = {}
    for nm, shape, dt, kind in (
            ("x", [D, TOK], F32, "ExternalInput"), ("qp", [D, TOK], BF16, "ExternalInput"),
            ("num", [TOK, D], BF16, "ExternalInput"), ("den", [128, 32], F32, "ExternalInput"),
            ("kvp", [D, D], F32, "ExternalInput"), ("zp", [128, 8], F32, "ExternalInput"),
            ("WoT", [D, D], F32, "ExternalInput"), ("bo", [128, 8], F32, "ExternalInput"),
            ("ident", [128, 128], F32, "ExternalInput"),
            ("out", [D, TOK], F32, "ExternalOutput")):
        io[nm] = nc.dram_tensor(nm, shape, dt, kind=kind).ap()
    s = Sched(nc)
    emit_a2(s, nc, io, None)
    s.run()
    s.stack.close()
    return nc


def attn_host(x_fm_percore, Wq, bq_, Wk, bk_, Wv, bv_, Wo, bo_):
    nc1 = build_a1()
    epsc = np.full((1, 1), 1e-5, np.float32)
    triu_ = np.triu(np.ones((128, 128), np.float32))
    ident_ = np.eye(128, dtype=np.float32)
    bv_b = np.ascontiguousarray(np.broadcast_to(bv_.reshape(8, 128).T.reshape(128, 8)[:, :], (128, 8)))
    bv_row_ = np.ascontiguousarray(np.broadcast_to(np.asarray(bv_, np.float32)[None, :], (128, 1024)))
    in1 = [{
        "x": x_fm_percore[c],
        "WqT": wT(Wq), "WkT": wT(Wk), "WvT": wT(Wv),
        "bq": pc(bq_), "bk": pc(bk_), "bv_bc": bv_row_,
        "triu": triu_, "ident": ident_, "epsc": epsc,
    } for c in range(N_CORES)]
    r1 = run_bass_kernel_spmd(nc1, in1, core_ids=list(range(N_CORES)))
    r1 = r1.results

    nc2 = build_a2()
    zeros_kv = np.zeros((D, D), np.float32)
    zeros_z = np.zeros((128, 8), np.float32)
    in2 = []
    for c in range(N_CORES):
        odd = c % 2 == 1
        in2.append({
            "x": x_fm_percore[c],
            "qp": r1[c]["qp"], "num": r1[c]["num"], "den": r1[c]["den"],
            "kvp": r1[c - 1]["kv"] if odd else zeros_kv,
            "zp": r1[c - 1]["z"] if odd else zeros_z,
            "WoT": wT(Wo), "bo": pc(bo_), "ident": ident_,
        })
    r2 = run_bass_kernel_spmd(nc2, in2, core_ids=list(range(N_CORES)))
    return [r2.results[c]["out"] for c in range(N_CORES)]


def emit_tpN(s, pTs, pT_name, id_bf, pairs, first_wait):
    n = len(pairs)
    tpc = {}

    def emit_t(fc):
        slot = fc % 2
        w = list(first_wait) if fc == 0 else []
        if (pT_name, slot) in s.bank_free:
            w.extend(s.bank_free[(pT_name, slot)])
        s.do("tensor", lambda e, fc=fc, slot=slot: e.transpose(
            pTs[slot][:, 0:128], pairs[fc][0], id_bf[:, :]),
            wait=w, inc="tP")
        tpc[fc] = s.cnt["tP"]

    def emit_c(fc):
        slot = fc % 2
        s.do("vector", lambda e, fc=fc, slot=slot: e.tensor_copy(
            pairs[fc][1], pTs[slot][:, 0:128]),
            wait=[("tP", tpc[fc])], inc="tV")
        s.bank_free[(pT_name, slot)] = [("tV", s.cnt["tV"])]
    emit_t(0)
    if n > 1:
        emit_t(1)
    for fc in range(2, n):
        emit_c(fc - 2)
        emit_t(fc)
    if n > 1:
        emit_c(n - 2)
    emit_c(n - 1)
    return s.cnt["tV"]


NCH = 17
DIN_P = 2208


def emit_m1(s, nc, io, C, pre="m1_"):
    x_in = io["x"]; xh_in = io["xh"]; w1T = io["w1T"]; convT = io.get("convT")
    convb = io["convb"]; prelu = io["prelu"]; alog = io["alog"]
    ident = io["ident"]; xc_out = io["xc"]; at_out = io["at"]; st_out = io["states"]
    pv = io.get("pv")
    scrC = io["scr"]["C"] if io.get("scr") else nc.dram_tensor(
        pre + "scrC", [NCH, 128, 8704], BF16).ap()
    stack = ExitStack()
    sb = lambda name, shape, dt=F32: stack.enter_context(nc.sbuf_tensor(pre + name, shape, dt))
    ps = lambda name, shape, dt=F32: stack.enter_context(nc.psum_tensor(pre + name, shape, dt))
    C = dict(C)
    C["ps_a"] = ps("lnps_a", [128, TILE])
    C["ps_b"] = ps("lnps_b", [128, TILE])
    win1_bf = sb("win1_bf", [128, 8 * DIN_P], BF16)
    pC = [sb(f"pC{i}", [128, 8704], BF16) for i in range(2)]
    convb_pc = sb("convb_pc", [128, NCH])
    a_pre = sb("a_pre", [128, 1])
    a_vec = sb("a_vec", [32, 1])
    one_v = sb("one_v", [128, 1])
    id_bf = sb("id_bf", [128, 128], BF16)
    id_f = sb("id_f", [128, 128])
    x_t = sb("m1x", [128, 8 * TILE])
    xh = sb("m1xh", [128, 8 * 3])
    xln = sb("m1xln", [128, 8 * TILE], BF16)
    xlnh = sb("m1xlnh", [128, 8 * 3], BF16)
    xbc = sb("xbc", [128, NCH * (TILE + 3)], BF16)
    xc_t = sb("xc_t", [128, NCH * TILE], BF16)
    at_t = sb("at_t", [32, TILE])
    t_cr = [sb(f"t_cr{i}", [128, TILE]) for i in range(2)]
    t_cz = [sb(f"t_cz{i}", [128, TILE]) for i in range(2)]
    sp1 = sb("sp1", [32, TILE])
    sp2 = sb("sp2", [32, TILE])
    x_tm = sb("x_tm", [128, 2048], BF16)
    bc_tm = sb("bc_tm", [128, 128], BF16)
    bdelta = sb("bdelta", [128, 2048], BF16)
    acum_c = sb("acum_c", [32, 128])
    acum_tm = sb("acum_tm", [128, 32])
    dstate = sb("dstate", [128, 32])
    asum = sb("asum", [32, 1])
    asum_row = sb("asum_row", [1, 32])
    g_row = sb("g_row", [1, 32])
    gbc_sb = sb("gbc_sb", [128, 32])
    gexp_t = sb("gexp_t", [64, 2048])
    state_sb = [sb(f"state{i}", [64, 2048]) for i in range(2)]
    prelu_sb = sb("prelu_sb", [1, 1])
    alog_sb = sb("alog_sb", [32, 1])

    p = [ps(f"m1p{i}", [128, TILE]) for i in range(2)]
    pT = [ps(f"m1pT{i}", [128, 1024], BF16) for i in range(2)]
    pF = ps("m1pF", [128, TILE])
    pG = ps("m1pG", [128, TILE])

    s.do("gpsimd", lambda e: e.dma_start(
        out=win1_bf.ap().rearrange("p (kc m) -> p kc m", m=DIN_P),
        in_=w1T.rearrange("(kc kp) m -> kp kc m", kp=128)), inc="dW", inc_k=16)
    s.do("gpsimd", lambda e: e.dma_start(out=id_bf[:, :], in_=ident), inc="dW", inc_k=16)
    m1_dwz = s.cnt["dW"]
    for dst, src in ((convb_pc[:, :], convb), (prelu_sb[:, :], prelu),
                     (alog_sb[:, :], alog), (id_f[:, :], ident)):
        s.do("sync", lambda e, dst=dst, src=src: e.dma_start(out=dst, in_=src), inc="dC", inc_k=16)
    m1_dc = s.cnt["dC"]
    s.do("vector", lambda e: e.memset(one_v[:, 0:1], 1.0), wait=[("dC", m1_dc)], inc="tV")
    s.do("scalar", lambda e: e.activation(a_vec[:, 0:1], alog_sb[:, 0:1], AF.Exp),
         wait=[("tV", s.cnt["tV"])], inc="tS")
    s.do("vector", lambda e: e.tensor_scalar_mul(a_vec[:, 0:1], a_vec[:, 0:1], -1.0),
         wait=[("tS", s.cnt["tS"])], inc="tV")
    s.do("tensor", lambda e: e.matmul(pG[:, 0:1], C["ones_f"][0:1, :], prelu_sb[0:1, 0:1],
                                      start=True, stop=True),
         wait=[("tV", s.cnt["tV"])], inc="tP")
    s.do("vector", lambda e: e.tensor_copy(a_pre[:, 0:1], pG[:, 0:1]),
         wait=[("tP", s.cnt["tP"])], inc="tV")
    s.do("vector", lambda e: e.memset(state_sb[0][:, :], 0.0), inc="tV")
    init_v = s.cnt["tV"]

    x_fm = x_in.rearrange("(fc fp) t -> fp fc t", fp=128)
    xh_fm = xh_in.rearrange("(fc fp) t -> fp fc t", fp=128)
    xc_fm = xc_out.rearrange("(cc cp) t -> cp cc t", cp=128)
    convf = convT.rearrange("(ic ip) m -> ip ic m", ip=128) if pv is None else None

    s.do("sync", lambda e: e.dma_start(
        out=xh.ap().rearrange("p (fc t) -> p fc t", t=3), in_=xh_fm[:, :, 0:3]),
        inc="dXh", inc_k=16)
    m1_dxh = s.cnt["dXh"]
    if "hmask" in io:
        hmask_pc = sb("hmask_pc", [128, 1])
        s.do("sync", lambda e: e.dma_start(out=hmask_pc[:, 0:1], in_=io["hmask"]),
             inc="dXh", inc_k=16)
        m1_dxh = s.cnt["dXh"]
        s.do("vector", lambda e: e.tensor_scalar_mul(xh[:, :], xh[:, :], hmask_pc[:, 0:1]),
             wait=[("dXh", m1_dxh), ("tV", init_v)], inc="tV")
    vlnh = emit_ln(s, C, xh[:, :], xlnh, 3,
                   first_wait=[("dXh", m1_dxh), ("dW", m1_dwz), ("tV", s.cnt["tV"]),
                               ("tV", init_v)])
    for oc in range(NCH):
        pb = p[oc % 2]
        pbn = f"m1p{oc % 2}"
        for kc in range(8):
            w = []
            if kc == 0:
                w = [("tV", vlnh)]
                if pbn in s.bank_free:
                    w.extend(s.bank_free[pbn])
            s.do("tensor", lambda e, oc=oc, kc=kc, pb=pb: e.matmul(
                pb[:, 0:3], win1_bf[:, kc * DIN_P + oc * 128: kc * DIN_P + oc * 128 + 128],
                xlnh[:, kc * 3:(kc + 1) * 3], start=(kc == 0), stop=(kc == 7)),
                wait=w, inc="tP" if kc == 7 else None)
        pe = s.cnt["tP"]
        s.do("vector", lambda e, oc=oc, pb=pb: e.tensor_copy(
            xbc[:, oc * (TILE + 3): oc * (TILE + 3) + 3], pb[:, 0:3]),
            wait=[("tP", pe)], inc="tV")
        s.bank_free[pbn] = [("tV", s.cnt["tV"])]

    panel_written = {}
    pan_free = {}
    scratch_dep = {}
    loads_dw = {}

    def issue_cpanel(ti, oc):
        i = oc % 2
        sem = f"dP{i}" if ti > 0 else f"dP{i}c"
        w = list(pan_free.get(i, []))
        if pv is not None:
            if ti == 0:
                w.append(pv[("C", oc)])
            s.do("sync", lambda e, oc=oc, i=i: e.dma_start(
                out=pC[i][:, :], in_=scrC[oc]), wait=w, inc=sem, inc_k=16)
            loads_dw[(ti, oc)] = (sem, s.cnt[sem])
            return
        if ti == 0:
            for k in range(4):
                s.do("gpsimd", lambda e, oc=oc, i=i, k=k: e.dma_start(
                    out=pC[i].ap().rearrange("p (ic k m) -> p ic k m", k=4, m=128)[:, :, k, :],
                    in_=convf[:, :, k * 2176 + oc * 128: k * 2176 + (oc + 1) * 128]),
                    wait=w if k == 0 else (), inc=sem, inc_k=16)
            dw = s.cnt[sem]
            s.do("sync", lambda e, oc=oc, i=i: e.dma_start(
                out=scrC[oc], in_=pC[i][:, :]), wait=[(sem, dw)], inc=f"dS{i}", inc_k=16)
            panel_written[oc] = s.cnt[f"dS{i}"]
            loads_dw[(ti, oc)] = (sem, dw)
            scratch_dep[i] = (f"dS{i}", s.cnt[f"dS{i}"])
        else:
            if (ti, "first") not in loads_dw:
                w = w + [("dS0", s.cnt.get("dS0", 0)), ("dS1", s.cnt.get("dS1", 0))]
                loads_dw[(ti, "first")] = True
            s.do("sync", lambda e, oc=oc, i=i: e.dma_start(
                out=pC[i][:, :], in_=scrC[oc]), wait=w, inc=sem, inc_k=16)
            loads_dw[(ti, oc)] = (sem, s.cnt[sem])

    xlns_fm = io["xlns"].rearrange("(fc fp) t -> fp fc t", fp=128) \
        if "xlns" in io else None
    st_buf = 0
    for ti in range(NT):
        s.do("sync", lambda e, ti=ti: e.dma_start(
            out=x_t.ap().rearrange("p (fc t) -> p fc t", t=TILE),
            in_=x_fm[:, :, ti * TILE:(ti + 1) * TILE]),
            wait=[("tV", s.cnt["tV"])], inc="dX0", inc_k=16)
        dx = s.cnt["dX0"]
        lw = [("dX0", dx)]
        if xlns_fm is not None and ti > 0:
            lw.append(("sXL", s.cnt["sXL"]))
        vln = emit_ln(s, C, x_t[:, :], xln, TILE, first_wait=lw)
        if xlns_fm is not None:
            s.do("scalar", lambda e, ti=ti: e.dma_start(
                out=xlns_fm[:, :, ti * TILE:(ti + 1) * TILE],
                in_=xln.ap().rearrange("p (fc t) -> p fc t", t=TILE)),
                wait=[("tV", vln)], inc="sXL", inc_k=16)

        if ti > 0:
            s.do("vector", lambda e: e.tensor_copy(
                bass.AP(xbc.ap().tensor, xbc.ap().offset,
                        [xbc.ap().ap[0], [TILE + 3, NCH], [1, 3]]),
                bass.AP(xbc.ap().tensor, xbc.ap().offset + TILE,
                        [xbc.ap().ap[0], [TILE + 3, NCH], [1, 3]])), inc="tV")
        halo_v = s.cnt["tV"]

        for oc in range(NCH):
            pb = p[oc % 2]
            pbn = f"m1p{oc % 2}"
            for kc in range(8):
                w = []
                if kc == 0:
                    w = [("tV", max(vln, halo_v))]
                    if pbn in s.bank_free:
                        w.extend(s.bank_free[pbn])
                s.do("tensor", lambda e, oc=oc, kc=kc, pb=pb: e.matmul(
                    pb[:, :], win1_bf[:, kc * DIN_P + oc * 128: kc * DIN_P + oc * 128 + 128],
                    xln[:, kc * TILE:(kc + 1) * TILE], start=(kc == 0), stop=(kc == 7)),
                    wait=w, inc="tP" if kc == 7 else None)
            pe = s.cnt["tP"]
            s.do("vector", lambda e, oc=oc, pb=pb: e.tensor_copy(
                xbc[:, oc * (TILE + 3) + 3: oc * (TILE + 3) + 3 + TILE], pb[:, :]),
                wait=[("tP", pe)], inc="tV")
            s.bank_free[pbn] = [("tV", s.cnt["tV"])]
        pb = p[NCH % 2]
        pbn = f"m1p{NCH % 2}"
        for kc in range(8):
            w = list(s.bank_free.get(pbn, [])) if kc == 0 else []
            s.do("tensor", lambda e, kc=kc, pb=pb: e.matmul(
                pb[0:32, :], win1_bf[:, kc * DIN_P + 2176: kc * DIN_P + 2208],
                xln[:, kc * TILE:(kc + 1) * TILE], start=(kc == 0), stop=(kc == 7)),
                wait=w, inc="tP" if kc == 7 else None)
        pe_dt = s.cnt["tP"]
        atw = [("tP", pe_dt)]
        if s.cnt.get("sAT", 0) > 0:
            atw.append(("sAT", s.cnt["sAT"]))
        s.do("scalar", lambda e, pb=pb: e.activation(sp1[:, :], pb[0:32, :], AF.Abs),
             wait=atw, inc="tS")
        s.do("scalar", lambda e: e.activation(sp1[:, :], sp1[:, :], AF.Exp, scale=-1.0),
             wait=[("tS", s.cnt["tS"])], inc="tS")
        s.do("scalar", lambda e: e.activation(sp1[:, :], sp1[:, :], AF.Ln, bias=one_v[0:32, 0:1]),
             wait=[("tS", s.cnt["tS"])], inc="tS")
        s.do("scalar", lambda e, pb=pb: e.activation(sp2[:, :], pb[0:32, :], AF.Relu),
             wait=[("tS", s.cnt["tS"])], inc="tS")
        s.bank_free[pbn] = [("tS", s.cnt["tS"])]
        s.do("vector", lambda e: e.tensor_tensor(sp1[:, :], sp1[:, :], sp2[:, :], ALU.add),
             wait=[("tS", s.cnt["tS"])], inc="tV")
        s.do("vector", lambda e: e.tensor_scalar_mul(at_t[:, :], sp1[:, :], a_vec[:, 0:1]),
             wait=[("tV", s.cnt["tV"])], inc="tV")
        at_v = s.cnt["tV"]
        s.do("scalar", lambda e, ti=ti: e.dma_start(
            out=at_out[:, ti * TILE:(ti + 1) * TILE], in_=at_t[:, :]),
            wait=[("tV", at_v)], inc="sAT", inc_k=16)
        xbc_v = at_v

        issue_cpanel(ti, 0)
        issue_cpanel(ti, 1)
        for oc in range(NCH):
            pb = p[oc % 2]
            pbn = f"m1p{oc % 2}"
            sem_cnt = loads_dw[(ti, oc)]
            k_i = 0
            for ic in range(NCH):
                for k in range(4):
                    w = []
                    if k_i == 0:
                        w = [sem_cnt, ("tV", xbc_v)]
                        if ti > 0:
                            w.append(("sXC", s.cnt["sXC"]))
                        if pbn in s.bank_free:
                            w.extend(s.bank_free[pbn])
                    s.do("tensor", lambda e, oc=oc, ic=ic, k=k, pb=pb, k_i=k_i, i=oc % 2: e.matmul(
                        pb[:, :], pC[i][:, (ic * 4 + k) * 128:(ic * 4 + k + 1) * 128],
                        xbc[:, ic * (TILE + 3) + k: ic * (TILE + 3) + k + TILE],
                        start=(k_i == 0), stop=(k_i == 4 * NCH - 1)),
                        wait=w, inc="tP" if k_i == 4 * NCH - 1 else None)
                    k_i += 1
            pe = s.cnt["tP"]
            pan_free[oc % 2] = [("tP", pe)] + ([scratch_dep[oc % 2]] if (ti == 0 and oc % 2 in scratch_dep) else [])
            i2 = oc % 2
            sw = [("tP", pe)]
            if ("t_cr", i2) in s.bank_free:
                sw.extend(s.bank_free[("t_cr", i2)])
            s.do("scalar", lambda e, oc=oc, pb=pb, i2=i2: e.activation(
                t_cr[i2][:, :], pb[:, :], AF.Relu, bias=convb_pc[:, oc:oc + 1]),
                wait=sw, inc="tS")
            relu_c = s.cnt["tS"]
            vw = [("tP", pe), ("tS", relu_c)]
            if ("t_cz", i2) in s.bank_free:
                vw.extend(s.bank_free[("t_cz", i2)])
            s.do("vector", lambda e, oc=oc, pb=pb, i2=i2: e.tensor_scalar(
                t_cz[i2][:, :], pb[:, :], convb_pc[:, oc:oc + 1], 0.0, ALU.add, ALU.min),
                wait=vw, inc="tV")
            s.bank_free[pbn] = [("tS", relu_c), ("tV", s.cnt["tV"])]
            s.do("vector", lambda e, oc=oc, i2=i2: e.scalar_tensor_tensor(
                xc_t[:, oc * TILE:(oc + 1) * TILE], t_cz[i2][:, :], a_pre[:, 0:1],
                t_cr[i2][:, :], ALU.mult, ALU.add),
                wait=[("tV", s.cnt["tV"])], inc="tV")
            s.bank_free[("t_cr", i2)] = [("tV", s.cnt["tV"])]
            s.bank_free[("t_cz", i2)] = [("tV", s.cnt["tV"])]
            if oc + 2 < NCH:
                issue_cpanel(ti, oc + 2)
        xc_v = s.cnt["tV"]
        s.do("scalar", lambda e, ti=ti: e.dma_start(
            out=xc_fm[:, :, ti * TILE:(ti + 1) * TILE],
            in_=xc_t.ap().rearrange("p (cc t) -> p cc t", t=TILE)),
            wait=[("tV", xc_v)], inc="sXC", inc_k=16)

        for c in range(4):
            g = ti * 4 + c
            prs = [(xc_t[:, ch * TILE + c * 128: ch * TILE + (c + 1) * 128],
                    x_tm[:, ch * 128:(ch + 1) * 128]) for ch in range(16)]
            prs.append((xc_t[:, 16 * TILE + c * 128: 16 * TILE + (c + 1) * 128],
                        bc_tm[:, 0:128]))
            tp_v = emit_tpN(s, pT, "m1pT", id_bf, prs, [("tV", xc_v)])
            s.do("vector", lambda e, c=c: e.tensor_tensor_scan(
                acum_c[:, :], at_t[:, c * 128:(c + 1) * 128], at_t[:, c * 128:(c + 1) * 128],
                0.0, ALU.add, ALU.bypass),
                wait=[("tV", s.cnt["tV"])], inc="tV")
            scan_v = s.cnt["tV"]
            s.do("vector", lambda e, c=c: e.reduce_sum(
                asum[:, 0:1], at_t[:, c * 128:(c + 1) * 128], mybir.AxisListType.X),
                wait=[("tV", scan_v)], inc="tV")
            sum_v = s.cnt["tV"]
            w = [("tV", scan_v)] + s.bank_free.get("m1pF", [])
            s.do("tensor", lambda e: e.transpose(pF[:, 0:32], acum_c[:, :], id_f[0:32, 0:32]),
                 wait=w, inc="tP")
            s.do("vector", lambda e: e.tensor_copy(acum_tm[:, :], pF[:, 0:32]),
                 wait=[("tP", s.cnt["tP"])], inc="tV")
            s.bank_free["m1pF"] = [("tV", s.cnt["tV"])]
            w = [("tV", sum_v)] + s.bank_free.get("m1pF", [])
            s.do("tensor", lambda e: e.transpose(pF[0:1, 0:32], asum[:, 0:1], id_f[0:32, 0:32]),
                 wait=w, inc="tP")
            s.do("vector", lambda e: e.tensor_copy(asum_row[0:1, :], pF[0:1, 0:32]),
                 wait=[("tP", s.cnt["tP"])], inc="tV")
            s.bank_free["m1pF"] = [("tV", s.cnt["tV"])]
            row_v = s.cnt["tV"]
            s.do("scalar", lambda e: e.activation(g_row[0:1, :], asum_row[0:1, :], AF.Exp),
                 wait=[("tV", row_v)], inc="tS")
            g_s = s.cnt["tS"]
            w = [("tV", row_v)] + s.bank_free.get("m1pG", [])
            s.do("tensor", lambda e: e.matmul(pG[:, 0:32], C["ones_f"][0:1, :], asum_row[0:1, :],
                                              start=True, stop=True), wait=w)
            s.do("tensor", lambda e: e.matmul(pG[:, 32:64], C["ones_f"][0:1, :], g_row[0:1, :],
                                              start=True, stop=True),
                 wait=[("tS", g_s)], inc="tP")
            bc_pe = s.cnt["tP"]
            s.do("vector", lambda e: e.tensor_tensor(dstate[:, :], pG[:, 0:32], acum_tm[:, :],
                                                     ALU.subtract),
                 wait=[("tP", bc_pe)], inc="tV")
            s.do("scalar", lambda e: e.activation(dstate[:, :], dstate[:, :], AF.Exp),
                 wait=[("tV", s.cnt["tV"])], inc="tS")
            ds_s = s.cnt["tS"]
            s.do("vector", lambda e: e.tensor_copy(gbc_sb[:, :], pG[:, 32:64]),
                 wait=[("tP", bc_pe)], inc="tV")
            s.bank_free["m1pG"] = [("tV", s.cnt["tV"])]
            gbc_v = s.cnt["tV"]
            s.do("vector", lambda e: e.tensor_tensor(
                bdelta[:, :],
                bass.AP(bc_tm.ap().tensor, bc_tm.ap().offset,
                        [bc_tm.ap().ap[0], [0, 32], [1, 64]]),
                bass.AP(dstate.ap().tensor, dstate.ap().offset,
                        [dstate.ap().ap[0], [1, 32], [0, 64]]),
                ALU.mult), wait=[("tS", ds_s), ("tV", tp_v)], inc="tV")
            bd_v = s.cnt["tV"]
            cur = state_sb[st_buf]
            nxt = state_sb[1 - st_buf]
            stw = [("tV", s.cnt["tV"])]
            if s.cnt.get(f"sST{st_buf}", 0) > 0:
                pass
            s.do("scalar", lambda e, g=g, cur=cur: e.dma_start(
                out=st_out[g], in_=cur[:, :]), wait=stw, inc=f"sST{st_buf}", inc_k=16)
            for grp in range(4):
                pb = p[grp % 2]
                pbn = f"m1p{grp % 2}"
                for hh in range(8):
                    h = grp * 8 + hh
                    w = []
                    if hh == 0:
                        w = [("tV", bd_v)]
                        if pbn in s.bank_free:
                            w.extend(s.bank_free[pbn])
                    s.do("tensor", lambda e, h=h, hh=hh, pb=pb: e.matmul(
                        pb[0:64, hh * 64:(hh + 1) * 64],
                        bdelta[:, h * 64:(h + 1) * 64],
                        x_tm[:, h * 64:(h + 1) * 64],
                        start=True, stop=True),
                        wait=w, inc="tP" if hh == 7 else None)
                pe_g = s.cnt["tP"]
                gw = [("tP", pe_g), ("tV", gbc_v)]
                if ("gx", grp % 2) in s.bank_free:
                    gw.extend(s.bank_free[("gx", grp % 2)])
                s.do("vector", lambda e, grp=grp, cur=cur: e.tensor_tensor(
                    gexp_t[0:64, grp * 512:(grp + 1) * 512],
                    cur[:, grp * 512:(grp + 1) * 512],
                    bass.AP(gbc_sb.ap().tensor, gbc_sb.ap().offset + grp * 8,
                            [[gbc_sb.ap().ap[0][0], 64], [1, 8], [0, 64]]),
                    ALU.mult), wait=gw, inc="tV")
                nw = [("tV", s.cnt["tV"])]
                if s.cnt.get(f"sST{1 - st_buf}", 0) > 0:
                    nw.append((f"sST{1 - st_buf}", s.cnt[f"sST{1 - st_buf}"]))
                s.do("vector", lambda e, grp=grp, nxt=nxt, pb=pb: e.tensor_tensor(
                    nxt[:, grp * 512:(grp + 1) * 512],
                    gexp_t[0:64, grp * 512:(grp + 1) * 512],
                    pb[0:64, 0:512], ALU.add),
                    wait=nw, inc="tV")
                s.bank_free[pbn] = [("tV", s.cnt["tV"])]
                s.bank_free[("gx", grp % 2)] = [("tV", s.cnt["tV"])]
            st_buf = 1 - st_buf
    s.do("scalar", lambda e: e.dma_start(
        out=st_out[TOK // 128], in_=state_sb[st_buf][:, :]),
        wait=[("tV", s.cnt["tV"])], inc=f"sST{st_buf}", inc_k=16)
    for nm in ("sXC", "sAT", "sST0", "sST1", "dS0", "dS1"):
        s.wait("scalar", nm, s.cnt.get(nm, 0))
    stack.close()


def build_m1():
    nc = bass.Bass("TRN2", target_bir_lowering=False)
    io = {}
    for nm, shape, dt, kind in (
            ("x", [D, TOK + 3], F32, "ExternalInput"), ("w1T", [D, DIN_P], F32, "ExternalInput"),
            ("convT", [2176, 4 * 2176], F32, "ExternalInput"),
            ("convb", [128, NCH], F32, "ExternalInput"), ("prelu", [1, 1], F32, "ExternalInput"),
            ("alog", [32, 1], F32, "ExternalInput"), ("ident", [128, 128], F32, "ExternalInput"),
            ("epsc", [1, 1], F32, "ExternalInput"),
            ("xc", [2176, TOK], BF16, "ExternalOutput"), ("at", [32, TOK], F32, "ExternalOutput"),
            ("states", [TOK // 128 + 1, 64, 2048], F32, "ExternalOutput")):
        io[nm] = nc.dram_tensor(nm, shape, dt, kind=kind).ap()
    s = Sched(nc)
    C = common_tiles(s, nc)
    io3 = dict(io)
    io3["xh"] = io["x"][:, 0:3]
    io3["x"] = io["x"][:, 3:TOK + 3]
    emit_m1(s, nc, io3, C)
    s.run()
    s.stack.close()
    return nc


def emit_m2(s, nc, io, C, pre="m2_"):
    x_in = io["x"]; xc_in = io["xc"]; at_in = io["at"]; st_in = io["states"]
    hin_in = io["hin"]; w2T = io["w2T"]; w3T = io["w3T"]; lng = io["lng"]
    lnb = io["lnb"]; triu = io["triu"]; ident = io["ident"]; out = io["out"]
    stack = ExitStack()
    sb = lambda name, shape, dt=F32: stack.enter_context(nc.sbuf_tensor(pre + name, shape, dt))
    ps = lambda name, shape, dt=F32: stack.enter_context(nc.psum_tensor(pre + name, shape, dt))
    C = dict(C)
    C["ps_a"] = ps("lnps_a", [128, TILE])
    C["ps_b"] = ps("lnps_b", [128, TILE])

    win2_bf = sb("win2_bf", [128, 8 * 2048], BF16)
    win3_bf = sb("win3_bf", [128, 16 * D], BF16)
    lng_pc = sb("lng_pc", [128, 16])
    lnb_pc = sb("lnb_pc", [128, 16], BF16)
    cb = sb("cb", [128, 8])
    cb_row = sb("cb_row", [1, 1024])
    triu_sb = sb("triu_sb", [128, 128])
    id_bf = sb("id_bf", [128, 128], BF16)
    id_f = sb("id_f", [128, 128])
    ones128 = sb("ones128", [128, 128])
    x_t = sb("m2x", [128, 8 * TILE])
    xln = sb("m2xln", [128, 8 * TILE], BF16)
    at_t = sb("m2at", [32, TILE])
    xcb = [sb(f"xcb{i}", [128, NCH * 128], BF16) for i in range(3)]
    u2b = [sb(f"u2b{i}", [128, 2048], BF16) for i in range(3)]
    x_tmb = [sb(f"xtmb{i}", [128, 2048], BF16) for i in range(2)]
    bc_tmb = [sb(f"bctmb{i}", [128, 128], BF16) for i in range(2)]
    c0b = [sb(f"c0b{i}", [64, 128], BF16) for i in range(2)]
    smb = [sb(f"smb{i}", [128, 128], BF16) for i in range(2)]
    acumb = [sb(f"acumb{i}", [128, 32]) for i in range(2)]
    doutb = [sb(f"doutb{i}", [128, 32]) for i in range(2)]
    acum_c = sb("m2acumc", [32, 128])
    asum = sb("m2asum", [32, 1])
    asum_row = sb("m2asumrow", [1, 32])
    r_sb = sb("r_sb", [32, 1])
    g_row = sb("m2grow", [1, 32])
    stb = sb("stb", [64, 2048], BF16)
    hsb = [sb(f"hsb{i}", [64, 2048], BF16) for i in range(2)]
    hin_sb = sb("hin_sb", [64, 2048], BF16)
    y_allb = [sb(f"y_all{i}", [128, 2048], BF16) for i in range(2)]
    yo_sc = sb("yo_sc", [128, 512], BF16)
    hn_fm = sb("hn_fm", [128, 16 * 256], BF16)
    dg4b = [sb(f"dg4{i}", [128, 512]) for i in range(2)]
    d4b = [sb(f"d4{i}", [128, 512]) for i in range(2)]
    mrow = sb("m2mrow", [128, 1])
    eps_col = sb("eps_col", [128, 1])
    vrow = sb("m2vrow", [128, 1])
    rrow = sb("m2rrow", [128, 1])

    p = [ps(f"m2p{i}", [128, TILE]) for i in range(3)]
    pT = [ps(f"m2pT{i}", [128, 1024], BF16) for i in range(2)]

    s.do("gpsimd", lambda e: e.dma_start(
        out=win2_bf.ap().rearrange("p (kc m) -> p kc m", m=2048),
        in_=w2T.rearrange("(kc kp) m -> kp kc m", kp=128)), inc="dW", inc_k=16)
    s.do("gpsimd", lambda e: e.dma_start(
        out=win3_bf.ap().rearrange("p (kc m) -> p kc m", m=D),
        in_=w3T.rearrange("(kc kp) m -> kp kc m", kp=128)), inc="dW", inc_k=16)
    s.do("gpsimd", lambda e: e.dma_start(out=lng_pc[:, :], in_=lng), inc="dW", inc_k=16)
    s.do("gpsimd", lambda e: e.dma_start(out=lnb_pc[:, :], in_=lnb), inc="dW", inc_k=16)
    s.do("gpsimd", lambda e: e.dma_start(out=id_bf[:, :], in_=ident), inc="dW", inc_k=16)
    m2_dw = s.cnt["dW"]
    for dst, src in ((triu_sb[:, :], triu), (id_f[:, :], ident)):
        s.do("sync", lambda e, dst=dst, src=src: e.dma_start(out=dst, in_=src), inc="dC", inc_k=16)
    s.do("gpsimd", lambda e: e.dma_start(out=hin_sb[:, :], in_=hin_in), inc="dHin", inc_k=16)
    m2_dc = s.cnt["dC"]
    m2_dhin = s.cnt["dHin"]
    if "hmask" in io:
        hmask2 = sb("hmask2", [128, 1])
        s.do("sync", lambda e: e.dma_start(out=hmask2[:, 0:1], in_=io["hmask"]), inc="dC", inc_k=16)
        m2_dc = s.cnt["dC"]
        s.do("vector", lambda e: e.tensor_scalar_mul(hin_sb[:, :], hin_sb[:, :], hmask2[0:64, 0:1]),
             wait=[("dC", m2_dc), ("dHin", m2_dhin)], inc="tV")
    s.do("vector", lambda e: e.memset(ones128[:, :], 1.0), wait=[("dC", m2_dc), ("dW", m2_dw)])
    s.do("vector", lambda e: e.memset(eps_col[:, 0:1], 1e-5))
    s.do("vector", lambda e: e.memset(r_sb[:, 0:1], 0.0), inc="tV")

    for half in range(2):
        pscb = C["ps_a"] if half == 0 else C["ps_b"]
        for kc in range(16):
            w = [("dW", m2_dw)] if kc == 0 else []
            s.do("tensor", lambda e, kc=kc, half=half, pscb=pscb: e.matmul(
                pscb[0:1, 0:512], lnb_pc[:, kc:kc + 1],
                win3_bf[:, kc * D + half * 512: kc * D + (half + 1) * 512],
                start=(kc == 0), stop=(kc == 15)),
                wait=w, inc="tP" if kc == 15 else None)
        s.do("vector", lambda e, half=half, pscb=pscb: e.tensor_copy(
            cb_row[0:1, half * 512:(half + 1) * 512], pscb[0:1, 0:512]),
            wait=[("tP", s.cnt["tP"])], inc="tV")
    tr_v = s.cnt["tV"]
    for c8 in range(8):
        s.do("tensor", lambda e, c8=c8: e.transpose(
            C["ps_a"][:, 0:1], cb_row[0:1, c8 * 128:(c8 + 1) * 128], id_f[0:1, 0:1]),
            wait=[("tV", tr_v)], inc="tP")
        s.do("vector", lambda e, c8=c8: e.tensor_copy(cb[:, c8:c8 + 1], C["ps_a"][:, 0:1]),
             wait=[("tP", s.cnt["tP"])], inc="tV")
        tr_v = s.cnt["tV"]
    for kc in range(16):
        s.do("vector", lambda e, kc=kc: e.tensor_scalar_mul(
            win3_bf[:, kc * D:(kc + 1) * D], win3_bf[:, kc * D:(kc + 1) * D],
            lng_pc[:, kc:kc + 1]), wait=[("tV", s.cnt["tV"])], inc="tV")
    init_v = s.cnt["tV"]

    x_fm = x_in.rearrange("(fc fp) t -> fp fc t", fp=128)
    xc_fm = xc_in.rearrange("(cc cp) t -> cp cc t", cp=128)
    xln_fm = io["xln_in"].rearrange("(fc fp) t -> fp fc t", fp=128)
    out_fm = out.rearrange("(fc fp) t -> fp fc t", fp=128)

    H = {}

    def head(g):
        ti, c = divmod(g, 4)
        i = g % 2
        i3 = g % 3
        h = {"i3": i3}
        w = list(s.bank_free.get(("xcb", i3), []))
        s.do("sync", lambda e, ti=ti, c=c, i3=i3: e.dma_start(
            out=xcb[i3].ap().rearrange("p (cc t) -> p cc t", t=128),
            in_=xc_fm[:, :, ti * TILE + c * 128: ti * TILE + (c + 1) * 128]),
            wait=w, inc=f"dXC{i3}", inc_k=16)
        h["dxc"] = s.cnt[f"dXC{i3}"]
        dxl = H[("dxl", ti)]
        for piece in range(4):
            pb = p[piece % 2]
            pbn = f"m2p{piece % 2}"
            for kc in range(8):
                w = []
                if kc == 0:
                    w = [("dXL", dxl)]
                    if ti == 0:
                        w.append(("tV", init_v))
                    if pbn in s.bank_free:
                        w.extend(s.bank_free[pbn])
                s.do("tensor", lambda e, c=c, piece=piece, kc=kc, pb=pb: e.matmul(
                    pb[:, :], xln[:, kc * TILE + c * 128: kc * TILE + (c + 1) * 128],
                    win2_bf[:, kc * 2048 + piece * 512: kc * 2048 + (piece + 1) * 512],
                    start=(kc == 0), stop=(kc == 7)),
                    wait=w, inc="tP" if kc == 7 else None)
            pe = s.cnt["tP"]
            sw = [("tP", pe)] + list(s.bank_free.get(("u2", i3), []) if piece == 0 else [])
            s.do("scalar", lambda e, piece=piece, pb=pb, i3=i3: e.activation(
                u2b[i3][:, piece * 512:(piece + 1) * 512], pb[:, :], AF.Gelu),
                wait=sw, inc="tS")
            s.bank_free[pbn] = [("tS", s.cnt["tS"])]
        h["u2_v"] = s.cnt["tS"]

        prs = [(xcb[i3][:, ch * 128:(ch + 1) * 128],
                x_tmb[i][:, ch * 128:(ch + 1) * 128]) for ch in range(16)]
        prs.append((xcb[i3][:, 16 * 128:17 * 128], bc_tmb[i][:, 0:128]))
        tpw = [(f"dXC{i3}", h["dxc"])] + list(s.bank_free.get(("xtm", i), []))
        h["tp_v"] = emit_tpN(s, pT, "m2pT", id_bf, prs, tpw)
        w = list(s.bank_free.get(("m2pT", 0), []))
        s.do("tensor", lambda e, i=i: e.transpose(pT[0][0:64, 0:128], bc_tmb[i][:, 64:128],
                                                  id_bf[:, :]),
             wait=w + [("tV", h["tp_v"])], inc="tP")
        s.do("vector", lambda e, i=i: e.tensor_copy(c0b[i][:, :], pT[0][0:64, 0:128]),
             wait=[("tP", s.cnt["tP"])], inc="tV")
        s.bank_free[("m2pT", 0)] = [("tV", s.cnt["tV"])]
        h["c0_v"] = s.cnt["tV"]

        s.do("vector", lambda e, c=c, ti=ti: e.tensor_tensor_scan(
            acum_c[:, :], at_t[:, c * 128:(c + 1) * 128], at_t[:, c * 128:(c + 1) * 128],
            0.0, ALU.add, ALU.bypass),
            wait=[(f"dAT", H[("dat", ti)]), ("tV", s.cnt["tV"])], inc="tV")
        scan_v = s.cnt["tV"]
        s.do("vector", lambda e, c=c: e.reduce_sum(
            asum[:, 0:1], at_t[:, c * 128:(c + 1) * 128], mybir.AxisListType.X),
            wait=[("tV", scan_v)], inc="tV")
        sum_v = s.cnt["tV"]
        w = [("tV", scan_v)] + s.bank_free.get("m2p2", [])
        s.do("tensor", lambda e: e.transpose(p[2][:, 0:32], acum_c[:, :], id_f[0:32, 0:32]),
             wait=w, inc="tP")
        s.do("vector", lambda e, i=i: e.tensor_copy(acumb[i][:, :], p[2][:, 0:32]),
             wait=[("tP", s.cnt["tP"])], inc="tV")
        s.bank_free["m2p2"] = [("tV", s.cnt["tV"])]
        h["acum_v"] = s.cnt["tV"]
        s.do("scalar", lambda e, i=i: e.activation(doutb[i][:, :], acumb[i][:, :], AF.Exp),
             wait=[("tV", h["acum_v"])], inc="tS")
        h["dout_s"] = s.cnt["tS"]
        w = s.bank_free.get("m2p2", [])
        s.do("tensor", lambda e: e.transpose(p[2][0:1, 0:32], r_sb[:, 0:1], id_f[0:32, 0:32]),
             wait=w + [("tV", s.cnt["tV"])], inc="tP")
        s.do("vector", lambda e: e.tensor_copy(asum_row[0:1, :], p[2][0:1, 0:32]),
             wait=[("tP", s.cnt["tP"])], inc="tV")
        s.bank_free["m2p2"] = [("tV", s.cnt["tV"])]
        s.do("scalar", lambda e: e.activation(g_row[0:1, :], asum_row[0:1, :], AF.Exp),
             wait=[("tV", s.cnt["tV"])], inc="tS")
        grow_s = s.cnt["tS"]
        s.do("vector", lambda e: e.tensor_tensor(r_sb[:, 0:1], r_sb[:, 0:1], asum[:, 0:1],
                                                 ALU.add),
             wait=[("tV", sum_v), ("tS", grow_s)], inc="tV")

        stw = list(s.bank_free.get("stb", []))
        s.do("gpsimd", lambda e, g=g: e.dma_start(out=stb[:, :], in_=st_in[g]),
             wait=stw, inc="dST", inc_k=16)
        dst_c = s.cnt["dST"]
        w = s.bank_free.get("m2p2", [])
        s.do("tensor", lambda e: e.matmul(p[2][0:64, 0:32], C["ones_f"][0:1, 0:64],
                                          g_row[0:1, :], start=True, stop=True),
             wait=w + [("tS", grow_s)], inc="tP")
        gbc_pe = s.cnt["tP"]
        hbw = [("tP", gbc_pe), ("dHin", m2_dhin)] + list(s.bank_free.get(("hsb", i), []))
        s.do("vector", lambda e, i=i: e.tensor_tensor(
            hsb[i][:, :], hin_sb[:, :],
            bass.AP(p[2].ap().tensor, p[2].ap().offset,
                    [[p[2].ap().ap[0][0], 64], [1, 32], [0, 64]]),
            ALU.mult), wait=hbw, inc="tV")
        s.bank_free["m2p2"] = [("tV", s.cnt["tV"])]
        s.do("vector", lambda e, i=i: e.tensor_tensor(hsb[i][:, :], hsb[i][:, :], stb[:, :],
                                                      ALU.add),
             wait=[("tV", s.cnt["tV"]), ("dST", dst_c)], inc="tV")
        s.bank_free["stb"] = [("tV", s.cnt["tV"])]
        h["st_v"] = s.cnt["tV"]

        w = [("tV", h["c0_v"]), (f"dXC{i3}", h["dxc"])] + s.bank_free.get("m2p2", [])
        s.do("tensor", lambda e, i=i, i3=i3: e.matmul(
            p[2][:, 0:128], xcb[i3][0:64, 16 * 128:17 * 128],
            c0b[i][:, :], start=True, stop=True), wait=w, inc="tP")
        sc_pe = s.cnt["tP"]
        s.bank_free[("xcb", i3)] = [("tP", sc_pe), ("tV", h["tp_v"])]
        smw = [("tP", sc_pe)] + list(s.bank_free.get(("smb", i), []))
        s.do("vector", lambda e, i=i: e.tensor_tensor(smb[i][:, :], p[2][:, 0:128],
                                                      triu_sb[:, :], ALU.mult),
             wait=smw, inc="tV")
        s.bank_free["m2p2"] = [("tV", s.cnt["tV"])]
        h["sm_v"] = s.cnt["tV"]
        H[g] = h

    def tail_sl(g):
        ti, c = divmod(g, 4)
        i = g % 2
        h = H[g]
        y_all = y_allb[i]
        nb = 8
        st = {}

        def e_dg(b):
            bi = b % 2
            h0 = b * 4
            w = [("tV", h["sm_v"])]
            if ("dg", bi) in s.bank_free:
                w.extend(s.bank_free[("dg", bi)])
            s.do("vector", lambda e, h0=h0, i=i, bi=bi: e.tensor_tensor(
                dg4b[bi][:, :],
                bass.AP(id_f.ap().tensor, id_f.ap().offset,
                        [id_f.ap().ap[0], [0, 4], [1, 128]]),
                bass.AP(acumb[i].ap().tensor, acumb[i].ap().offset + h0,
                        [acumb[i].ap().ap[0], [1, 4], [0, 128]]),
                ALU.mult), wait=w, inc="tV")
            st[("dg_v", b)] = s.cnt["tV"]

        def e_p(b):
            bi = b % 2
            w = list(s.bank_free.get(("psA", bi), []))
            pband = C["ps_a"] if bi == 0 else C["ps_b"]
            s.do("tensor", lambda e, bi=bi, pband=pband: e.matmul(
                pband[:, :], ones128[:, :], dg4b[bi][:, :], start=True, stop=True),
                wait=w + [("tV", st[("dg_v", b)])], inc="tP")
            st[("p_pe", b)] = s.cnt["tP"]

        def e_d4(b):
            bi = b % 2
            h0 = b * 4
            pband = C["ps_a"] if bi == 0 else C["ps_b"]
            s.do("vector", lambda e, h0=h0, i=i, bi=bi, pband=pband: e.tensor_tensor(
                d4b[bi][:, :], pband[:, :],
                bass.AP(acumb[i].ap().tensor, acumb[i].ap().offset + h0,
                        [acumb[i].ap().ap[0], [1, 4], [0, 128]]),
                ALU.subtract), wait=[("tP", st[("p_pe", b)])], inc="tV")
            s.bank_free[("psA", bi)] = [("tV", s.cnt["tV"])]
            s.do("vector", lambda e, bi=bi: e.tensor_scalar_min(
                d4b[bi][:, :], d4b[bi][:, :], 0.0),
                wait=[("tV", s.cnt["tV"])], inc="tV")
            st[("min_v", b)] = s.cnt["tV"]

        def e_exp(b):
            bi = b % 2
            s.do("scalar", lambda e, bi=bi: e.activation(d4b[bi][:, :], d4b[bi][:, :], AF.Exp),
                 wait=[("tV", st[("min_v", b)])], inc="tS")
            st[("exp_s", b)] = s.cnt["tS"]

        def e_sl(b):
            bi = b % 2
            b4 = b % 2
            w = [("tS", st[("exp_s", b)])]
            grp_prev = b // 2 - 1
            if ("slrd", b % 2) in s.bank_free:
                w.extend(s.bank_free[("slrd", b % 2)])
            s.do("vector", lambda e, b=b, i=i, bi=bi: e.tensor_tensor(
                sl_sb[:, (b % 2) * 512:(b % 2) * 512 + 512], d4b[bi][:, :],
                bass.AP(smb[i].ap().tensor, smb[i].ap().offset,
                        [smb[i].ap().ap[0], [0, 4], [1, 128]]),
                ALU.mult), wait=w, inc="tV")
            s.bank_free[("dg", bi)] = [("tV", s.cnt["tV"])]
            st[("sl_v", b)] = s.cnt["tV"]

        def e_y(grp):
            sl_v = st[("sl_v", 2 * grp + 1)]
            w1 = [("tV", sl_v), ("tV", h["tp_v"])] + s.bank_free.get("m2p1", [])
            w2 = [("tV", h["st_v"]), ("tV", h["c0_v"])] + s.bank_free.get("m2p2", [])
            for hh in range(8):
                hd = grp * 8 + hh
                s.do("tensor", lambda e, hd=hd, hh=hh, i=i: e.matmul(
                    p[1][:, hh * 64:(hh + 1) * 64],
                    sl_sb[:, (hh // 4) * 512 + (hh % 4) * 128:
                             (hh // 4) * 512 + (hh % 4) * 128 + 128],
                    x_tmb[i][:, hd * 64:(hd + 1) * 64], start=True, stop=True),
                    wait=w1 if hh == 0 else (), inc="tP" if hh == 7 else None)
            yd_pe = s.cnt["tP"]
            for hh in range(8):
                hd = grp * 8 + hh
                s.do("tensor", lambda e, hd=hd, hh=hh, i=i: e.matmul(
                    p[2][0:128, hh * 64:(hh + 1) * 64], c0b[i][:, :],
                    hsb[i][:, hd * 64:(hd + 1) * 64], start=True, stop=True),
                    wait=w2 if hh == 0 else (), inc="tP" if hh == 7 else None)
            yo_pe = s.cnt["tP"]
            s.bank_free[("slrd", 0)] = [("tP", yd_pe)]
            s.bank_free[("slrd", 1)] = [("tP", yd_pe)]
            s.do("vector", lambda e, grp=grp, i=i: e.tensor_tensor(
                yo_sc[:, :], p[2][:, 0:512],
                bass.AP(doutb[i].ap().tensor, doutb[i].ap().offset + grp * 8,
                        [doutb[i].ap().ap[0], [1, 8], [0, 64]]),
                ALU.mult), wait=[("tP", yo_pe), ("tS", h["dout_s"])], inc="tV")
            s.bank_free["m2p2"] = [("tV", s.cnt["tV"])]
            yw = [("tV", s.cnt["tV"]), ("tP", yd_pe)]
            if grp == 0:
                yw.extend(s.bank_free.get(("yall", i), []))
            s.do("vector", lambda e, grp=grp, i=i, y_all=y_all: e.tensor_tensor(
                y_all[:, grp * 512:(grp + 1) * 512], yo_sc[:, :], p[1][:, 0:512],
                ALU.add), wait=yw, inc="tV")
            s.bank_free["m2p1"] = [("tV", s.cnt["tV"])]
            if grp == 3:
                s.bank_free[("hsb", i)] = [("tP", yo_pe)]
                h["y_v"] = s.cnt["tV"]

        e_dg(0); e_p(0); e_dg(1); e_d4(0); e_p(1); e_exp(0); e_d4(1); e_sl(0)
        e_exp(1); e_sl(1); e_y(0)
        for grp in range(1, 4):
            b0, b1 = 2 * grp, 2 * grp + 1
            e_dg(b0); e_p(b0); e_dg(b1); e_d4(b0); e_p(b1); e_exp(b0)
            e_d4(b1); e_sl(b0); e_exp(b1); e_sl(b1); e_y(grp)

    def tail_ln(g):
        ti, c = divmod(g, 4)
        i = g % 2
        h = H.pop(g)
        i3 = h["i3"]
        y_all = y_allb[i]
        hn = y_all
        hga = xcb[i3]
        s.do("vector", lambda e, i3=i3, hga=hga, y_all=y_all: e.tensor_tensor(
            hga[:, 0:2048], y_all[:, :], u2b[i3][:, :], ALU.mult),
             wait=[("tV", h["y_v"]), ("tS", h["u2_v"])], inc="tV")
        hga_v = s.cnt["tV"]
        s.bank_free[("u2", i3)] = [("tV", hga_v)]
        s.do("vector", lambda e, hga=hga: e.reduce_sum(
            mrow[:, 0:1], hga[:, 0:2048], mybir.AxisListType.X),
             wait=[("tV", hga_v)], inc="tV")
        s.do("scalar", lambda e, i3=i3, hga=hga: e.activation(
            u2b[i3][:, :], hga[:, 0:2048], AF.Square),
             wait=[("tV", hga_v)], inc="tS")
        s.do("vector", lambda e, i3=i3: e.reduce_sum(vrow[:, 0:1], u2b[i3][:, :],
                                                     mybir.AxisListType.X),
             wait=[("tS", s.cnt["tS"])], inc="tV")
        sq_v = s.cnt["tV"]
        s.bank_free[("u2", i3)] = [("tV", sq_v)]
        s.do("vector", lambda e: e.tensor_scalar_mul(mrow[:, 0:1], mrow[:, 0:1], 1.0 / 2048),
             wait=[("tV", sq_v)], inc="tV")
        s.do("vector", lambda e: e.tensor_tensor(rrow[:, 0:1], mrow[:, 0:1], mrow[:, 0:1],
                                                 ALU.mult),
             wait=[("tV", s.cnt["tV"])], inc="tV")
        s.do("vector", lambda e: e.scalar_tensor_tensor(
            vrow[:, 0:1], vrow[:, 0:1], 1.0 / 2048, rrow[:, 0:1], ALU.mult, ALU.subtract),
            wait=[("tV", s.cnt["tV"])], inc="tV")
        s.do("scalar", lambda e: e.activation(rrow[:, 0:1], vrow[:, 0:1], AF.Ln,
                                              bias=eps_col[:, 0:1]),
             wait=[("tV", s.cnt["tV"])], inc="tS")
        s.do("scalar", lambda e: e.activation(rrow[:, 0:1], rrow[:, 0:1], AF.Exp, scale=-0.5),
             wait=[("tS", s.cnt["tS"])], inc="tS")
        rstd_s = s.cnt["tS"]
        s.do("vector", lambda e, hga=hga, hn=hn: e.tensor_scalar(
            hn[:, :], hga[:, 0:2048], mrow[:, 0:1], rrow[:, 0:1], ALU.subtract, ALU.mult),
            wait=[("tS", rstd_s)], inc="tV")
        s.bank_free[("xcb", i3)] = [("tV", s.cnt["tV"])]
        hn_v = s.cnt["tV"]
        prs = [(hn[:, ch * 128:(ch + 1) * 128],
                hn_fm[:, ch * 256 + (c % 2) * 128: ch * 256 + (c % 2) * 128 + 128])
               for ch in range(16)]
        emit_tpN(s, pT, "m2pT", id_bf, prs,
                 [("tV", hn_v)] + list(s.bank_free.get("hnfm", [])))
        s.bank_free[("yall", i)] = [("tP", s.cnt["tP"])]
        if c % 2 == 1:
            hnfm_v = s.cnt["tV"]
            tcol = (c - 1) * 128
            for oc in range(8):
                pb = p[oc % 2]
                pbn = f"m2p{oc % 2}"
                for kc in range(16):
                    w = []
                    if kc == 0:
                        w = [("tV", hnfm_v)]
                        if pbn in s.bank_free:
                            w.extend(s.bank_free[pbn])
                    s.do("tensor", lambda e, oc=oc, kc=kc, pb=pb: e.matmul(
                        pb[:, 0:256], win3_bf[:, kc * D + oc * 128: kc * D + oc * 128 + 128],
                        hn_fm[:, kc * 256:(kc + 1) * 256], start=(kc == 0), stop=(kc == 15)),
                        wait=w, inc="tP" if kc == 15 else None)
                pe = s.cnt["tP"]
                s.do("vector", lambda e, oc=oc, pb=pb, tcol=tcol: e.scalar_tensor_tensor(
                    x_t[:, oc * TILE + tcol: oc * TILE + tcol + 256],
                    pb[:, 0:256], cb[:, oc:oc + 1],
                    x_t[:, oc * TILE + tcol: oc * TILE + tcol + 256],
                    ALU.add, ALU.add), wait=[("tP", pe)], inc="tV")
                s.bank_free[pbn] = [("tV", s.cnt["tV"])]
            s.bank_free["hnfm"] = [("tP", s.cnt["tP"])]

    sl_sb = sb("sl_sb", [128, 1024], BF16)

    for ti in range(NT):
        xw = [("tV", s.cnt["tV"])]
        if s.cnt.get("sO", 0) > 0:
            xw.append(("sO", s.cnt["sO"]))
        s.do("sync", lambda e, ti=ti: e.dma_start(
            out=x_t.ap().rearrange("p (fc t) -> p fc t", t=TILE),
            in_=x_fm[:, :, ti * TILE:(ti + 1) * TILE]), wait=xw, inc="dX0", inc_k=16)
        dx = s.cnt["dX0"]
        s.do("sync", lambda e, ti=ti: e.dma_start(
            out=at_t[:, :], in_=at_in[:, ti * TILE:(ti + 1) * TILE]),
            wait=[("tV", s.cnt["tV"])], inc="dAT", inc_k=16)
        H[("dat", ti)] = s.cnt["dAT"]
        xlw = [("tP", H[("xlnfree", ti - 1)])] if ti > 0 else []
        s.do("sync", lambda e, ti=ti: e.dma_start(
            out=xln.ap().rearrange("p (fc t) -> p fc t", t=TILE),
            in_=xln_fm[:, :, ti * TILE:(ti + 1) * TILE]),
            wait=xlw, inc="dXL", inc_k=16)
        H[("dxl", ti)] = s.cnt["dXL"]

        g0 = ti * 4
        head(g0)
        head(g0 + 1)
        tail_sl(g0)
        head(g0 + 2)
        tail_sl(g0 + 1)
        tail_ln(g0)
        head(g0 + 3)
        H[("xlnfree", ti)] = s.cnt["tP"]
        tail_sl(g0 + 2)
        tail_ln(g0 + 1)
        tail_sl(g0 + 3)
        tail_ln(g0 + 2)
        tail_ln(g0 + 3)
        s.do("scalar", lambda e, ti=ti: e.dma_start(
            out=out_fm[:, :, ti * TILE:(ti + 1) * TILE],
            in_=x_t.ap().rearrange("p (fc t) -> p fc t", t=TILE)),
            wait=[("tV", s.cnt["tV"])], inc="sO", inc_k=16)

    s.wait("scalar", "sO", s.cnt["sO"])
    stack.close()


def build_m2():
    nc = bass.Bass("TRN2", target_bir_lowering=False)
    io = {}
    for nm, shape, dt, kind in (
            ("x", [D, TOK], F32, "ExternalInput"), ("xc", [2176, TOK], BF16, "ExternalInput"),
            ("at", [32, TOK], F32, "ExternalInput"),
            ("states", [TOK // 128 + 1, 64, 2048], F32, "ExternalInput"),
            ("hin", [64, 2048], F32, "ExternalInput"), ("w2T", [D, 2048], F32, "ExternalInput"),
            ("w3T", [2048, D], F32, "ExternalInput"), ("lng", [128, 2048], F32, "ExternalInput"),
            ("lnb", [128, 2048], F32, "ExternalInput"), ("triu", [128, 128], F32, "ExternalInput"),
            ("ident", [128, 128], F32, "ExternalInput"), ("epsc", [1, 1], F32, "ExternalInput"),
            ("out", [D, TOK], F32, "ExternalOutput")):
        io[nm] = nc.dram_tensor(nm, shape, dt, kind=kind).ap()
    s = Sched(nc)
    C = common_tiles(s, nc)
    emit_m2(s, nc, io, C)
    s.run()
    s.stack.close()
    return nc


def m_host(x2_fm_percore, W_in1, A_log_, conv_w, conv_b_, prelu_a_, W_in2, W_in3, ln_g_, ln_b_):
    CH = 2176
    epsc = np.full((1, 1), 1e-5, np.float32)
    triu_ = np.triu(np.ones((128, 128), np.float32))
    ident_ = np.eye(128, dtype=np.float32)
    w1T = wT(np.asarray(W_in1)[2048:4256, :])
    convT_ = np.ascontiguousarray(np.asarray(conv_w).transpose(1, 2, 0).reshape(CH, 4 * CH))
    nc1 = build_m1()
    in1 = []
    for c in range(N_CORES):
        if c % 2 == 1:
            halo = x2_fm_percore[c - 1][:, 4093:4096]
        else:
            halo = np.zeros((D, 3), np.float32)
        x2e = np.ascontiguousarray(np.concatenate([halo, x2_fm_percore[c]], axis=1))
        in1.append({
            "x": x2e, "w1T": w1T, "convT": convT_,
            "convb": pc(conv_b_), "prelu": np.full((1, 1), prelu_a_, np.float32),
            "alog": np.asarray(A_log_, np.float32).reshape(32, 1),
            "ident": ident_, "epsc": epsc,
        })
    r1 = run_bass_kernel_spmd(nc1, in1, core_ids=list(range(N_CORES))).results

    nc2 = build_m2()
    zeros_h = np.zeros((64, 2048), np.float32)
    in2 = []
    for c in range(N_CORES):
        in2.append({
            "x": x2_fm_percore[c],
            "xc": r1[c]["xc"], "at": r1[c]["at"], "states": r1[c]["states"],
            "hin": r1[c - 1]["states"][TOK // 128] if c % 2 == 1 else zeros_h,
            "w2T": wT(W_in2), "w3T": wT(W_in3),
            "lng": np.ascontiguousarray(np.broadcast_to(np.asarray(ln_g_, np.float32)[None, :], (128, 2048))),
            "lnb": np.ascontiguousarray(np.broadcast_to(np.asarray(ln_b_, np.float32)[None, :], (128, 2048))),
            "triu": triu_, "ident": ident_, "epsc": epsc,
        })
    r2 = run_bass_kernel_spmd(nc2, in2, core_ids=list(range(N_CORES))).results
    return [r2[c]["out"] for c in range(N_CORES)]


CC_GROUPS = [[0, 1], [2, 3], [4, 5], [6, 7]]


def build_fused():
    nc = bass.Bass("TRN2", target_bir_lowering=False)
    EI = lambda nm, shape, dt=F32: nc.dram_tensor(nm, shape, dt, kind="ExternalInput").ap()
    IN = lambda nm, shape, dt=F32: nc.dram_tensor(nm, shape, dt).ap()

    io = {
        "x": EI("x", [D, TOK]),
        "mask": EI("mask", [128, 1]),
        "WqT": EI("WqT", [D, D]), "WkT": EI("WkT", [D, D]), "WvT": EI("WvT", [D, D]),
        "WoT": EI("WoT", [D, D]),
        "bq": EI("bq", [128, 8]), "bk": EI("bk", [128, 8]), "bo": EI("bo", [128, 8]),
        "bv_bc": EI("bv_bc", [128, 1024]),
        "wA": EI("wA", [32, 128, 2048]),
        "h1P": EI("h1P", [32, 128, 4096]), "h2P": EI("h2P", [32, 128, 4096]),
        "w3P": EI("w3P", [8, 128, 4096]),
        "s12": EI("s12", [128, 64]), "sh12": EI("sh12", [128, 64]), "s3": EI("s3", [128, 8]),
        "m_w1T": EI("m_w1T", [D, DIN_P]), "convP": EI("convP", [NCH, 128, 8704]),
        "convb": EI("convb", [128, NCH]), "prelu": EI("prelu", [1, 1]),
        "alog": EI("alog", [32, 1]),
        "m_w2T": EI("m_w2T", [D, 2048]), "m_w3T": EI("m_w3T", [2048, D]),
        "lng": EI("lng", [128, 16]), "lnb": EI("lnb", [128, 16]),
        "fw0": EI("fw0", [D, D]), "fw1": EI("fw1", [D, D]), "fw2": EI("fw2", [D, D]),
        "fbias": EI("fbias", [128, 8]),
        "triu": EI("triu", [128, 128]), "ident": EI("ident", [128, 128]),
        "out": nc.dram_tensor("out", [D, TOK], F32, kind="ExternalOutput").ap(),
    }
    x1 = IN("ix1", [D, TOK])
    x2 = IN("ix2", [D, TOK])
    x3 = IN("ix3", [D, TOK])
    qp = IN("iqp", [D, TOK], BF16)
    num = IN("inum", [TOK, D], BF16)
    den = IN("iden", [128, 32])
    kv = IN("ikv", [D, D], BF16)
    z = IN("iz", [128, 8])
    kvg = IN("ikvg", [2 * D, D], BF16)
    zg = IN("izg", [256, 8])
    xc = IN("ixc", [2176, TOK], BF16)
    at = IN("iat", [32, TOK])
    states = IN("istates", [TOK // 128 + 1, 64, 2048])
    hxl = IN("ihxl", [D, 3])
    hxg = IN("ihxg", [2 * D, 3])
    stg = IN("istg", [128, 2048])
    xlns = IN("ixln", [D, TOK], BF16)

    scr = {
        "A": IN("scrA", [32, 128, 2048], BF16),
        "H1": IN("scrH1", [32, 128, 4096], BF16),
        "H2": IN("scrH2", [32, 128, 4096], BF16),
        "W3": IN("scrW3", [8, 128, 4096], BF16),
        "C": IN("scrC", [NCH, 128, 8704], BF16),
    }
    s = Sched(nc)
    C = common_tiles(s, nc)
    fence = (C["m_row"], C["v_row"])

    emit_a1(s, nc, {
        "x": io["x"], "WqT": io["WqT"], "WkT": io["WkT"], "WvT": io["WvT"],
        "bq": io["bq"], "bk": io["bk"], "bv_bc": io["bv_bc"],
        "triu": io["triu"], "ident": io["ident"],
        "qp": qp, "num": num, "den": den, "kv": kv, "z": z}, C)
    tp_a1 = s.cnt.get("tP", 0)
    pv = emit_preconv(
        s, {"A": io["wA"], "H1": io["h1P"], "H2": io["h2P"], "W3": io["w3P"]},
        scr, groups=[("A", 32), ("H1", 32), ("H2", 32), ("W3", 8)],
        pace=(0, (tp_a1 * 9) // 10))
    s.barrier(fence)
    s.do("gpsimd", lambda e: e.collective_compute(
        "AllGather", ALU.bypass, replica_groups=CC_GROUPS, ins=[kv], outs=[kvg]),
        inc="cc", inc_k=1)
    s.do("gpsimd", lambda e: e.collective_compute(
        "AllGather", ALU.bypass, replica_groups=CC_GROUPS, ins=[z], outs=[zg]),
        inc="cc", inc_k=1)
    s.barrier(fence)
    emit_a2(s, nc, {
        "x": io["x"], "qp": qp, "num": num, "den": den,
        "kvp": kvg[0:D, :], "zp": zg[0:128, :], "mask": io["mask"],
        "WoT": io["WoT"], "bo": io["bo"], "ident": io["ident"], "out": x1}, C)
    s.barrier(fence)
    tp_a2 = s.cnt.get("tP", 0)
    emit_f0(s, nc, {
        "x": x1, "s12": io["s12"], "sh12": io["sh12"],
        "s3": io["s3"], "out": x2, "scr": scr, "pv": pv,
        "tile_order": [NT - 1] + list(range(NT - 1)),
        "halo_cb": lambda w: s.do(
            "sync", lambda e: e.dma_start(out=hxl, in_=x2[:, TOK - 3:TOK]),
            wait=w, inc="dHx", inc_k=16)}, C)
    pv.update(emit_preconv(
        s, {"C": io["convP"]}, scr, groups=[("C", NCH)],
        pace=(tp_a2, tp_a2 + ((s.cnt.get("tP", 0) - tp_a2) * 8) // 10)))
    s.do("gpsimd", lambda e: e.collective_compute(
        "AllGather", ALU.bypass, replica_groups=CC_GROUPS, ins=[hxl], outs=[hxg]),
        wait=[("dHx", s.cnt.get("dHx", 16))], inc="cc", inc_k=1)
    s.barrier(fence)
    emit_m1(s, nc, {
        "x": x2, "xh": hxg[0:D, :], "hmask": io["mask"],
        "w1T": io["m_w1T"], "convb": io["convb"],
        "prelu": io["prelu"], "alog": io["alog"], "ident": io["ident"],
        "xc": xc, "at": at, "states": states, "scr": scr, "pv": pv,
        "xlns": xlns}, C)
    s.barrier(fence)
    s.do("gpsimd", lambda e: e.collective_compute(
        "AllGather", ALU.bypass, replica_groups=CC_GROUPS,
        ins=[states[TOK // 128]], outs=[stg]), inc="cc", inc_k=1)
    s.barrier(fence)
    emit_m2(s, nc, {
        "x": x2, "xc": xc, "at": at, "states": states,
        "hin": stg[0:64, :], "hmask": io["mask"],
        "w2T": io["m_w2T"], "w3T": io["m_w3T"], "lng": io["lng"], "lnb": io["lnb"],
        "triu": io["triu"], "ident": io["ident"], "out": x3,
        "xln_in": xlns}, C)
    scrF1 = IN("scrF1", [3, 128, 8 * D], BF16)
    for i, wnm in enumerate(("fw0", "fw1", "fw2")):
        s.do("gpsimd", lambda e, i=i, wnm=wnm: e.dma_start(
            out=scrF1[i].rearrange("p (kc m) -> p kc m", m=D),
            in_=io[wnm].rearrange("(kc kp) m -> kp kc m", kp=128)),
            inc="dF1p", inc_k=16)
    f1_pre = [("dF1p", s.cnt["dF1p"])]
    s.barrier(fence)
    emit_f1(s, nc, {
        "x": x3, "w0": io["fw0"], "w1": io["fw1"], "w2": io["fw2"],
        "bias": io["fbias"], "out": io["out"],
        "preW": (scrF1, f1_pre)}, C)
    s.run()
    s.stack.close()
    return nc


def kernel_fused(**inp):
    CH = 2176
    triu_ = np.triu(np.ones((128, 128), np.float32))
    ident_ = np.eye(128, dtype=np.float32)
    x = np.asarray(inp["L0_attn_x"], np.float32)
    nc = build_fused()
    s12_ = np.concatenate([pc(inp["L0_ffn_w1_s"]), pc(inp["L0_ffn_w2_s"])], axis=1)
    sh12_ = np.concatenate([pc(inp["L0_ffn_h1_s"]), pc(inp["L0_ffn_h2_s"])], axis=1)
    common = {
        "WqT": wT(inp["L0_attn_Wq"]), "WkT": wT(inp["L0_attn_Wk"]),
        "WvT": wT(inp["L0_attn_Wv"]), "WoT": wT(inp["L0_attn_Wo"]),
        "bq": pc(inp["L0_attn_bq"]), "bk": pc(inp["L0_attn_bk"]),
        "bo": pc(inp["L0_attn_bo"]),
        "bv_bc": np.ascontiguousarray(np.broadcast_to(
            np.asarray(inp["L0_attn_bv"], np.float32)[None, :], (128, 1024))),
        "wA": np.ascontiguousarray(np.concatenate(
            [panP(wT(inp["L0_ffn_w1_w"]), 8, 32), panP(wT(inp["L0_ffn_w2_w"]), 8, 32)],
            axis=2)),
        "h1P": panP(wT(inp["L0_ffn_h1_w"]), 32, 32),
        "h2P": panP(wT(inp["L0_ffn_h2_w"]), 32, 32),
        "w3P": panP(wT(inp["L0_ffn_w3_w"]), 32, 8),
        "s12": s12_, "sh12": sh12_, "s3": pc(inp["L0_ffn_w3_s"]),
        "m_w1T": wT(np.asarray(inp["L1_attn_W_in1"], np.float32)[2048:4256, :]),
        "convP": np.ascontiguousarray(
            np.asarray(inp["L1_attn_conv_w"], np.float32).transpose(1, 2, 0).reshape(CH, 4 * CH)
            .reshape(17, 128, 4, 17, 128).transpose(3, 1, 0, 2, 4).reshape(17, 128, 8704)),
        "convb": pc(inp["L1_attn_conv_b"]),
        "prelu": np.asarray(inp["L1_attn_prelu_a"], np.float32).reshape(1, 1),
        "alog": np.asarray(inp["L1_attn_A_log"], np.float32).reshape(32, 1),
        "m_w2T": wT(inp["L1_attn_W_in2"]), "m_w3T": wT(inp["L1_attn_W_in3"]),
        "lng": pc(inp["L1_attn_ln_g"]),
        "lnb": pc(inp["L1_attn_ln_b"]),
        "fw0": np.ascontiguousarray(np.asarray(inp["L1_ffn_weight"], np.float32)),
        "fw1": np.ascontiguousarray(np.asarray(inp["L1_ffn_aux_weight1"], np.float32)),
        "fw2": np.ascontiguousarray(np.asarray(inp["L1_ffn_aux_weight2"], np.float32)),
        "fbias": pc(inp["L1_ffn_bias"]),
        "triu": triu_, "ident": ident_,
    }
    in_maps = []
    for c in range(N_CORES):
        b, h = c // 2, c % 2
        m = dict(common)
        m["x"] = fm(x[b, h * TOK:(h + 1) * TOK, :])
        m["mask"] = np.full((128, 1), 1.0 if h == 1 else 0.0, np.float32)
        in_maps.append(m)
    res = run_bass_kernel_spmd(nc, in_maps, core_ids=list(range(N_CORES))).results
    out = np.empty((4, 8192, 1024), np.float32)
    for c in range(N_CORES):
        b, h = c // 2, c % 2
        out[b, h * TOK:(h + 1) * TOK, :] = np.asarray(res[c]["out"]).T
    return out


def kernel(**inp):
    CH = 2176
    epsc = np.full((1, 1), 1e-5, np.float32)
    triu_ = np.triu(np.ones((128, 128), np.float32))
    ident_ = np.eye(128, dtype=np.float32)

    x = np.asarray(inp["L0_attn_x"], np.float32)
    x_pc = []
    for c in range(N_CORES):
        b, h = c // 2, c % 2
        x_pc.append(fm(x[b, h * TOK:(h + 1) * TOK, :]))

    cores = list(range(N_CORES))

    nc = build_a1()
    bv_row_ = np.ascontiguousarray(np.broadcast_to(
        np.asarray(inp["L0_attn_bv"], np.float32)[None, :], (128, 1024)))
    in1 = [{
        "x": x_pc[c],
        "WqT": wT(inp["L0_attn_Wq"]), "WkT": wT(inp["L0_attn_Wk"]),
        "WvT": wT(inp["L0_attn_Wv"]),
        "bq": pc(inp["L0_attn_bq"]), "bk": pc(inp["L0_attn_bk"]), "bv_bc": bv_row_,
        "triu": triu_, "ident": ident_, "epsc": epsc,
    } for c in cores]
    r1 = run_bass_kernel_spmd(nc, in1, core_ids=cores).results

    nc = build_a2()
    zeros_kv = np.zeros((D, D), np.float32)
    zeros_z = np.zeros((128, 8), np.float32)
    in2 = [{
        "x": x_pc[c],
        "qp": r1[c]["qp"], "num": r1[c]["num"], "den": r1[c]["den"],
        "kvp": r1[c - 1]["kv"] if c % 2 == 1 else zeros_kv,
        "zp": r1[c - 1]["z"] if c % 2 == 1 else zeros_z,
        "WoT": wT(inp["L0_attn_Wo"]), "bo": pc(inp["L0_attn_bo"]), "ident": ident_,
    } for c in cores]
    r2 = run_bass_kernel_spmd(nc, in2, core_ids=cores).results
    x1_pc = [np.ascontiguousarray(r2[c]["out"]) for c in cores]

    nc = build_f0()
    s12_ = np.concatenate([pc(inp["L0_ffn_w1_s"]), pc(inp["L0_ffn_w2_s"])], axis=1)
    sh12_ = np.concatenate([pc(inp["L0_ffn_h1_s"]), pc(inp["L0_ffn_h2_s"])], axis=1)
    in3 = [{
        "x": x1_pc[c],
        "w1T": wT(inp["L0_ffn_w1_w"]), "w2T": wT(inp["L0_ffn_w2_w"]),
        "h1T": wT(inp["L0_ffn_h1_w"]), "h2T": wT(inp["L0_ffn_h2_w"]),
        "w3T": wT(inp["L0_ffn_w3_w"]),
        "s12": s12_, "sh12": sh12_, "s3": pc(inp["L0_ffn_w3_s"]), "epsc": epsc,
    } for c in cores]
    r3 = run_bass_kernel_spmd(nc, in3, core_ids=cores).results
    x2_pc = [np.ascontiguousarray(r3[c]["out"]) for c in cores]

    w1T_ = wT(np.asarray(inp["L1_attn_W_in1"], np.float32)[2048:4256, :])
    convT_ = np.ascontiguousarray(
        np.asarray(inp["L1_attn_conv_w"], np.float32).transpose(1, 2, 0).reshape(CH, 4 * CH))
    nc = build_m1()
    in4 = []
    for c in cores:
        halo = x2_pc[c - 1][:, TOK - 3:TOK] if c % 2 == 1 else np.zeros((D, 3), np.float32)
        in4.append({
            "x": np.ascontiguousarray(np.concatenate([halo, x2_pc[c]], axis=1)),
            "w1T": w1T_, "convT": convT_,
            "convb": pc(inp["L1_attn_conv_b"]),
            "prelu": np.asarray(inp["L1_attn_prelu_a"], np.float32).reshape(1, 1),
            "alog": np.asarray(inp["L1_attn_A_log"], np.float32).reshape(32, 1),
            "ident": ident_, "epsc": epsc,
        })
    r4 = run_bass_kernel_spmd(nc, in4, core_ids=cores).results

    nc = build_m2()
    zeros_h = np.zeros((64, 2048), np.float32)
    lng_ = np.ascontiguousarray(np.broadcast_to(
        np.asarray(inp["L1_attn_ln_g"], np.float32)[None, :], (128, 2048)))
    lnb_ = np.ascontiguousarray(np.broadcast_to(
        np.asarray(inp["L1_attn_ln_b"], np.float32)[None, :], (128, 2048)))
    in5 = [{
        "x": x2_pc[c],
        "xc": r4[c]["xc"], "at": r4[c]["at"], "states": r4[c]["states"],
        "hin": r4[c - 1]["states"][TOK // 128] if c % 2 == 1 else zeros_h,
        "w2T": wT(inp["L1_attn_W_in2"]), "w3T": wT(inp["L1_attn_W_in3"]),
        "lng": lng_, "lnb": lnb_, "triu": triu_, "ident": ident_, "epsc": epsc,
    } for c in cores]
    r5 = run_bass_kernel_spmd(nc, in5, core_ids=cores).results
    x3_pc = [np.ascontiguousarray(r5[c]["out"]) for c in cores]

    nc = build_f1()
    in6 = [{
        "x": x3_pc[c],
        "w0": np.ascontiguousarray(np.asarray(inp["L1_ffn_weight"], np.float32)),
        "w1": np.ascontiguousarray(np.asarray(inp["L1_ffn_aux_weight1"], np.float32)),
        "w2": np.ascontiguousarray(np.asarray(inp["L1_ffn_aux_weight2"], np.float32)),
        "bias": pc(inp["L1_ffn_bias"]), "epsc": epsc,
    } for c in cores]
    r6 = run_bass_kernel_spmd(nc, in6, core_ids=cores).results

    out = np.empty((4, 8192, 1024), np.float32)
    for c in cores:
        b, h = c // 2, c % 2
        out[b, h * TOK:(h + 1) * TOK, :] = np.asarray(r6[c]["out"]).T
    return out





kernel_unfused = kernel
kernel = kernel_fused


# version 18
# speedup vs baseline: 1.0866x; 1.0866x over previous
import numpy as np
from contextlib import ExitStack
import concourse.bass as bass
import concourse.mybir as mybir
from concourse.bass_utils import run_bass_kernel_spmd

F32 = mybir.dt.float32
BF16 = mybir.dt.bfloat16
AF = mybir.ActivationFunctionType
ALU = mybir.AluOpType

TOK = 4096
N_CORES = 8
A1_LEVEL = 5
A1_PHI = "full"
A1_VTM = True
A1_TP = True
D = 1024
TILE = 512
NT = TOK // TILE

ENGINES = ("tensor", "vector", "scalar", "gpsimd", "sync")


class Sched:
    def __init__(self, nc):
        self.nc = nc
        self.q = {e: [] for e in ENGINES}
        self.sems = {}
        self.cnt = {}
        self.stack = ExitStack()
        self.bank_free = {}
        self.pending = {}

    def barrier(self, fence=None):
        if fence is not None:
            fa, fb = fence
            self.do("vector", lambda e: e.memset(fa[0:1, 0:1], 0.0), inc="tV")
            self.do("scalar", lambda e: e.copy(fb[0:1, 0:1], fa[0:1, 0:1]),
                    wait=[("tV", self.cnt["tV"])], inc="tS")
        snap = [(n, c) for n, c in self.cnt.items()
                if c > 0 and not n.startswith("pv")]
        for e in ENGINES:
            self.pending[e] = list(snap)

    def sem(self, name):
        if name not in self.sems:
            self.sems[name] = self.stack.enter_context(self.nc.semaphore(name))
            self.cnt[name] = 0
        return self.sems[name]

    def do(self, eng, fn, wait=(), inc=None, inc_k=1):
        wait = list(wait)
        if self.pending.get(eng):
            wait = self.pending.pop(eng) + wait
        sem_objs = [(self.sem(n), v) for n, v in wait if v > 0]
        tgt = None
        if inc is not None:
            self.sem(inc)
            self.cnt[inc] += inc_k
            tgt = (self.sems[inc], inc_k)

        def runner(e, sem_objs=sem_objs, fn=fn, tgt=tgt):
            for s_, v in sem_objs:
                e.wait_ge(s_, v)
            ins = fn(e)
            if tgt is not None:
                ins.then_inc(tgt[0], tgt[1])
        self.q[eng].append(runner)
        return self.cnt.get(inc) if inc else None

    def wait(self, eng, name, value):
        if value <= 0:
            return
        sem = self.sem(name)
        self.q[eng].append(lambda e: e.wait_ge(sem, value))

    def run(self):
        nc = self.nc
        with nc.Block() as block:
            for ename in ENGINES:
                def make(ename):
                    def body(e):
                        for f in self.q[ename]:
                            f(e)
                    return body
                getattr(block, ename)(make(ename))


def fm(x):
    return np.ascontiguousarray(np.asarray(x, dtype=np.float32).T)


def pc(v):
    v = np.asarray(v, dtype=np.float32)
    return np.ascontiguousarray(v.reshape(-1, 128).T)


def wT(w):
    return np.ascontiguousarray(np.asarray(w, dtype=np.float32).T)


def emit_ln(s, C, x_ap, xln, t_tok, first_wait=()):
    nF = 8
    s.do("vector", lambda e: e.tensor_copy(C["xb"][:, 0:nF * t_tok], x_ap),
         wait=first_wait, inc="tV")
    vc = s.cnt["tV"]
    for fc in range(nF):
        s.do("tensor", lambda e, fc=fc: e.matmul(
            C["ps_a"][0:1, 0:t_tok], C["ones_bf"][:, 0:1], C["xb"][:, fc * t_tok:(fc + 1) * t_tok],
            start=(fc == 0), stop=(fc == nF - 1)),
            wait=[("tV", vc)] if fc == 0 else (),
            inc="tP" if fc == nF - 1 else None)
    pe0 = s.cnt["tP"]
    s.do("scalar", lambda e: e.activation(C["xb"][:, 0:nF * t_tok], C["xb"][:, 0:nF * t_tok], AF.Square),
         wait=[("tV", vc), ("tP", pe0)], inc="tS")
    sc = s.cnt["tS"]
    for fc in range(nF):
        s.do("tensor", lambda e, fc=fc: e.matmul(
            C["ps_b"][0:1, 0:t_tok], C["ones_bf"][:, 0:1], C["xb"][:, fc * t_tok:(fc + 1) * t_tok],
            start=(fc == 0), stop=(fc == nF - 1)),
            wait=[("tS", sc)] if fc == 0 else (),
            inc="tP" if fc == nF - 1 else None)
    pe1 = s.cnt["tP"]
    s.do("vector", lambda e: e.tensor_scalar_mul(C["m_row"][0:1, 0:t_tok], C["ps_a"][0:1, 0:t_tok], 1.0 / D),
         wait=[("tP", pe1)], inc="tV")
    s.do("vector", lambda e: e.tensor_tensor(C["v_row"][0:1, 0:t_tok], C["m_row"][0:1, 0:t_tok],
                                             C["m_row"][0:1, 0:t_tok], ALU.mult),
         wait=[("tV", s.cnt["tV"])], inc="tV")
    s.do("vector", lambda e: e.scalar_tensor_tensor(
        C["v_row"][0:1, 0:t_tok], C["ps_b"][0:1, 0:t_tok], 1.0 / D, C["v_row"][0:1, 0:t_tok],
        ALU.mult, ALU.subtract), wait=[("tV", s.cnt["tV"])], inc="tV")
    vc2 = s.cnt["tV"]
    s.do("scalar", lambda e: e.activation(C["r_row"][0:1, 0:t_tok], C["v_row"][0:1, 0:t_tok],
                                          AF.Ln, bias=C["eps_v"][0:1, 0:1]), wait=[("tV", vc2)], inc="tS")
    s.do("scalar", lambda e: e.activation(C["r_row"][0:1, 0:t_tok], C["r_row"][0:1, 0:t_tok],
                                          AF.Exp, scale=-0.5), wait=[("tS", s.cnt["tS"])], inc="tS")
    sc2 = s.cnt["tS"]
    s.do("vector", lambda e: e.scalar_tensor_tensor(
        C["b_row"][0:1, 0:t_tok], C["m_row"][0:1, 0:t_tok], -1.0, C["r_row"][0:1, 0:t_tok],
        ALU.mult, ALU.mult), wait=[("tS", sc2)], inc="tV")
    vc3 = s.cnt["tV"]
    s.do("tensor", lambda e: e.matmul(C["ps_a"][:, 0:t_tok], C["ones_f"][0:1, :],
                                      C["r_row"][0:1, 0:t_tok], start=True, stop=True),
         wait=[("tS", sc2)])
    s.do("tensor", lambda e: e.matmul(C["ps_b"][:, 0:t_tok], C["ones_f"][0:1, :],
                                      C["b_row"][0:1, 0:t_tok], start=True, stop=True),
         wait=[("tV", vc3)], inc="tP")
    pe2 = s.cnt["tP"]
    tmp_w = {}
    tmp_r = {}

    def emit_tmp(fc):
        w = ([("tP", pe2)] if fc == 0 else [])
        if fc - 2 in tmp_r:
            w.append(("tV", tmp_r[fc - 2]))
        s.do("vector", lambda e, fc=fc: e.tensor_tensor(
            C["tmp"][fc % 2][:, 0:t_tok], x_ap[:, fc * t_tok:(fc + 1) * t_tok],
            C["ps_a"][:, 0:t_tok], ALU.mult), wait=w, inc="tV")
        tmp_w[fc] = s.cnt["tV"]

    def emit_xln(fc):
        s.do("vector", lambda e, fc=fc: e.tensor_tensor(
            xln[:, fc * t_tok:(fc + 1) * t_tok], C["tmp"][fc % 2][:, 0:t_tok],
            C["ps_b"][:, 0:t_tok], ALU.add), wait=[("tV", tmp_w[fc])], inc="tV")
        tmp_r[fc] = s.cnt["tV"]
    emit_tmp(0)
    for fc in range(1, nF):
        emit_tmp(fc)
        emit_xln(fc - 1)
    emit_xln(nF - 1)
    return s.cnt["tV"]


def common_tiles(s, nc, t_tok=TILE):
    sb = lambda name, shape, dt=F32: s.stack.enter_context(nc.sbuf_tensor(name, shape, dt))
    ps = lambda name, shape, dt=F32: s.stack.enter_context(nc.psum_tensor(name, shape, dt))
    C = {}
    C["xb"] = sb("c_xb", [128, 8 * t_tok], BF16)
    C["tmp"] = [sb(f"c_tmp{i}", [128, t_tok]) for i in range(2)]
    C["m_row"] = sb("c_m", [1, t_tok])
    C["v_row"] = sb("c_v", [1, t_tok])
    C["r_row"] = sb("c_r", [1, t_tok])
    C["b_row"] = sb("c_b", [1, t_tok])
    C["ones_bf"] = sb("c_1bf", [128, 1], BF16)
    C["ones_f"] = sb("c_1f", [1, 128])
    C["eps_v"] = sb("c_eps", [1, 1])
    s.do("vector", lambda e: e.memset(C["ones_bf"][:, 0:1], 1.0))
    s.do("vector", lambda e: e.memset(C["eps_v"][0:1, 0:1], 1e-5))
    s.do("vector", lambda e: e.memset(C["ones_f"][0:1, :], 1.0), inc="tV")
    return C



def emit_tp8(s, pTs, pT_name, id_bf, srcs, dsts, first_wait):
    tpc = {}

    def emit_t(fc):
        slot = fc % 2
        w = list(first_wait) if fc == 0 else []
        if (pT_name, slot) in s.bank_free:
            w.extend(s.bank_free[(pT_name, slot)])
        s.do("tensor", lambda e, fc=fc, slot=slot: e.transpose(
            pTs[slot][:, 0:128], srcs[fc], id_bf[:, :]),
            wait=w, inc="tP")
        tpc[fc] = s.cnt["tP"]

    def emit_c(fc):
        slot = fc % 2
        s.do("vector", lambda e, fc=fc, slot=slot: e.tensor_copy(
            dsts[fc], pTs[slot][:, 0:128]),
            wait=[("tP", tpc[fc])], inc="tV")
        s.bank_free[(pT_name, slot)] = [("tV", s.cnt["tV"])]
    emit_t(0)
    emit_t(1)
    for fc in range(2, 8):
        emit_c(fc - 2)
        emit_t(fc)
    emit_c(6)
    emit_c(7)
    return s.cnt["tV"]


def emit_preconv(s, io, scr, groups, pace=(0, 0)):
    pv = {}
    n_conv = sum(n_oc for _, n_oc in groups)
    tp0, tp1 = pace
    i = 0
    for stage, n_oc in groups:
        src = io[stage]
        for oc in range(n_oc):
            thr = tp0 + ((tp1 - tp0) * i) // n_conv
            w = [("tP", thr)] if thr > 0 else []
            s.do("gpsimd", lambda e, oc=oc, src=src, stage=stage: e.dma_start(
                out=scr[stage][oc], in_=src[oc]), wait=w, inc="pvw", inc_k=16)
            pv[(stage, oc)] = ("pvw", s.cnt["pvw"])
            i += 1
    return pv


def panP(w, kc_n, oc_n):
    return np.ascontiguousarray(
        np.asarray(w, np.float32).reshape(kc_n, 128, oc_n, 128)
        .transpose(2, 1, 0, 3).reshape(oc_n, 128, kc_n * 128))


def emit_f1(s, nc, io, C, pre="f1_"):
    x_in = io["x"]; ws = [io["w0"], io["w1"], io["w2"]]; bias = io["bias"]
    out = io["out"]
    stack = ExitStack()
    sb = lambda name, shape, dt=F32: stack.enter_context(nc.sbuf_tensor(pre + name, shape, dt))
    ps = lambda name, shape, dt=F32: stack.enter_context(nc.psum_tensor(pre + name, shape, dt))
    C = dict(C)
    C["ps_a"] = ps("lnps_a", [128, TILE])
    C["ps_b"] = ps("lnps_b", [128, TILE])
    x_t = [sb(f"x{i}", [128, 8 * TILE]) for i in range(2)]
    xln = sb("xln", [128, 8 * TILE], BF16)
    h = [[sb(f"h{i}_{b}", [128, 8 * TILE], BF16) for b in range(2)] for i in range(2)]
    o_t = [sb(f"o{i}", [128, 8 * TILE]) for i in range(2)]
    t_act = [sb(f"t_act{i}", [128, TILE]) for i in range(2)]
    t_act_free = {}
    ps_h = [ps(f"ps_h{i}", [128, TILE]) for i in range(2)]

    if "preW" in io:
        scrF1, pre_w = io["preW"]
        w_bf = [sb(f"w_bf{i}", [128, 8 * D], BF16) for i in range(3)]
        bias_pc = sb("bias_pc", [128, 8])
        dws = []
        for i in range(3):
            s.do("gpsimd", lambda e, i=i: e.dma_start(
                out=w_bf[i][:, 0:8 * D], in_=scrF1[i]),
                wait=(list(pre_w) if i == 0 else []), inc="dW", inc_k=16)
            dws.append(("dW", s.cnt["dW"]))
        s.do("sync", lambda e: e.dma_start(out=bias_pc[:, :], in_=bias), inc="dC", inc_k=16)
        f1_first = [dws[0], ("dC", s.cnt["dC"])]
        stage_dw = [[], [dws[1]], [dws[2]]]
    else:
        w_bf = [sb(f"w_bf{i}", [128, 8 * D], BF16) for i in range(3)]
        bias_pc = sb("bias_pc", [128, 8])
        for i in range(3):
            s.do("gpsimd", lambda e, i=i: e.dma_start(
                out=w_bf[i].ap().rearrange("p (kc m) -> p kc m", m=D),
                in_=ws[i].rearrange("(kc kp) m -> kp kc m", kp=128)),
                inc="dW", inc_k=16)
        f1_dw = s.cnt["dW"]
        s.do("sync", lambda e: e.dma_start(out=bias_pc[:, :], in_=bias), inc="dC", inc_k=16)
        f1_dc = s.cnt["dC"]
        stage_dw = [[], [], []]
        f1_first = [("dW", f1_dw), ("dC", f1_dc)]

    x_fm = x_in.rearrange("(fc fp) t -> fp fc t", fp=128)
    out_fm = out.rearrange("(fc fp) t -> fp fc t", fp=128)

    tile_done_v = {}
    for ti in range(NT):
        buf = ti % 2
        xw = [("tV", tile_done_v.get(ti - 2, 0))]
        s.do("sync", lambda e, ti=ti, buf=buf: e.dma_start(
            out=x_t[buf].ap().rearrange("p (fc t) -> p fc t", t=TILE),
            in_=x_fm[:, :, ti * TILE:(ti + 1) * TILE]),
            wait=xw, inc=f"dX{buf}", inc_k=16)
        dx = s.cnt[f"dX{buf}"]
        first = [(f"dX{buf}", dx)] + (f1_first if ti == 0 else [])
        vln = emit_ln(s, C, x_t[buf][:, :], xln, TILE, first_wait=first)

        srcs = [xln, h[0][buf], h[1][buf]]
        for stage in range(3):
            src = srcs[stage]
            stage_in_ready = s.cnt["tV"]
            for oc in range(8):
                pb = ps_h[oc % 2]
                pbname = f"ps_h{oc % 2}"
                for kc in range(8):
                    w = []
                    if kc == 0:
                        w.append(("tV", stage_in_ready if stage > 0 else vln))
                        if ti == 0:
                            w.extend(stage_dw[stage])
                        if pbname in s.bank_free:
                            w.extend(s.bank_free[pbname])
                    s.do("tensor", lambda e, stage=stage, oc=oc, kc=kc, pb=pb, src=src: e.matmul(
                        pb[:, :], w_bf[stage][:, kc * D + oc * 128: kc * D + oc * 128 + 128],
                        src[:, kc * TILE:(kc + 1) * TILE],
                        start=(kc == 0), stop=(kc == 7)),
                        wait=w, inc="tP" if kc == 7 else None)
                pe = s.cnt["tP"]
                if stage < 2:
                    s.do("vector", lambda e, stage=stage, oc=oc, pb=pb, buf=buf: e.tensor_copy(
                        h[stage][buf][:, oc * TILE:(oc + 1) * TILE], pb[:, :]),
                        wait=[("tP", pe)], inc="tV")
                    s.bank_free[pbname] = [("tV", s.cnt["tV"])]
                else:
                    ta = t_act[oc % 2]
                    sw = [("tP", pe)]
                    if (oc % 2) in t_act_free:
                        sw.append(t_act_free[oc % 2])
                    s.do("scalar", lambda e, oc=oc, pb=pb, ta=ta: e.activation(
                        ta[:, :], pb[:, :], AF.Tanh, bias=bias_pc[:, oc:oc + 1]),
                        wait=sw, inc="tS")
                    s.bank_free[pbname] = [("tS", s.cnt["tS"])]
                    stc = s.cnt["tS"]
                    rw = [("tS", stc)]
                    if oc == 0 and ti >= 2:
                        rw.append((f"sO{buf}", s.cnt.get(f"sO{buf}", 0)))
                    s.do("vector", lambda e, oc=oc, buf=buf, ta=ta: e.tensor_tensor(
                        o_t[buf][:, oc * TILE:(oc + 1) * TILE], ta[:, :],
                        x_t[buf][:, oc * TILE:(oc + 1) * TILE], ALU.add),
                        wait=rw, inc="tV")
                    t_act_free[oc % 2] = ("tV", s.cnt["tV"])
        tile_done_v[ti] = s.cnt["tV"]
        s.do("scalar", lambda e, ti=ti, buf=buf: e.dma_start(
            out=out_fm[:, :, ti * TILE:(ti + 1) * TILE],
            in_=o_t[buf].ap().rearrange("p (fc t) -> p fc t", t=TILE)),
            wait=[("tV", tile_done_v[ti])], inc=f"sO{buf}", inc_k=16)

    s.wait("scalar", "sO0", s.cnt["sO0"])
    s.wait("scalar", "sO1", s.cnt["sO1"])
    stack.close()


def build_f1():
    nc = bass.Bass("TRN2", target_bir_lowering=False)
    io = {}
    for nm, shape, dt, kind in (
            ("x", [D, TOK], F32, "ExternalInput"), ("w0", [D, D], F32, "ExternalInput"),
            ("w1", [D, D], F32, "ExternalInput"), ("w2", [D, D], F32, "ExternalInput"),
            ("bias", [128, 8], F32, "ExternalInput"), ("epsc", [1, 1], F32, "ExternalInput"),
            ("out", [D, TOK], F32, "ExternalOutput")):
        io[nm] = nc.dram_tensor(nm, shape, dt, kind=kind).ap()
    s = Sched(nc)
    C = common_tiles(s, nc)
    emit_f1(s, nc, io, C)
    s.run()
    s.stack.close()
    return nc


def f1_host(x3_fm_percore, weight, aux1, aux2, bias_):
    nc = build_f1()
    epsc = np.full((1, 1), 1e-5, np.float32)
    in_maps = [{
        "x": x3_fm_percore[c],
        "w0": np.ascontiguousarray(weight), "w1": np.ascontiguousarray(aux1),
        "w2": np.ascontiguousarray(aux2),
        "bias": pc(bias_), "epsc": epsc,
    } for c in range(N_CORES)]
    res = run_bass_kernel_spmd(nc, in_maps, core_ids=list(range(N_CORES)))
    return [res.results[c]["out"] for c in range(N_CORES)]


def emit_f0(s, nc, io, C, pre="f0_"):
    x_in = io["x"]; w1T = io.get("w1T"); w2T = io.get("w2T"); h1T = io.get("h1T")
    h2T = io.get("h2T"); w3T = io.get("w3T"); s12 = io["s12"]; sh12 = io["sh12"]
    s3 = io["s3"]; out = io["out"]
    pv = io.get("pv")
    scr = io.get("scr") or {
        "A": nc.dram_tensor(pre + "scrA", [32, 128, 2048], BF16).ap(),
        "H1": nc.dram_tensor(pre + "scrH1", [32, 128, 4096], BF16).ap(),
        "H2": nc.dram_tensor(pre + "scrH2", [32, 128, 4096], BF16).ap(),
        "W3": nc.dram_tensor(pre + "scrW3", [8, 128, 4096], BF16).ap(),
    }
    stack = ExitStack()
    sb = lambda name, shape, dt=F32: stack.enter_context(nc.sbuf_tensor(pre + name, shape, dt))
    ps = lambda name, shape, dt=F32: stack.enter_context(nc.psum_tensor(pre + name, shape, dt))
    C = dict(C)
    C["ps_a"] = ps("lnps_a", [128, TILE])
    C["ps_b"] = ps("lnps_b", [128, TILE])
    s12_pc = sb("s12_pc", [128, 64])
    sh12_pc = sb("sh12_pc", [128, 64])
    s3_pc = sb("s3_pc", [128, 8])
    x_t = [sb(f"x{i}", [128, 8 * TILE]) for i in range(2)]
    xln = sb("xln", [128, 8 * TILE], BF16)
    g0 = sb("g0", [128, 32 * TILE], BF16)
    g1 = sb("g1", [128, 32 * TILE], BF16)
    o_t = [sb(f"o{i}", [128, 8 * TILE]) for i in range(2)]
    t_act = [sb(f"t_act{i}", [128, TILE]) for i in range(2)]
    pA = [sb(f"pA{i}", [128, 2048], BF16) for i in range(2)]
    pH = [sb(f"pH{i}", [128, 4096], BF16) for i in range(2)]
    ps1 = [ps(f"f0ps1_{i}", [128, TILE]) for i in range(2)]
    ps2 = [ps(f"f0ps2_{i}", [128, TILE]) for i in range(2)]

    for dst, src in ((s12_pc[:, :], s12), (sh12_pc[:, :], sh12), (s3_pc[:, :], s3)):
        s.do("sync", lambda e, dst=dst, src=src: e.dma_start(out=dst, in_=src), inc="dC", inc_k=16)
    f0_dc = s.cnt["dC"]

    x_fm = x_in.rearrange("(fc fp) t -> fp fc t", fp=128)
    out_fm = out.rearrange("(fc fp) t -> fp fc t", fp=128)
    if pv is None:
        fsrc = {
            "A": None,
            "H1": h1T.rearrange("(kc kp) m -> kp kc m", kp=128),
            "H2": h2T.rearrange("(kc kp) m -> kp kc m", kp=128),
            "W3": w3T.rearrange("(kc kp) m -> kp kc m", kp=128),
        }
        w1f = w1T.rearrange("(kc kp) m -> kp kc m", kp=128)
        w2f = w2T.rearrange("(kc kp) m -> kp kc m", kp=128)

    plan = ([("A", oc) for oc in range(32)] + [("H1", oc) for oc in range(32)]
            + [("H2", oc) for oc in range(32)] + [("W3", oc) for oc in range(8)])

    panel_written = {}
    pan_free = {}
    scratch_dep = {}
    loads_dw = {}

    def issue_load(it, k):
        stage, oc = plan[k]
        bufs, nb = (pA, 2048) if stage == "A" else (pH, 4096)
        i = (k % 2)
        bid = ("A" if stage == "A" else "H") + str(i)
        sem = ("dA" if stage == "A" else "dH") + str(i) + ("c" if it == 0 else "")
        w = list(pan_free.get(bid, []))
        if pv is not None:
            if it == 0:
                w.append(pv[(stage, oc)])
            s.do("sync", lambda e, oc=oc, i=i, sc_=scr[stage], bufs=bufs, nb=nb: e.dma_start(
                out=bufs[i][:, 0:nb], in_=sc_[oc]), wait=w, inc=sem, inc_k=16)
            loads_dw[(it, stage, oc)] = (sem, s.cnt[sem])
            return
        if it == 0:
            if stage == "A":
                s.do("gpsimd", lambda e, oc=oc, i=i: e.dma_start(
                    out=pA[i].ap().rearrange("p (w kc m) -> p w kc m", w=2, m=128)[:, 0, :, :],
                    in_=w1f[:, :, oc * 128:(oc + 1) * 128]), wait=w, inc=sem, inc_k=16)
                s.do("gpsimd", lambda e, oc=oc, i=i: e.dma_start(
                    out=pA[i].ap().rearrange("p (w kc m) -> p w kc m", w=2, m=128)[:, 1, :, :],
                    in_=w2f[:, :, oc * 128:(oc + 1) * 128]), inc=sem, inc_k=16)
            else:
                s.do("gpsimd", lambda e, oc=oc, i=i, sf=fsrc[stage], bufs=bufs: e.dma_start(
                    out=bufs[i].ap().rearrange("p (kc m) -> p kc m", m=128),
                    in_=sf[:, :, oc * 128:(oc + 1) * 128]), wait=w, inc=sem, inc_k=16)
            dw = s.cnt[sem]
            dsem = "dS" + bid
            s.do("sync", lambda e, oc=oc, i=i, sc_=scr[stage], bufs=bufs, nb=nb: e.dma_start(
                out=sc_[oc], in_=bufs[i][:, 0:nb]),
                wait=[(sem, dw)], inc=dsem, inc_k=16)
            panel_written[(stage, oc)] = s.cnt[dsem]
            loads_dw[(it, stage, oc)] = (sem, dw)
            scratch_dep[bid] = (dsem, s.cnt[dsem])
        else:
            if (it, "first") not in loads_dw:
                w = w + [(nm, s.cnt.get(nm, 0)) for nm in
                         ("dSA0", "dSA1", "dSH0", "dSH1")]
                loads_dw[(it, "first")] = True
            s.do("sync", lambda e, oc=oc, i=i, sc_=scr[stage], bufs=bufs, nb=nb: e.dma_start(
                out=bufs[i][:, 0:nb], in_=sc_[oc]),
                wait=w, inc=sem, inc_k=16)
            loads_dw[(it, stage, oc)] = (sem, s.cnt[sem])

    tile_done_v = {}
    tile_order = list(io.get("tile_order") or range(NT))
    for it, ti in enumerate(tile_order):
        buf = it % 2
        s.do("sync", lambda e, ti=ti, buf=buf: e.dma_start(
            out=x_t[buf].ap().rearrange("p (fc t) -> p fc t", t=TILE),
            in_=x_fm[:, :, ti * TILE:(ti + 1) * TILE]),
            wait=[("tV", tile_done_v.get(it - 2, 0))], inc=f"dX{buf}", inc_k=16)
        dx = s.cnt[f"dX{buf}"]
        first = [(f"dX{buf}", dx)] + ([("dC", f0_dc)] if it == 0 else [])
        vln = emit_ln(s, C, x_t[buf][:, :], xln, TILE, first_wait=first)

        issue_load(it, 0)
        issue_load(it, 1)
        t_act_free = {}
        for k, (stage, oc) in enumerate(plan):
            i = k % 2
            bid = ("A" if stage == "A" else "H") + str(i)
            pbname = f"f0ps1_{oc % 2}"
            pb = ps1[oc % 2]
            dwk = loads_dw[(it, stage, oc)]
            if stage == "A":
                pb2 = ps2[oc % 2]
                pb2name = f"f0ps2_{oc % 2}"
                for kc in range(8):
                    w = []
                    if kc == 0:
                        w = [dwk, ("tV", vln)]
                        if pbname in s.bank_free:
                            w.extend(s.bank_free[pbname])
                    s.do("tensor", lambda e, kc=kc, pb=pb, i=i: e.matmul(
                        pb[:, :], pA[i][:, kc * 128:(kc + 1) * 128],
                        xln[:, kc * TILE:(kc + 1) * TILE],
                        start=(kc == 0), stop=(kc == 7)), wait=w)
                for kc in range(8):
                    w = []
                    if kc == 0 and pb2name in s.bank_free:
                        w = list(s.bank_free[pb2name])
                    s.do("tensor", lambda e, kc=kc, pb2=pb2, i=i: e.matmul(
                        pb2[:, :], pA[i][:, 1024 + kc * 128:1024 + (kc + 1) * 128],
                        xln[:, kc * TILE:(kc + 1) * TILE],
                        start=(kc == 0), stop=(kc == 7)), wait=w,
                        inc="tP" if kc == 7 else None)
                pe = s.cnt["tP"]
                pan_free[bid] = [("tP", pe)] + ([scratch_dep[bid]] if bid in scratch_dep else [])
                ta = t_act[oc % 2]
                sw = [("tP", pe)]
                if (oc % 2) in t_act_free:
                    sw.append(t_act_free[oc % 2])
                s.do("scalar", lambda e, oc=oc, pb=pb, ta=ta: e.activation(
                    ta[:, :], pb[:, :], AF.Silu, scale=s12_pc[:, oc:oc + 1]),
                    wait=sw, inc="tS")
                s.bank_free[pbname] = [("tS", s.cnt["tS"])]
                stc = s.cnt["tS"]
                s.do("vector", lambda e, oc=oc, pb2=pb2, ta=ta: e.scalar_tensor_tensor(
                    g0[:, oc * TILE:(oc + 1) * TILE], pb2[:, :], s12_pc[:, 32 + oc:32 + oc + 1],
                    ta[:, :], ALU.mult, ALU.mult),
                    wait=[("tS", stc)], inc="tV")
                s.bank_free[pb2name] = [("tV", s.cnt["tV"])]
                t_act_free[oc % 2] = ("tV", s.cnt["tV"])
                if k == 31:
                    g0_ready = s.cnt["tV"]
            elif stage in ("H1", "H2"):
                src_g = g0 if stage == "H1" else g1
                dst_g = g1 if stage == "H1" else g0
                ready = g0_ready if stage == "H1" else g1_ready
                for kc in range(32):
                    w = []
                    if kc == 0:
                        w = [dwk]
                        w.append(("tV", ready) if stage == "H1" else ("tS", ready))
                        if pbname in s.bank_free:
                            w.extend(s.bank_free[pbname])
                    s.do("tensor", lambda e, kc=kc, pb=pb, i=i, src_g=src_g: e.matmul(
                        pb[:, :], pH[i][:, kc * 128:(kc + 1) * 128],
                        src_g[:, kc * TILE:(kc + 1) * TILE],
                        start=(kc == 0), stop=(kc == 31)), wait=w,
                        inc="tP" if kc == 31 else None)
                pe = s.cnt["tP"]
                pan_free[bid] = [("tP", pe)] + ([scratch_dep[bid]] if bid in scratch_dep else [])
                s.do("scalar", lambda e, oc=oc, pb=pb, dst_g=dst_g, off=(0 if stage == "H1" else 32): e.activation(
                    dst_g[:, oc * TILE:(oc + 1) * TILE], pb[:, :], AF.Gelu,
                    scale=sh12_pc[:, off + oc:off + oc + 1]),
                    wait=[("tP", pe)], inc="tS")
                s.bank_free[pbname] = [("tS", s.cnt["tS"])]
                if k == 63:
                    g1_ready = s.cnt["tS"]
                if k == 95:
                    g2_ready = s.cnt["tS"]
            else:
                for kc in range(32):
                    w = []
                    if kc == 0:
                        w = [dwk, ("tS", g2_ready)]
                        if pbname in s.bank_free:
                            w.extend(s.bank_free[pbname])
                    s.do("tensor", lambda e, kc=kc, pb=pb, i=i: e.matmul(
                        pb[:, :], pH[i][:, kc * 128:(kc + 1) * 128],
                        g0[:, kc * TILE:(kc + 1) * TILE],
                        start=(kc == 0), stop=(kc == 31)), wait=w,
                        inc="tP" if kc == 31 else None)
                pe = s.cnt["tP"]
                pan_free[bid] = [("tP", pe)] + ([scratch_dep[bid]] if bid in scratch_dep else [])
                rw = [("tP", pe)]
                if oc == 0 and it >= 2:
                    rw.append((f"sO{buf}", s.cnt.get(f"sO{buf}", 0)))
                s.do("vector", lambda e, oc=oc, pb=pb, buf=buf: e.scalar_tensor_tensor(
                    o_t[buf][:, oc * TILE:(oc + 1) * TILE], pb[:, :], s3_pc[:, oc:oc + 1],
                    x_t[buf][:, oc * TILE:(oc + 1) * TILE], ALU.mult, ALU.add),
                    wait=rw, inc="tV")
                s.bank_free[pbname] = [("tV", s.cnt["tV"])]
            if k + 2 < len(plan):
                issue_load(it, k + 2)
        tile_done_v[it] = s.cnt["tV"]
        s.do("scalar", lambda e, ti=ti, buf=buf: e.dma_start(
            out=out_fm[:, :, ti * TILE:(ti + 1) * TILE],
            in_=o_t[buf].ap().rearrange("p (fc t) -> p fc t", t=TILE)),
            wait=[("tV", tile_done_v[it])], inc=f"sO{buf}", inc_k=16)
        if it == 0 and "halo_cb" in io:
            io["halo_cb"]([(f"sO{buf}", s.cnt[f"sO{buf}"])])

    s.wait("scalar", "sO0", s.cnt["sO0"])
    s.wait("scalar", "sO1", s.cnt["sO1"])
    stack.close()


def build_f0():
    nc = bass.Bass("TRN2", target_bir_lowering=False)
    io = {}
    for nm, shape, dt, kind in (
            ("x", [D, TOK], F32, "ExternalInput"), ("w1T", [D, 4096], F32, "ExternalInput"),
            ("w2T", [D, 4096], F32, "ExternalInput"), ("h1T", [4096, 4096], F32, "ExternalInput"),
            ("h2T", [4096, 4096], F32, "ExternalInput"), ("w3T", [4096, D], F32, "ExternalInput"),
            ("s12", [128, 64], F32, "ExternalInput"), ("sh12", [128, 64], F32, "ExternalInput"),
            ("s3", [128, 8], F32, "ExternalInput"), ("epsc", [1, 1], F32, "ExternalInput"),
            ("out", [D, TOK], F32, "ExternalOutput")):
        io[nm] = nc.dram_tensor(nm, shape, dt, kind=kind).ap()
    s = Sched(nc)
    C = common_tiles(s, nc)
    emit_f0(s, nc, io, C)
    s.run()
    s.stack.close()
    return nc


def f0_host(x1_fm_percore, w1_w, w1_s, w2_w, w2_s, h1_w, h1_s, h2_w, h2_s, w3_w, w3_s):
    nc = build_f0()
    epsc = np.full((1, 1), 1e-5, np.float32)
    s12_ = np.concatenate([pc(w1_s), pc(w2_s)], axis=1)
    sh12_ = np.concatenate([pc(h1_s), pc(h2_s)], axis=1)
    in_maps = [{
        "x": x1_fm_percore[c],
        "w1T": wT(w1_w), "w2T": wT(w2_w), "h1T": wT(h1_w), "h2T": wT(h2_w),
        "w3T": wT(w3_w),
        "s12": s12_, "sh12": sh12_, "s3": pc(w3_s), "epsc": epsc,
    } for c in range(N_CORES)]
    res = run_bass_kernel_spmd(nc, in_maps, core_ids=list(range(N_CORES)))
    return [res.results[c]["out"] for c in range(N_CORES)]


def emit_a1(s, nc, io, C, pre="a1_"):
    x_in = io["x"]; WqT = io["WqT"]; WkT = io["WkT"]; WvT = io["WvT"]
    bq = io["bq"]; bk = io["bk"]; bv_bc = io["bv_bc"]
    triu = io["triu"]; ident = io["ident"]
    qp_out = io["qp"]; num_out = io["num"]; den_out = io["den"]
    kv_out = io["kv"]; z_out = io["z"]
    stack = ExitStack()
    sb = lambda name, shape, dt=F32: stack.enter_context(nc.sbuf_tensor(pre + name, shape, dt))
    ps = lambda name, shape, dt=F32: stack.enter_context(nc.psum_tensor(pre + name, shape, dt))
    C = dict(C)
    C["ps_a"] = ps("lnps_a", [128, TILE])
    C["ps_b"] = ps("lnps_b", [128, TILE])
    w_bf = {k: sb(f"w_{k}", [128, 8 * D], BF16) for k in ("q", "k", "v")}
    bq_pc = sb("bq_pc", [128, 8])
    bk_pc = sb("bk_pc", [128, 8])
    bv_row = sb("bv_row", [128, 1024])
    triu_sb = sb("triu_sb", [128, 128])
    id_bf = sb("id_bf", [128, 128], BF16)
    x_t = [sb("x0", [128, 8 * TILE])]
    xln = sb("xln", [128, 8 * TILE], BF16)
    qp_sb = sb("qp_sb", [128, 8 * TILE], BF16)
    kp_sb = sb("kp_sb", [128, 8 * TILE], BF16)
    kp_tm = sb("kp_tm", [128, 4 * 1024], BF16)
    v_tm = sb("v_tm", [128, 4 * 1024], BF16)
    asb = [sb(f"asb{i}", [128, TILE], BF16) for i in range(4)]
    s_f32 = sb("s_f32", [128, 8 * 1024])
    s_bf = sb("s_bf", [128, 8 * 1024], BF16)
    z_f32 = sb("z_f32", [128, 8])
    z_bf = sb("z_bf", [128, 8], BF16)
    zt8 = sb("zt8", [128, 8])
    num_sb = [sb(f"num_sb{i}", [128, 1024], BF16) for i in range(2)]
    den_sb = sb("den_sb", [128, 32])
    t_r = [sb(f"t_r{i}", [128, TILE]) for i in range(2)]
    t_z = [sb(f"t_z{i}", [128, TILE]) for i in range(2)]
    t_e = [sb(f"t_e{i}", [128, TILE]) for i in range(2)]

    p = [ps(f"a1p{i}", [128, TILE]) for i in range(4)]
    pT = [ps(f"a1pT{i}", [128, 1024], BF16) for i in range(2)]

    for k, src in (("q", WqT), ("k", WkT), ("v", WvT)):
        s.do("gpsimd", lambda e, k=k, src=src: e.dma_start(
            out=w_bf[k].ap().rearrange("p (kc m) -> p kc m", m=D),
            in_=src.rearrange("(kc kp) m -> kp kc m", kp=128)), inc="dW", inc_k=16)
    s.do("gpsimd", lambda e: e.dma_start(out=id_bf[:, :], in_=ident), inc="dW", inc_k=16)
    a1_dw = s.cnt["dW"]
    for dst, src in ((bq_pc[:, :], bq), (bk_pc[:, :], bk), (bv_row[:, :], bv_bc),
                     (triu_sb[:, :], triu)):
        s.do("sync", lambda e, dst=dst, src=src: e.dma_start(out=dst, in_=src), inc="dC", inc_k=16)
    a1_dc = s.cnt["dC"]
    s.do("vector", lambda e: e.memset(s_f32[:, :], 0.0))
    s.do("vector", lambda e: e.memset(s_bf[:, :], 0.0))
    s.do("vector", lambda e: e.memset(z_f32[:, :], 0.0))
    s.do("vector", lambda e: e.memset(z_bf[:, :], 0.0))
    s.do("vector", lambda e: e.memset(den_sb[:, :], 0.0), inc="tV")

    x_fm = x_in.rearrange("(fc fp) t -> fp fc t", fp=128)
    qp_fm = qp_out.rearrange("(fc fp) t -> fp fc t", fp=128)

    def phi_proj(which, dst, b_pc, free_guard):
        for oc in range(8):
            pb = p[oc % 2]
            pbn = f"a1p{oc % 2}"
            for kc in range(8):
                w = []
                if kc == 0:
                    w = list(free_guard)
                    if pbn in s.bank_free:
                        w.extend(s.bank_free[pbn])
                s.do("tensor", lambda e, oc=oc, kc=kc, pb=pb, which=which: e.matmul(
                    pb[:, :], w_bf[which][:, kc * D + oc * 128: kc * D + oc * 128 + 128],
                    xln[:, kc * TILE:(kc + 1) * TILE],
                    start=(kc == 0), stop=(kc == 7)), wait=w,
                    inc="tP" if kc == 7 else None)
            pe = s.cnt["tP"]
            i = oc % 2
            if A1_PHI == "copy":
                s.do("vector", lambda e, pb=pb, oc=oc, dst=dst: e.tensor_copy(
                    dst[:, oc * TILE:(oc + 1) * TILE], pb[:, :]),
                    wait=[("tP", pe)], inc="tV")
                s.bank_free[pbn] = [("tV", s.cnt["tV"])]
                continue
            sw = [("tP", pe)]
            if ("t_r", i) in s.bank_free:
                sw.extend(s.bank_free[("t_r", i)])
            s.do("scalar", lambda e, pb=pb, oc=oc, i=i, b_pc=b_pc: e.activation(
                t_r[i][:, :], pb[:, :], AF.Relu, bias=b_pc[:, oc:oc + 1]),
                wait=sw, inc="tS")
            relu_c = s.cnt["tS"]
            vw = [("tP", pe), ("tS", relu_c)]
            if ("t_z", i) in s.bank_free:
                vw.extend(s.bank_free[("t_z", i)])
            s.do("vector", lambda e, pb=pb, oc=oc, i=i, b_pc=b_pc: e.tensor_scalar(
                t_z[i][:, :], pb[:, :], b_pc[:, oc:oc + 1], 0.0, ALU.add, ALU.min),
                wait=vw, inc="tV")
            s.bank_free[pbn] = [("tS", relu_c), ("tV", s.cnt["tV"])]
            ew = [("tV", s.cnt["tV"])]
            if ("t_e", i) in s.bank_free:
                ew.extend(s.bank_free[("t_e", i)])
            s.do("scalar", lambda e, i=i: e.activation(t_e[i][:, :], t_z[i][:, :], AF.Exp),
                 wait=ew, inc="tS")
            s.bank_free[("t_z", i)] = [("tS", s.cnt["tS"])]
            exp_c = s.cnt["tS"]
            s.do("vector", lambda e, i=i, oc=oc, dst=dst: e.tensor_tensor(
                dst[:, oc * TILE:(oc + 1) * TILE], t_e[i][:, :], t_r[i][:, :], ALU.add),
                wait=[("tS", exp_c)], inc="tV")
            s.bank_free[("t_r", i)] = [("tV", s.cnt["tV"])]
            s.bank_free[("t_e", i)] = [("tV", s.cnt["tV"])]
        return s.cnt["tV"]

    tile_done_v = {}
    for sc_i in range(NT):
        buf = 0
        s.do("sync", lambda e, sc_i=sc_i, buf=buf: e.dma_start(
            out=x_t[buf].ap().rearrange("p (fc t) -> p fc t", t=TILE),
            in_=x_fm[:, :, sc_i * TILE:(sc_i + 1) * TILE]),
            wait=[("tV", tile_done_v.get(sc_i - 1, 0))], inc="dX0", inc_k=16)
        dx = s.cnt["dX0"]
        first = [("dX0", dx)] + ([("dW", a1_dw), ("dC", a1_dc)] if sc_i == 0 else [])
        vln = emit_ln(s, C, x_t[buf][:, :], xln, TILE, first_wait=first)
        tile_done_v[sc_i] = vln

        if A1_LEVEL >= 1:
            qg = [("tV", vln)] + ([("sQ", 16 * sc_i)] if sc_i > 0 else [])
            vq = phi_proj("q", qp_sb, bq_pc, qg)
            vk = phi_proj("k", kp_sb, bk_pc, [("tV", vq)])
        else:
            vq = vk = s.cnt["tV"]

        if A1_LEVEL >= 1:
            s.do("scalar", lambda e, sc_i=sc_i: e.dma_start(
                out=qp_fm[:, :, sc_i * TILE:(sc_i + 1) * TILE],
                in_=qp_sb.ap().rearrange("p (fc t) -> p fc t", t=TILE)),
                wait=[("tV", vq)], inc="sQ", inc_k=16)

        for c in range(4 if A1_LEVEL >= 2 else 0):
            for half in range(2 if A1_VTM else 0):
                pb = p[2 + half]
                pbn = f"a1p{2 + half}"
                for kc in range(8):
                    w = []
                    if kc == 0:
                        w = [("tV", vk)]
                        if pbn in s.bank_free:
                            w.extend(s.bank_free[pbn])
                    s.do("tensor", lambda e, c=c, kc=kc, pb=pb, half=half: e.matmul(
                        pb[:, :], xln[:, kc * TILE + c * 128: kc * TILE + c * 128 + 128],
                        w_bf["v"][:, kc * D + half * TILE: kc * D + (half + 1) * TILE],
                        start=(kc == 0), stop=(kc == 7)), wait=w,
                        inc="tP" if kc == 7 else None)
                pe = s.cnt["tP"]
                s.do("vector", lambda e, c=c, half=half, pb=pb: e.tensor_tensor(
                    v_tm[:, c * 1024 + half * TILE: c * 1024 + (half + 1) * TILE],
                    pb[:, :], bv_row[:, half * TILE:(half + 1) * TILE], ALU.add),
                    wait=[("tP", pe)], inc="tV")
                s.bank_free[pbn] = [("tV", s.cnt["tV"])]
            if A1_TP:
                emit_tp8(
                    s, pT, "a1pT", id_bf,
                    [kp_sb[:, fc * TILE + c * 128: fc * TILE + c * 128 + 128] for fc in range(8)],
                    [kp_tm[:, c * 1024 + fc * 128: c * 1024 + (fc + 1) * 128] for fc in range(8)],
                    [("tV", vk)])
        v_ready = s.cnt["tV"]

        for sch in range(4 if A1_LEVEL >= 3 else 0):
            n = TILE - sch * 128
            pb = p[sch % 2]
            pbn = f"a1p{sch % 2}"
            for fc in range(8):
                w = []
                if fc == 0:
                    w = [("tV", v_ready)]
                    if pbn in s.bank_free:
                        w.extend(s.bank_free[pbn])
                s.do("tensor", lambda e, sch=sch, fc=fc, pb=pb, n=n: e.matmul(
                    pb[:, 0:n], kp_sb[:, fc * TILE + sch * 128: fc * TILE + (sch + 1) * 128],
                    qp_sb[:, fc * TILE + sch * 128: fc * TILE + TILE],
                    start=(fc == 0), stop=(fc == 7)), wait=w,
                    inc="tP" if fc == 7 else None)
            pe = s.cnt["tP"]
            s.do("vector", lambda e, sch=sch, pb=pb: e.tensor_tensor(
                asb[sch][:, sch * 128:(sch + 1) * 128], pb[:, 0:128], triu_sb[:, :], ALU.mult),
                wait=[("tP", pe)], inc="tV")
            if sch < 3:
                s.do("vector", lambda e, sch=sch, pb=pb, n=n: e.tensor_copy(
                    asb[sch][:, (sch + 1) * 128: TILE], pb[:, 128:n]), inc="tV")
            s.bank_free[pbn] = [("tV", s.cnt["tV"])]
        a_ready = s.cnt["tV"]

        for qc in range(4 if A1_LEVEL >= 4 else 0):
            g = sc_i * 4 + qc
            k = 0
            for sch in range(qc + 1):
                w = []
                if k == 0:
                    w = [("tV", a_ready)]
                    if "a1p0" in s.bank_free:
                        w.extend(s.bank_free["a1p0"])
                s.do("tensor", lambda e, sch=sch, qc=qc, k=k: e.matmul(
                    p[0][:, 0:1], asb[sch][:, qc * 128:(qc + 1) * 128], C["ones_bf"][:, 0:1],
                    start=(k == 0), stop=False), wait=w)
                k += 1
            for fc in range(8):
                s.do("tensor", lambda e, fc=fc, qc=qc: e.matmul(
                    p[0][:, 0:1], qp_sb[:, fc * TILE + qc * 128: fc * TILE + (qc + 1) * 128],
                    z_bf[:, fc:fc + 1], start=False, stop=(fc == 7)),
                    inc="tP" if fc == 7 else None)
            pe_den = s.cnt["tP"]
            s.do("vector", lambda e, g=g: e.tensor_copy(den_sb[:, g:g + 1], p[0][:, 0:1]),
                 wait=[("tP", pe_den)], inc="tV")
            s.bank_free["a1p0"] = [("tV", s.cnt["tV"])]
            nbuf = num_sb[qc % 2]
            for half in range(2):
                pb = p[2 + half]
                pbn = f"a1p{2 + half}"
                k = 0
                tot = (qc + 1) + 8
                for sch in range(qc + 1):
                    w = []
                    if k == 0:
                        w = []
                        if pbn in s.bank_free:
                            w.extend(s.bank_free[pbn])
                    s.do("tensor", lambda e, sch=sch, qc=qc, pb=pb, half=half, k=k: e.matmul(
                        pb[:, :], asb[sch][:, qc * 128:(qc + 1) * 128],
                        v_tm[:, sch * 1024 + half * TILE: sch * 1024 + (half + 1) * TILE],
                        start=(k == 0), stop=False), wait=w)
                    k += 1
                for fc in range(8):
                    s.do("tensor", lambda e, fc=fc, qc=qc, pb=pb, half=half: e.matmul(
                        pb[:, :], qp_sb[:, fc * TILE + qc * 128: fc * TILE + (qc + 1) * 128],
                        s_bf[:, fc * 1024 + half * TILE: fc * 1024 + (half + 1) * TILE],
                        start=False, stop=(fc == 7)),
                        inc="tP" if fc == 7 else None)
                pe_num = s.cnt["tP"]
                w = [("tP", pe_num)]
                if ("num", qc % 2) in s.bank_free:
                    w.extend(s.bank_free[("num", qc % 2)])
                s.do("vector", lambda e, pb=pb, half=half, nbuf=nbuf: e.tensor_copy(
                    nbuf[:, half * TILE:(half + 1) * TILE], pb[:, :]), wait=w, inc="tV")
                s.bank_free[pbn] = [("tV", s.cnt["tV"])]
            nv = s.cnt["tV"]
            nsem = f"sN{qc % 2}"
            s.do("scalar", lambda e, g=g, nbuf=nbuf: e.dma_start(
                out=num_out[g * 128:(g + 1) * 128, :], in_=nbuf[:, :]),
                wait=[("tV", nv)], inc=nsem, inc_k=16)
            s.bank_free[("num", qc % 2)] = [(nsem, s.cnt[nsem])]

        for dc in range(8 if A1_LEVEL >= 5 else 0):
            for half in range(2):
                pb = p[half]
                pbn = f"a1p{half}"
                for c in range(4):
                    w = []
                    if c == 0:
                        w = []
                        if pbn in s.bank_free:
                            w.extend(s.bank_free[pbn])
                    s.do("tensor", lambda e, dc=dc, c=c, pb=pb, half=half: e.matmul(
                        pb[:, :], kp_tm[:, c * 1024 + dc * 128:c * 1024 + (dc + 1) * 128],
                        v_tm[:, c * 1024 + half * TILE: c * 1024 + (half + 1) * TILE],
                        start=(c == 0), stop=(c == 3)), wait=w,
                        inc="tP" if c == 3 else None)
                pe = s.cnt["tP"]
                s.do("vector", lambda e, dc=dc, half=half, pb=pb: e.tensor_tensor(
                    s_f32[:, dc * 1024 + half * TILE: dc * 1024 + (half + 1) * TILE],
                    s_f32[:, dc * 1024 + half * TILE: dc * 1024 + (half + 1) * TILE],
                    pb[:, :], ALU.add), wait=[("tP", pe)], inc="tV")
                s.do("vector", lambda e, dc=dc, half=half: e.tensor_copy(
                    s_bf[:, dc * 1024 + half * TILE: dc * 1024 + (half + 1) * TILE],
                    s_f32[:, dc * 1024 + half * TILE: dc * 1024 + (half + 1) * TILE]),
                    wait=[("tV", s.cnt["tV"])], inc="tV")
                s.bank_free[pbn] = [("tV", s.cnt["tV"])]

        zw = [("tP", s.cnt["tP"])]
        if A1_LEVEL >= 5:
            for fc in range(8):
                s.do("vector", lambda e, fc=fc: e.reduce_sum(
                    zt8[:, fc:fc + 1], kp_sb[:, fc * TILE:(fc + 1) * TILE],
                    mybir.AxisListType.X),
                    wait=(zw if fc == 0 else []), inc="tV")
            s.do("vector", lambda e: e.tensor_tensor(
                z_f32[:, 0:8], z_f32[:, 0:8], zt8[:, 0:8], ALU.add),
                wait=[("tV", s.cnt["tV"])], inc="tV")
            s.do("vector", lambda e: e.tensor_copy(z_bf[:, 0:8], z_f32[:, 0:8]),
                 wait=[("tV", s.cnt["tV"])], inc="tV")

    vfin = s.cnt["tV"]
    s.do("scalar", lambda e: e.dma_start(out=den_out, in_=den_sb[:, :]),
         wait=[("tV", vfin)], inc="sO", inc_k=16)
    if io.get("kv_defer"):
        s.kv_info = (s_f32.ap().rearrange("p (dc m) -> p dc m", m=1024), vfin)
    else:
        s.do("scalar", lambda e: e.dma_start(
            out=kv_out.rearrange("(dc dp) m -> dp dc m", dp=128),
            in_=s_f32.ap().rearrange("p (dc m) -> p dc m", m=1024)),
            wait=[("tV", vfin)], inc="sO", inc_k=16)
    s.do("scalar", lambda e: e.dma_start(out=z_out, in_=z_f32[:, :]), inc="sO", inc_k=16)
    s.wait("scalar", "sO", s.cnt["sO"])
    s.wait("scalar", "sQ", s.cnt.get("sQ", 0))
    s.wait("scalar", "sN0", s.cnt.get("sN0", 0))
    s.wait("scalar", "sN1", s.cnt.get("sN1", 0))
    stack.close()


def build_a1():
    nc = bass.Bass("TRN2", target_bir_lowering=False)
    io = {}
    for nm, shape, dt, kind in (
            ("x", [D, TOK], F32, "ExternalInput"), ("WqT", [D, D], F32, "ExternalInput"),
            ("WkT", [D, D], F32, "ExternalInput"), ("WvT", [D, D], F32, "ExternalInput"),
            ("bq", [128, 8], F32, "ExternalInput"), ("bk", [128, 8], F32, "ExternalInput"),
            ("bv_bc", [128, 1024], F32, "ExternalInput"),
            ("triu", [128, 128], F32, "ExternalInput"), ("ident", [128, 128], F32, "ExternalInput"),
            ("epsc", [1, 1], F32, "ExternalInput"),
            ("qp", [D, TOK], BF16, "ExternalOutput"), ("num", [TOK, D], BF16, "ExternalOutput"),
            ("den", [128, 32], F32, "ExternalOutput"), ("kv", [D, D], F32, "ExternalOutput"),
            ("z", [128, 8], F32, "ExternalOutput")):
        io[nm] = nc.dram_tensor(nm, shape, dt, kind=kind).ap()
    s = Sched(nc)
    C = common_tiles(s, nc)
    emit_a1(s, nc, io, C)
    s.run()
    s.stack.close()
    return nc


def emit_a2(s, nc, io, C, pre="a2_"):
    x_in = io["x"]; qp_in = io["qp"]; num_in = io["num"]; den_in = io["den"]
    kvp_in = io["kvp"]; zp_in = io["zp"]; WoT = io["WoT"]; bo = io["bo"]
    ident = io["ident"]; out = io["out"]
    stack = ExitStack()
    sb = lambda name, shape, dt=F32: stack.enter_context(nc.sbuf_tensor(pre + name, shape, dt))
    ps = lambda name, shape, dt=F32: stack.enter_context(nc.psum_tensor(pre + name, shape, dt))
    wo_bf = sb("wo_bf", [128, 8 * D], BF16)
    kvp_bf = sb("kvp_bf", [128, 8 * D], BF16)
    zp_bf = sb("zp_bf", [128, 8], BF16)
    bo_pc = sb("bo_pc", [128, 8])
    id_bf = sb("id_bf", [128, 128], BF16)
    x_t = [sb(f"x{i}", [128, 8 * TILE]) for i in range(2)]
    qp_t = [sb(f"qp{i}", [128, 8 * TILE], BF16) for i in range(2)]
    num_t = [sb(f"num{i}", [128, 1024], BF16) for i in range(2)]
    den_t = sb("den_t", [128, 32])
    dfull = sb("dfull", [128, 1])
    rden = sb("rden", [128, 1])
    t0 = sb("t0", [128, 1024])
    o_tm = sb("o_tm", [128, 1024], BF16)
    o_fm = sb("o_fm", [128, 8 * TILE], BF16)
    o_t = [sb(f"o{i}", [128, 8 * TILE]) for i in range(2)]

    p = [ps(f"a2p{i}", [128, TILE]) for i in range(6)]
    pT = [ps(f"a2pT{i}", [128, 1024], BF16) for i in range(2)]

    s.do("gpsimd", lambda e: e.dma_start(
        out=wo_bf.ap().rearrange("p (kc m) -> p kc m", m=D),
        in_=WoT.rearrange("(kc kp) m -> kp kc m", kp=128)), inc="dW", inc_k=16)
    s.do("gpsimd", lambda e: e.dma_start(
        out=kvp_bf.ap().rearrange("p (kc m) -> p kc m", m=D),
        in_=kvp_in.rearrange("(kc kp) m -> kp kc m", kp=128)), inc="dW", inc_k=16)
    s.do("gpsimd", lambda e: e.dma_start(out=zp_bf[:, :], in_=zp_in), inc="dW", inc_k=16)
    s.do("gpsimd", lambda e: e.dma_start(out=id_bf[:, :], in_=ident), inc="dW", inc_k=16)
    a2_dw = s.cnt["dW"]
    for dst, src in ((bo_pc[:, :], bo), (den_t[:, :], den_in)):
        s.do("sync", lambda e, dst=dst, src=src: e.dma_start(out=dst, in_=src), inc="dC", inc_k=16)
    a2_dc = s.cnt["dC"]
    if "mask" in io:
        mask_pc = sb("mask_pc", [128, 1])
        s.do("sync", lambda e: e.dma_start(out=mask_pc[:, 0:1], in_=io["mask"]), inc="dC", inc_k=16)
        a2_dc = s.cnt["dC"]
        s.do("vector", lambda e: e.tensor_scalar_mul(kvp_bf[:, :], kvp_bf[:, :], mask_pc[:, 0:1]),
             wait=[("dW", a2_dw), ("dC", a2_dc)], inc="tV")
        s.do("vector", lambda e: e.tensor_scalar_mul(zp_bf[:, :], zp_bf[:, :], mask_pc[:, 0:1]),
             wait=[("tV", s.cnt["tV"])], inc="tV")
        a2_mask_v = s.cnt["tV"]
    else:
        a2_mask_v = 0

    x_fm = x_in.rearrange("(fc fp) t -> fp fc t", fp=128)
    qp_fm = qp_in.rearrange("(fc fp) t -> fp fc t", fp=128)
    out_fm = out.rearrange("(fc fp) t -> fp fc t", fp=128)

    tile_done_v = {}
    for sc_i in range(NT):
        buf = sc_i % 2
        prev = [("tV", tile_done_v.get(sc_i - 2, 0))]
        s.do("sync", lambda e, sc_i=sc_i, buf=buf: e.dma_start(
            out=x_t[buf].ap().rearrange("p (fc t) -> p fc t", t=TILE),
            in_=x_fm[:, :, sc_i * TILE:(sc_i + 1) * TILE]), wait=prev, inc=f"dX{buf}", inc_k=16)
        s.do("sync", lambda e, sc_i=sc_i, buf=buf: e.dma_start(
            out=qp_t[buf].ap().rearrange("p (fc t) -> p fc t", t=TILE),
            in_=qp_fm[:, :, sc_i * TILE:(sc_i + 1) * TILE]), inc=f"dQ{buf}", inc_k=16)
        dxx = s.cnt[f"dX{buf}"]
        dqq = s.cnt[f"dQ{buf}"]
        for c in range(4):
            g = sc_i * 4 + c
            nb = num_t[c % 2]
            nw = [s.bank_free.get(("numt", c % 2), ("tV", 0))]
            s.do("sync", lambda e, g=g, nb=nb: e.dma_start(
                out=nb[:, :], in_=num_in[g * 128:(g + 1) * 128, :]),
                wait=[w for w in nw if w[1] > 0], inc=f"dN{c % 2}", inc_k=16)
            dxc = (f"dN{c % 2}", s.cnt[f"dN{c % 2}"])
            pp = (p[0], p[1]) if c % 2 == 0 else (p[2], p[3])
            ppn = ("a2p0", "a2p1") if c % 2 == 0 else ("a2p2", "a2p3")
            for half in range(2):
                pb = pp[half]
                for fc in range(8):
                    w = []
                    if fc == 0:
                        w = [(f"dQ{buf}", dqq), ("dW", a2_dw), ("tV", a2_mask_v)]
                        if ppn[half] in s.bank_free:
                            w.extend(s.bank_free[ppn[half]])
                    s.do("tensor", lambda e, c=c, fc=fc, pb=pb, half=half, buf=buf: e.matmul(
                        pb[:, :], qp_t[buf][:, fc * TILE + c * 128: fc * TILE + (c + 1) * 128],
                        kvp_bf[:, fc * D + half * TILE: fc * D + (half + 1) * TILE],
                        start=(fc == 0), stop=(fc == 7)), wait=w,
                        inc="tP" if fc == 7 else None)
            for fc in range(8):
                w = []
                if fc == 0 and "a2p4" in s.bank_free:
                    w = list(s.bank_free["a2p4"])
                s.do("tensor", lambda e, c=c, fc=fc, buf=buf: e.matmul(
                    p[4][:, 0:1], qp_t[buf][:, fc * TILE + c * 128: fc * TILE + (c + 1) * 128],
                    zp_bf[:, fc:fc + 1], start=(fc == 0), stop=(fc == 7)),
                    wait=w, inc="tP" if fc == 7 else None)
            pe = s.cnt["tP"]
            s.do("vector", lambda e, g=g: e.scalar_tensor_tensor(
                dfull[:, 0:1], p[4][:, 0:1], 1e-6, den_t[:, g:g + 1], ALU.add, ALU.add),
                wait=[("tP", pe), ("dC", a2_dc)], inc="tV")
            s.bank_free["a2p4"] = [("tV", s.cnt["tV"])]
            s.do("vector", lambda e: e.reciprocal(rden[:, 0:1], dfull[:, 0:1]),
                 wait=[("tV", s.cnt["tV"])], inc="tV")
            t0_w = {}
            for half in range(2):
                s.do("vector", lambda e, half=half, nb=nb, pp=pp: e.tensor_tensor(
                    t0[:, half * TILE:(half + 1) * TILE], pp[half][:, :],
                    nb[:, half * TILE:(half + 1) * TILE], ALU.add),
                    wait=[dxc] if half == 0 else (), inc="tV")
                t0_w[half] = s.cnt["tV"]
            for half in range(2):
                s.do("vector", lambda e, half=half: e.tensor_scalar_mul(
                    o_tm[:, half * TILE:(half + 1) * TILE],
                    t0[:, half * TILE:(half + 1) * TILE], rden[:, 0:1]),
                    wait=[("tV", t0_w[half])], inc="tV")
                s.bank_free[ppn[half]] = [("tV", s.cnt["tV"])]
            s.bank_free[("numt", c % 2)] = ("tV", s.cnt["tV"])
            otm_v = s.cnt["tV"]
            emit_tp8(
                s, pT, "a2pT", id_bf,
                [o_tm[:, fc * 128:(fc + 1) * 128] for fc in range(8)],
                [o_fm[:, fc * TILE + c * 128: fc * TILE + (c + 1) * 128] for fc in range(8)],
                [("tV", otm_v)])
        ofm_v = s.cnt["tV"]
        for oc in range(8):
            pb = p[4 + (oc % 2)]
            pbn = f"a2p{4 + (oc % 2)}"
            for kc in range(8):
                w = []
                if kc == 0:
                    w = [("tV", ofm_v)]
                    if pbn in s.bank_free:
                        w.extend(s.bank_free[pbn])
                s.do("tensor", lambda e, oc=oc, kc=kc, pb=pb: e.matmul(
                    pb[:, :], wo_bf[:, kc * D + oc * 128: kc * D + oc * 128 + 128],
                    o_fm[:, kc * TILE:(kc + 1) * TILE],
                    start=(kc == 0), stop=(kc == 7)), wait=w,
                    inc="tP" if kc == 7 else None)
            pe = s.cnt["tP"]
            rw = [("tP", pe), (f"dX{buf}", dxx)]
            if oc == 0 and sc_i >= 2:
                rw.append((f"sO{buf}", s.cnt.get(f"sO{buf}", 0)))
            s.do("vector", lambda e, oc=oc, pb=pb, buf=buf: e.scalar_tensor_tensor(
                o_t[buf][:, oc * TILE:(oc + 1) * TILE], pb[:, :], bo_pc[:, oc:oc + 1],
                x_t[buf][:, oc * TILE:(oc + 1) * TILE], ALU.add, ALU.add),
                wait=rw, inc="tV")
            s.bank_free[pbn] = [("tV", s.cnt["tV"])]
        tile_done_v[sc_i] = s.cnt["tV"]
        s.do("scalar", lambda e, sc_i=sc_i, buf=buf: e.dma_start(
            out=out_fm[:, :, sc_i * TILE:(sc_i + 1) * TILE],
            in_=o_t[buf].ap().rearrange("p (fc t) -> p fc t", t=TILE)),
            wait=[("tV", tile_done_v[sc_i])], inc=f"sO{buf}", inc_k=16)

    s.wait("scalar", "sO0", s.cnt["sO0"])
    s.wait("scalar", "sO1", s.cnt["sO1"])
    stack.close()


def build_a2():
    nc = bass.Bass("TRN2", target_bir_lowering=False)
    io = {}
    for nm, shape, dt, kind in (
            ("x", [D, TOK], F32, "ExternalInput"), ("qp", [D, TOK], BF16, "ExternalInput"),
            ("num", [TOK, D], BF16, "ExternalInput"), ("den", [128, 32], F32, "ExternalInput"),
            ("kvp", [D, D], F32, "ExternalInput"), ("zp", [128, 8], F32, "ExternalInput"),
            ("WoT", [D, D], F32, "ExternalInput"), ("bo", [128, 8], F32, "ExternalInput"),
            ("ident", [128, 128], F32, "ExternalInput"),
            ("out", [D, TOK], F32, "ExternalOutput")):
        io[nm] = nc.dram_tensor(nm, shape, dt, kind=kind).ap()
    s = Sched(nc)
    emit_a2(s, nc, io, None)
    s.run()
    s.stack.close()
    return nc


def attn_host(x_fm_percore, Wq, bq_, Wk, bk_, Wv, bv_, Wo, bo_):
    nc1 = build_a1()
    epsc = np.full((1, 1), 1e-5, np.float32)
    triu_ = np.triu(np.ones((128, 128), np.float32))
    ident_ = np.eye(128, dtype=np.float32)
    bv_b = np.ascontiguousarray(np.broadcast_to(bv_.reshape(8, 128).T.reshape(128, 8)[:, :], (128, 8)))
    bv_row_ = np.ascontiguousarray(np.broadcast_to(np.asarray(bv_, np.float32)[None, :], (128, 1024)))
    in1 = [{
        "x": x_fm_percore[c],
        "WqT": wT(Wq), "WkT": wT(Wk), "WvT": wT(Wv),
        "bq": pc(bq_), "bk": pc(bk_), "bv_bc": bv_row_,
        "triu": triu_, "ident": ident_, "epsc": epsc,
    } for c in range(N_CORES)]
    r1 = run_bass_kernel_spmd(nc1, in1, core_ids=list(range(N_CORES)))
    r1 = r1.results

    nc2 = build_a2()
    zeros_kv = np.zeros((D, D), np.float32)
    zeros_z = np.zeros((128, 8), np.float32)
    in2 = []
    for c in range(N_CORES):
        odd = c % 2 == 1
        in2.append({
            "x": x_fm_percore[c],
            "qp": r1[c]["qp"], "num": r1[c]["num"], "den": r1[c]["den"],
            "kvp": r1[c - 1]["kv"] if odd else zeros_kv,
            "zp": r1[c - 1]["z"] if odd else zeros_z,
            "WoT": wT(Wo), "bo": pc(bo_), "ident": ident_,
        })
    r2 = run_bass_kernel_spmd(nc2, in2, core_ids=list(range(N_CORES)))
    return [r2.results[c]["out"] for c in range(N_CORES)]


def emit_tpN(s, pTs, pT_name, id_bf, pairs, first_wait):
    n = len(pairs)
    tpc = {}

    def emit_t(fc):
        slot = fc % 2
        w = list(first_wait) if fc == 0 else []
        if (pT_name, slot) in s.bank_free:
            w.extend(s.bank_free[(pT_name, slot)])
        s.do("tensor", lambda e, fc=fc, slot=slot: e.transpose(
            pTs[slot][:, 0:128], pairs[fc][0], id_bf[:, :]),
            wait=w, inc="tP")
        tpc[fc] = s.cnt["tP"]

    def emit_c(fc):
        slot = fc % 2
        s.do("vector", lambda e, fc=fc, slot=slot: e.tensor_copy(
            pairs[fc][1], pTs[slot][:, 0:128]),
            wait=[("tP", tpc[fc])], inc="tV")
        s.bank_free[(pT_name, slot)] = [("tV", s.cnt["tV"])]
    emit_t(0)
    if n > 1:
        emit_t(1)
    for fc in range(2, n):
        emit_c(fc - 2)
        emit_t(fc)
    if n > 1:
        emit_c(n - 2)
    emit_c(n - 1)
    return s.cnt["tV"]


NCH = 17
DIN_P = 2208


def emit_m1(s, nc, io, C, pre="m1_"):
    x_in = io["x"]; xh_in = io["xh"]; w1T = io["w1T"]; convT = io.get("convT")
    convb = io["convb"]; prelu = io["prelu"]; alog = io["alog"]
    ident = io["ident"]; xc_out = io["xc"]; at_out = io["at"]; st_out = io["states"]
    pv = io.get("pv")
    scrC = io["scr"]["C"] if io.get("scr") else nc.dram_tensor(
        pre + "scrC", [NCH, 128, 8704], BF16).ap()
    stack = ExitStack()
    sb = lambda name, shape, dt=F32: stack.enter_context(nc.sbuf_tensor(pre + name, shape, dt))
    ps = lambda name, shape, dt=F32: stack.enter_context(nc.psum_tensor(pre + name, shape, dt))
    C = dict(C)
    C["ps_a"] = ps("lnps_a", [128, TILE])
    C["ps_b"] = ps("lnps_b", [128, TILE])
    win1_bf = sb("win1_bf", [128, 8 * DIN_P], BF16)
    pC = [sb(f"pC{i}", [128, 8704], BF16) for i in range(2)]
    convb_pc = sb("convb_pc", [128, NCH])
    a_pre = sb("a_pre", [128, 1])
    a_vec = sb("a_vec", [32, 1])
    one_v = sb("one_v", [128, 1])
    id_bf = sb("id_bf", [128, 128], BF16)
    id_f = sb("id_f", [128, 128])
    x_t = sb("m1x", [128, 8 * TILE])
    xh = sb("m1xh", [128, 8 * 3])
    xln = sb("m1xln", [128, 8 * TILE], BF16)
    xlnh = sb("m1xlnh", [128, 8 * 3], BF16)
    xbc = sb("xbc", [128, NCH * (TILE + 3)], BF16)
    xc_t = sb("xc_t", [128, NCH * TILE], BF16)
    at_t = sb("at_t", [32, TILE])
    t_cr = [sb(f"t_cr{i}", [128, TILE]) for i in range(2)]
    t_cz = [sb(f"t_cz{i}", [128, TILE]) for i in range(2)]
    sp1 = sb("sp1", [32, TILE])
    sp2 = sb("sp2", [32, TILE])
    x_tm = sb("x_tm", [128, 2048], BF16)
    bc_tm = sb("bc_tm", [128, 128], BF16)
    bdelta = sb("bdelta", [128, 2048], BF16)
    acum_c = sb("acum_c", [32, 128])
    acum_tm = sb("acum_tm", [128, 32])
    dstate = sb("dstate", [128, 32])
    asum = sb("asum", [32, 1])
    asum_row = sb("asum_row", [1, 32])
    g_row = sb("g_row", [1, 32])
    gbc_sb = sb("gbc_sb", [128, 32])
    gexp_t = sb("gexp_t", [64, 2048])
    state_sb = [sb(f"state{i}", [64, 2048]) for i in range(2)]
    prelu_sb = sb("prelu_sb", [1, 1])
    alog_sb = sb("alog_sb", [32, 1])

    p = [ps(f"m1p{i}", [128, TILE]) for i in range(2)]
    pT = [ps(f"m1pT{i}", [128, 1024], BF16) for i in range(2)]
    pF = ps("m1pF", [128, TILE])
    pG = ps("m1pG", [128, TILE])

    s.do("gpsimd", lambda e: e.dma_start(
        out=win1_bf.ap().rearrange("p (kc m) -> p kc m", m=DIN_P),
        in_=w1T.rearrange("(kc kp) m -> kp kc m", kp=128)), inc="dW", inc_k=16)
    s.do("gpsimd", lambda e: e.dma_start(out=id_bf[:, :], in_=ident), inc="dW", inc_k=16)
    m1_dwz = s.cnt["dW"]
    for dst, src in ((convb_pc[:, :], convb), (prelu_sb[:, :], prelu),
                     (alog_sb[:, :], alog), (id_f[:, :], ident)):
        s.do("sync", lambda e, dst=dst, src=src: e.dma_start(out=dst, in_=src), inc="dC", inc_k=16)
    m1_dc = s.cnt["dC"]
    s.do("vector", lambda e: e.memset(one_v[:, 0:1], 1.0), wait=[("dC", m1_dc)], inc="tV")
    s.do("scalar", lambda e: e.activation(a_vec[:, 0:1], alog_sb[:, 0:1], AF.Exp),
         wait=[("tV", s.cnt["tV"])], inc="tS")
    s.do("vector", lambda e: e.tensor_scalar_mul(a_vec[:, 0:1], a_vec[:, 0:1], -1.0),
         wait=[("tS", s.cnt["tS"])], inc="tV")
    s.do("tensor", lambda e: e.matmul(pG[:, 0:1], C["ones_f"][0:1, :], prelu_sb[0:1, 0:1],
                                      start=True, stop=True),
         wait=[("tV", s.cnt["tV"])], inc="tP")
    s.do("vector", lambda e: e.tensor_copy(a_pre[:, 0:1], pG[:, 0:1]),
         wait=[("tP", s.cnt["tP"])], inc="tV")
    s.do("vector", lambda e: e.memset(state_sb[0][:, :], 0.0), inc="tV")
    init_v = s.cnt["tV"]

    x_fm = x_in.rearrange("(fc fp) t -> fp fc t", fp=128)
    xh_fm = xh_in.rearrange("(fc fp) t -> fp fc t", fp=128)
    xc_fm = xc_out.rearrange("(cc cp) t -> cp cc t", cp=128)
    convf = convT.rearrange("(ic ip) m -> ip ic m", ip=128) if pv is None else None

    s.do("sync", lambda e: e.dma_start(
        out=xh.ap().rearrange("p (fc t) -> p fc t", t=3), in_=xh_fm[:, :, 0:3]),
        inc="dXh", inc_k=16)
    m1_dxh = s.cnt["dXh"]
    if "hmask" in io:
        hmask_pc = sb("hmask_pc", [128, 1])
        s.do("sync", lambda e: e.dma_start(out=hmask_pc[:, 0:1], in_=io["hmask"]),
             inc="dXh", inc_k=16)
        m1_dxh = s.cnt["dXh"]
        s.do("vector", lambda e: e.tensor_scalar_mul(xh[:, :], xh[:, :], hmask_pc[:, 0:1]),
             wait=[("dXh", m1_dxh), ("tV", init_v)], inc="tV")
    vlnh = emit_ln(s, C, xh[:, :], xlnh, 3,
                   first_wait=[("dXh", m1_dxh), ("dW", m1_dwz), ("tV", s.cnt["tV"]),
                               ("tV", init_v)])
    for oc in range(NCH):
        pb = p[oc % 2]
        pbn = f"m1p{oc % 2}"
        for kc in range(8):
            w = []
            if kc == 0:
                w = [("tV", vlnh)]
                if pbn in s.bank_free:
                    w.extend(s.bank_free[pbn])
            s.do("tensor", lambda e, oc=oc, kc=kc, pb=pb: e.matmul(
                pb[:, 0:3], win1_bf[:, kc * DIN_P + oc * 128: kc * DIN_P + oc * 128 + 128],
                xlnh[:, kc * 3:(kc + 1) * 3], start=(kc == 0), stop=(kc == 7)),
                wait=w, inc="tP" if kc == 7 else None)
        pe = s.cnt["tP"]
        s.do("vector", lambda e, oc=oc, pb=pb: e.tensor_copy(
            xbc[:, oc * (TILE + 3): oc * (TILE + 3) + 3], pb[:, 0:3]),
            wait=[("tP", pe)], inc="tV")
        s.bank_free[pbn] = [("tV", s.cnt["tV"])]

    panel_written = {}
    pan_free = {}
    scratch_dep = {}
    loads_dw = {}

    def issue_cpanel(ti, oc):
        i = oc % 2
        sem = f"dP{i}" if ti > 0 else f"dP{i}c"
        w = list(pan_free.get(i, []))
        if pv is not None:
            if ti == 0:
                w.append(pv[("C", oc)])
            s.do("sync", lambda e, oc=oc, i=i: e.dma_start(
                out=pC[i][:, :], in_=scrC[oc]), wait=w, inc=sem, inc_k=16)
            loads_dw[(ti, oc)] = (sem, s.cnt[sem])
            return
        if ti == 0:
            for k in range(4):
                s.do("gpsimd", lambda e, oc=oc, i=i, k=k: e.dma_start(
                    out=pC[i].ap().rearrange("p (ic k m) -> p ic k m", k=4, m=128)[:, :, k, :],
                    in_=convf[:, :, k * 2176 + oc * 128: k * 2176 + (oc + 1) * 128]),
                    wait=w if k == 0 else (), inc=sem, inc_k=16)
            dw = s.cnt[sem]
            s.do("sync", lambda e, oc=oc, i=i: e.dma_start(
                out=scrC[oc], in_=pC[i][:, :]), wait=[(sem, dw)], inc=f"dS{i}", inc_k=16)
            panel_written[oc] = s.cnt[f"dS{i}"]
            loads_dw[(ti, oc)] = (sem, dw)
            scratch_dep[i] = (f"dS{i}", s.cnt[f"dS{i}"])
        else:
            if (ti, "first") not in loads_dw:
                w = w + [("dS0", s.cnt.get("dS0", 0)), ("dS1", s.cnt.get("dS1", 0))]
                loads_dw[(ti, "first")] = True
            s.do("sync", lambda e, oc=oc, i=i: e.dma_start(
                out=pC[i][:, :], in_=scrC[oc]), wait=w, inc=sem, inc_k=16)
            loads_dw[(ti, oc)] = (sem, s.cnt[sem])

    xlns_fm = io["xlns"].rearrange("(fc fp) t -> fp fc t", fp=128) \
        if "xlns" in io else None
    st_buf = 0
    for ti in range(NT):
        s.do("sync", lambda e, ti=ti: e.dma_start(
            out=x_t.ap().rearrange("p (fc t) -> p fc t", t=TILE),
            in_=x_fm[:, :, ti * TILE:(ti + 1) * TILE]),
            wait=[("tV", s.cnt["tV"])], inc="dX0", inc_k=16)
        dx = s.cnt["dX0"]
        lw = [("dX0", dx)]
        if xlns_fm is not None and ti > 0:
            lw.append(("sXL", s.cnt["sXL"]))
        vln = emit_ln(s, C, x_t[:, :], xln, TILE, first_wait=lw)
        if xlns_fm is not None:
            s.do("scalar", lambda e, ti=ti: e.dma_start(
                out=xlns_fm[:, :, ti * TILE:(ti + 1) * TILE],
                in_=xln.ap().rearrange("p (fc t) -> p fc t", t=TILE)),
                wait=[("tV", vln)], inc="sXL", inc_k=16)

        if ti > 0:
            s.do("vector", lambda e: e.tensor_copy(
                bass.AP(xbc.ap().tensor, xbc.ap().offset,
                        [xbc.ap().ap[0], [TILE + 3, NCH], [1, 3]]),
                bass.AP(xbc.ap().tensor, xbc.ap().offset + TILE,
                        [xbc.ap().ap[0], [TILE + 3, NCH], [1, 3]])), inc="tV")
        halo_v = s.cnt["tV"]

        for oc in range(NCH):
            pb = p[oc % 2]
            pbn = f"m1p{oc % 2}"
            for kc in range(8):
                w = []
                if kc == 0:
                    w = [("tV", max(vln, halo_v))]
                    if pbn in s.bank_free:
                        w.extend(s.bank_free[pbn])
                s.do("tensor", lambda e, oc=oc, kc=kc, pb=pb: e.matmul(
                    pb[:, :], win1_bf[:, kc * DIN_P + oc * 128: kc * DIN_P + oc * 128 + 128],
                    xln[:, kc * TILE:(kc + 1) * TILE], start=(kc == 0), stop=(kc == 7)),
                    wait=w, inc="tP" if kc == 7 else None)
            pe = s.cnt["tP"]
            s.do("vector", lambda e, oc=oc, pb=pb: e.tensor_copy(
                xbc[:, oc * (TILE + 3) + 3: oc * (TILE + 3) + 3 + TILE], pb[:, :]),
                wait=[("tP", pe)], inc="tV")
            s.bank_free[pbn] = [("tV", s.cnt["tV"])]
        pb = p[NCH % 2]
        pbn = f"m1p{NCH % 2}"
        for kc in range(8):
            w = list(s.bank_free.get(pbn, [])) if kc == 0 else []
            s.do("tensor", lambda e, kc=kc, pb=pb: e.matmul(
                pb[0:32, :], win1_bf[:, kc * DIN_P + 2176: kc * DIN_P + 2208],
                xln[:, kc * TILE:(kc + 1) * TILE], start=(kc == 0), stop=(kc == 7)),
                wait=w, inc="tP" if kc == 7 else None)
        pe_dt = s.cnt["tP"]
        atw = [("tP", pe_dt)]
        if s.cnt.get("sAT", 0) > 0:
            atw.append(("sAT", s.cnt["sAT"]))
        s.do("scalar", lambda e, pb=pb: e.activation(sp1[:, :], pb[0:32, :], AF.Abs),
             wait=atw, inc="tS")
        s.do("scalar", lambda e: e.activation(sp1[:, :], sp1[:, :], AF.Exp, scale=-1.0),
             wait=[("tS", s.cnt["tS"])], inc="tS")
        s.do("scalar", lambda e: e.activation(sp1[:, :], sp1[:, :], AF.Ln, bias=one_v[0:32, 0:1]),
             wait=[("tS", s.cnt["tS"])], inc="tS")
        s.do("scalar", lambda e, pb=pb: e.activation(sp2[:, :], pb[0:32, :], AF.Relu),
             wait=[("tS", s.cnt["tS"])], inc="tS")
        s.bank_free[pbn] = [("tS", s.cnt["tS"])]
        s.do("vector", lambda e: e.tensor_tensor(sp1[:, :], sp1[:, :], sp2[:, :], ALU.add),
             wait=[("tS", s.cnt["tS"])], inc="tV")
        s.do("vector", lambda e: e.tensor_scalar_mul(at_t[:, :], sp1[:, :], a_vec[:, 0:1]),
             wait=[("tV", s.cnt["tV"])], inc="tV")
        at_v = s.cnt["tV"]
        s.do("scalar", lambda e, ti=ti: e.dma_start(
            out=at_out[:, ti * TILE:(ti + 1) * TILE], in_=at_t[:, :]),
            wait=[("tV", at_v)], inc="sAT", inc_k=16)
        xbc_v = at_v

        issue_cpanel(ti, 0)
        issue_cpanel(ti, 1)
        for oc in range(NCH):
            pb = p[oc % 2]
            pbn = f"m1p{oc % 2}"
            sem_cnt = loads_dw[(ti, oc)]
            k_i = 0
            for ic in range(NCH):
                for k in range(4):
                    w = []
                    if k_i == 0:
                        w = [sem_cnt, ("tV", xbc_v)]
                        if ti > 0:
                            w.append(("sXC", s.cnt["sXC"]))
                        if pbn in s.bank_free:
                            w.extend(s.bank_free[pbn])
                    s.do("tensor", lambda e, oc=oc, ic=ic, k=k, pb=pb, k_i=k_i, i=oc % 2: e.matmul(
                        pb[:, :], pC[i][:, (ic * 4 + k) * 128:(ic * 4 + k + 1) * 128],
                        xbc[:, ic * (TILE + 3) + k: ic * (TILE + 3) + k + TILE],
                        start=(k_i == 0), stop=(k_i == 4 * NCH - 1)),
                        wait=w, inc="tP" if k_i == 4 * NCH - 1 else None)
                    k_i += 1
            pe = s.cnt["tP"]
            pan_free[oc % 2] = [("tP", pe)] + ([scratch_dep[oc % 2]] if (ti == 0 and oc % 2 in scratch_dep) else [])
            i2 = oc % 2
            sw = [("tP", pe)]
            if ("t_cr", i2) in s.bank_free:
                sw.extend(s.bank_free[("t_cr", i2)])
            s.do("scalar", lambda e, oc=oc, pb=pb, i2=i2: e.activation(
                t_cr[i2][:, :], pb[:, :], AF.Relu, bias=convb_pc[:, oc:oc + 1]),
                wait=sw, inc="tS")
            relu_c = s.cnt["tS"]
            vw = [("tP", pe), ("tS", relu_c)]
            if ("t_cz", i2) in s.bank_free:
                vw.extend(s.bank_free[("t_cz", i2)])
            s.do("vector", lambda e, oc=oc, pb=pb, i2=i2: e.tensor_scalar(
                t_cz[i2][:, :], pb[:, :], convb_pc[:, oc:oc + 1], 0.0, ALU.add, ALU.min),
                wait=vw, inc="tV")
            s.bank_free[pbn] = [("tS", relu_c), ("tV", s.cnt["tV"])]
            s.do("vector", lambda e, oc=oc, i2=i2: e.scalar_tensor_tensor(
                xc_t[:, oc * TILE:(oc + 1) * TILE], t_cz[i2][:, :], a_pre[:, 0:1],
                t_cr[i2][:, :], ALU.mult, ALU.add),
                wait=[("tV", s.cnt["tV"])], inc="tV")
            s.bank_free[("t_cr", i2)] = [("tV", s.cnt["tV"])]
            s.bank_free[("t_cz", i2)] = [("tV", s.cnt["tV"])]
            if oc + 2 < NCH:
                issue_cpanel(ti, oc + 2)
        xc_v = s.cnt["tV"]
        s.do("scalar", lambda e, ti=ti: e.dma_start(
            out=xc_fm[:, :, ti * TILE:(ti + 1) * TILE],
            in_=xc_t.ap().rearrange("p (cc t) -> p cc t", t=TILE)),
            wait=[("tV", xc_v)], inc="sXC", inc_k=16)

        for c in range(4):
            g = ti * 4 + c
            prs = [(xc_t[:, ch * TILE + c * 128: ch * TILE + (c + 1) * 128],
                    x_tm[:, ch * 128:(ch + 1) * 128]) for ch in range(16)]
            prs.append((xc_t[:, 16 * TILE + c * 128: 16 * TILE + (c + 1) * 128],
                        bc_tm[:, 0:128]))
            tp_v = emit_tpN(s, pT, "m1pT", id_bf, prs, [("tV", xc_v)])
            s.do("vector", lambda e, c=c: e.tensor_tensor_scan(
                acum_c[:, :], at_t[:, c * 128:(c + 1) * 128], at_t[:, c * 128:(c + 1) * 128],
                0.0, ALU.add, ALU.bypass),
                wait=[("tV", s.cnt["tV"])], inc="tV")
            scan_v = s.cnt["tV"]
            s.do("vector", lambda e, c=c: e.reduce_sum(
                asum[:, 0:1], at_t[:, c * 128:(c + 1) * 128], mybir.AxisListType.X),
                wait=[("tV", scan_v)], inc="tV")
            sum_v = s.cnt["tV"]
            w = [("tV", scan_v)] + s.bank_free.get("m1pF", [])
            s.do("tensor", lambda e: e.transpose(pF[:, 0:32], acum_c[:, :], id_f[0:32, 0:32]),
                 wait=w, inc="tP")
            s.do("vector", lambda e: e.tensor_copy(acum_tm[:, :], pF[:, 0:32]),
                 wait=[("tP", s.cnt["tP"])], inc="tV")
            s.bank_free["m1pF"] = [("tV", s.cnt["tV"])]
            w = [("tV", sum_v)] + s.bank_free.get("m1pF", [])
            s.do("tensor", lambda e: e.transpose(pF[0:1, 0:32], asum[:, 0:1], id_f[0:32, 0:32]),
                 wait=w, inc="tP")
            s.do("vector", lambda e: e.tensor_copy(asum_row[0:1, :], pF[0:1, 0:32]),
                 wait=[("tP", s.cnt["tP"])], inc="tV")
            s.bank_free["m1pF"] = [("tV", s.cnt["tV"])]
            row_v = s.cnt["tV"]
            s.do("scalar", lambda e: e.activation(g_row[0:1, :], asum_row[0:1, :], AF.Exp),
                 wait=[("tV", row_v)], inc="tS")
            g_s = s.cnt["tS"]
            w = [("tV", row_v)] + s.bank_free.get("m1pG", [])
            s.do("tensor", lambda e: e.matmul(pG[:, 0:32], C["ones_f"][0:1, :], asum_row[0:1, :],
                                              start=True, stop=True), wait=w)
            s.do("tensor", lambda e: e.matmul(pG[:, 32:64], C["ones_f"][0:1, :], g_row[0:1, :],
                                              start=True, stop=True),
                 wait=[("tS", g_s)], inc="tP")
            bc_pe = s.cnt["tP"]
            s.do("vector", lambda e: e.tensor_tensor(dstate[:, :], pG[:, 0:32], acum_tm[:, :],
                                                     ALU.subtract),
                 wait=[("tP", bc_pe)], inc="tV")
            s.do("scalar", lambda e: e.activation(dstate[:, :], dstate[:, :], AF.Exp),
                 wait=[("tV", s.cnt["tV"])], inc="tS")
            ds_s = s.cnt["tS"]
            s.do("vector", lambda e: e.tensor_copy(gbc_sb[:, :], pG[:, 32:64]),
                 wait=[("tP", bc_pe)], inc="tV")
            s.bank_free["m1pG"] = [("tV", s.cnt["tV"])]
            gbc_v = s.cnt["tV"]
            s.do("vector", lambda e: e.tensor_tensor(
                bdelta[:, :],
                bass.AP(bc_tm.ap().tensor, bc_tm.ap().offset,
                        [bc_tm.ap().ap[0], [0, 32], [1, 64]]),
                bass.AP(dstate.ap().tensor, dstate.ap().offset,
                        [dstate.ap().ap[0], [1, 32], [0, 64]]),
                ALU.mult), wait=[("tS", ds_s), ("tV", tp_v)], inc="tV")
            bd_v = s.cnt["tV"]
            cur = state_sb[st_buf]
            nxt = state_sb[1 - st_buf]
            stw = [("tV", s.cnt["tV"])]
            if s.cnt.get(f"sST{st_buf}", 0) > 0:
                pass
            s.do("scalar", lambda e, g=g, cur=cur: e.dma_start(
                out=st_out[g], in_=cur[:, :]), wait=stw, inc=f"sST{st_buf}", inc_k=16)
            for grp in range(4):
                pb = p[grp % 2]
                pbn = f"m1p{grp % 2}"
                for hh in range(8):
                    h = grp * 8 + hh
                    w = []
                    if hh == 0:
                        w = [("tV", bd_v)]
                        if pbn in s.bank_free:
                            w.extend(s.bank_free[pbn])
                    s.do("tensor", lambda e, h=h, hh=hh, pb=pb: e.matmul(
                        pb[0:64, hh * 64:(hh + 1) * 64],
                        bdelta[:, h * 64:(h + 1) * 64],
                        x_tm[:, h * 64:(h + 1) * 64],
                        start=True, stop=True),
                        wait=w, inc="tP" if hh == 7 else None)
                pe_g = s.cnt["tP"]
                gw = [("tP", pe_g), ("tV", gbc_v)]
                if ("gx", grp % 2) in s.bank_free:
                    gw.extend(s.bank_free[("gx", grp % 2)])
                s.do("vector", lambda e, grp=grp, cur=cur: e.tensor_tensor(
                    gexp_t[0:64, grp * 512:(grp + 1) * 512],
                    cur[:, grp * 512:(grp + 1) * 512],
                    bass.AP(gbc_sb.ap().tensor, gbc_sb.ap().offset + grp * 8,
                            [[gbc_sb.ap().ap[0][0], 64], [1, 8], [0, 64]]),
                    ALU.mult), wait=gw, inc="tV")
                nw = [("tV", s.cnt["tV"])]
                if s.cnt.get(f"sST{1 - st_buf}", 0) > 0:
                    nw.append((f"sST{1 - st_buf}", s.cnt[f"sST{1 - st_buf}"]))
                s.do("vector", lambda e, grp=grp, nxt=nxt, pb=pb: e.tensor_tensor(
                    nxt[:, grp * 512:(grp + 1) * 512],
                    gexp_t[0:64, grp * 512:(grp + 1) * 512],
                    pb[0:64, 0:512], ALU.add),
                    wait=nw, inc="tV")
                s.bank_free[pbn] = [("tV", s.cnt["tV"])]
                s.bank_free[("gx", grp % 2)] = [("tV", s.cnt["tV"])]
            st_buf = 1 - st_buf
    s.do("scalar", lambda e: e.dma_start(
        out=st_out[TOK // 128], in_=state_sb[st_buf][:, :]),
        wait=[("tV", s.cnt["tV"])], inc=f"sST{st_buf}", inc_k=16)
    for nm in ("sXC", "sAT", "sST0", "sST1", "dS0", "dS1"):
        s.wait("scalar", nm, s.cnt.get(nm, 0))
    stack.close()


def build_m1():
    nc = bass.Bass("TRN2", target_bir_lowering=False)
    io = {}
    for nm, shape, dt, kind in (
            ("x", [D, TOK + 3], F32, "ExternalInput"), ("w1T", [D, DIN_P], F32, "ExternalInput"),
            ("convT", [2176, 4 * 2176], F32, "ExternalInput"),
            ("convb", [128, NCH], F32, "ExternalInput"), ("prelu", [1, 1], F32, "ExternalInput"),
            ("alog", [32, 1], F32, "ExternalInput"), ("ident", [128, 128], F32, "ExternalInput"),
            ("epsc", [1, 1], F32, "ExternalInput"),
            ("xc", [2176, TOK], BF16, "ExternalOutput"), ("at", [32, TOK], F32, "ExternalOutput"),
            ("states", [TOK // 128 + 1, 64, 2048], F32, "ExternalOutput")):
        io[nm] = nc.dram_tensor(nm, shape, dt, kind=kind).ap()
    s = Sched(nc)
    C = common_tiles(s, nc)
    io3 = dict(io)
    io3["xh"] = io["x"][:, 0:3]
    io3["x"] = io["x"][:, 3:TOK + 3]
    emit_m1(s, nc, io3, C)
    s.run()
    s.stack.close()
    return nc


def emit_m2(s, nc, io, C, pre="m2_"):
    x_in = io["x"]; xc_in = io["xc"]; at_in = io["at"]; st_in = io["states"]
    hin_in = io["hin"]; w2T = io["w2T"]; w3T = io["w3T"]; lng = io["lng"]
    lnb = io["lnb"]; triu = io["triu"]; ident = io["ident"]; out = io["out"]
    stack = ExitStack()
    sb = lambda name, shape, dt=F32: stack.enter_context(nc.sbuf_tensor(pre + name, shape, dt))
    ps = lambda name, shape, dt=F32: stack.enter_context(nc.psum_tensor(pre + name, shape, dt))
    C = dict(C)
    C["ps_a"] = ps("lnps_a", [128, TILE])
    C["ps_b"] = ps("lnps_b", [128, TILE])

    win2_bf = sb("win2_bf", [128, 8 * 2048], BF16)
    win3_bf = sb("win3_bf", [128, 16 * D], BF16)
    lng_pc = sb("lng_pc", [128, 16])
    lnb_pc = sb("lnb_pc", [128, 16], BF16)
    cb = sb("cb", [128, 8])
    cb_row = sb("cb_row", [1, 1024])
    triu_sb = sb("triu_sb", [128, 128])
    id_bf = sb("id_bf", [128, 128], BF16)
    id_f = sb("id_f", [128, 128])
    ones128 = sb("ones128", [128, 128])
    x_t = sb("m2x", [128, 8 * TILE])
    xln = sb("m2xln", [128, 8 * TILE], BF16)
    at_t = sb("m2at", [32, TILE])
    xcb = [sb(f"xcb{i}", [128, NCH * 128], BF16) for i in range(3)]
    u2b = [sb(f"u2b{i}", [128, 2048], BF16) for i in range(3)]
    x_tmb = [sb(f"xtmb{i}", [128, 2048], BF16) for i in range(2)]
    bc_tmb = [sb(f"bctmb{i}", [128, 128], BF16) for i in range(2)]
    c0b = [sb(f"c0b{i}", [64, 128], BF16) for i in range(2)]
    smb = [sb(f"smb{i}", [128, 128], BF16) for i in range(2)]
    acumb = [sb(f"acumb{i}", [128, 32]) for i in range(2)]
    doutb = [sb(f"doutb{i}", [128, 32]) for i in range(2)]
    acum_c = sb("m2acumc", [32, 128])
    asum = sb("m2asum", [32, 1])
    asum_row = sb("m2asumrow", [1, 32])
    r_sb = sb("r_sb", [32, 1])
    g_row = sb("m2grow", [1, 32])
    stb = sb("stb", [64, 2048], BF16)
    hsb = [sb(f"hsb{i}", [64, 2048], BF16) for i in range(2)]
    hin_sb = sb("hin_sb", [64, 2048], BF16)
    y_allb = [sb(f"y_all{i}", [128, 2048], BF16) for i in range(2)]
    yo_sc = sb("yo_sc", [128, 512], BF16)
    hn_fm = sb("hn_fm", [128, 16 * 256], BF16)
    dg4b = [sb(f"dg4{i}", [128, 512]) for i in range(2)]
    d4b = [sb(f"d4{i}", [128, 512]) for i in range(2)]
    mrow = sb("m2mrow", [128, 1])
    eps_col = sb("eps_col", [128, 1])
    vrow = sb("m2vrow", [128, 1])
    rrow = sb("m2rrow", [128, 1])

    p = [ps(f"m2p{i}", [128, TILE]) for i in range(3)]
    pT = [ps(f"m2pT{i}", [128, 1024], BF16) for i in range(2)]

    s.do("gpsimd", lambda e: e.dma_start(
        out=win2_bf.ap().rearrange("p (kc m) -> p kc m", m=2048),
        in_=w2T.rearrange("(kc kp) m -> kp kc m", kp=128)), inc="dW", inc_k=16)
    s.do("gpsimd", lambda e: e.dma_start(
        out=win3_bf.ap().rearrange("p (kc m) -> p kc m", m=D),
        in_=w3T.rearrange("(kc kp) m -> kp kc m", kp=128)), inc="dW", inc_k=16)
    s.do("gpsimd", lambda e: e.dma_start(out=lng_pc[:, :], in_=lng), inc="dW", inc_k=16)
    s.do("gpsimd", lambda e: e.dma_start(out=lnb_pc[:, :], in_=lnb), inc="dW", inc_k=16)
    s.do("gpsimd", lambda e: e.dma_start(out=id_bf[:, :], in_=ident), inc="dW", inc_k=16)
    m2_dw = s.cnt["dW"]
    for dst, src in ((triu_sb[:, :], triu), (id_f[:, :], ident)):
        s.do("sync", lambda e, dst=dst, src=src: e.dma_start(out=dst, in_=src), inc="dC", inc_k=16)
    s.do("gpsimd", lambda e: e.dma_start(out=hin_sb[:, :], in_=hin_in), inc="dHin", inc_k=16)
    m2_dc = s.cnt["dC"]
    m2_dhin = s.cnt["dHin"]
    if "hmask" in io:
        hmask2 = sb("hmask2", [128, 1])
        s.do("sync", lambda e: e.dma_start(out=hmask2[:, 0:1], in_=io["hmask"]), inc="dC", inc_k=16)
        m2_dc = s.cnt["dC"]
        s.do("vector", lambda e: e.tensor_scalar_mul(hin_sb[:, :], hin_sb[:, :], hmask2[0:64, 0:1]),
             wait=[("dC", m2_dc), ("dHin", m2_dhin)], inc="tV")
    s.do("vector", lambda e: e.memset(ones128[:, :], 1.0), wait=[("dC", m2_dc), ("dW", m2_dw)])
    s.do("vector", lambda e: e.memset(eps_col[:, 0:1], 1e-5))
    s.do("vector", lambda e: e.memset(r_sb[:, 0:1], 0.0), inc="tV")

    for half in range(2):
        pscb = C["ps_a"] if half == 0 else C["ps_b"]
        for kc in range(16):
            w = [("dW", m2_dw)] if kc == 0 else []
            s.do("tensor", lambda e, kc=kc, half=half, pscb=pscb: e.matmul(
                pscb[0:1, 0:512], lnb_pc[:, kc:kc + 1],
                win3_bf[:, kc * D + half * 512: kc * D + (half + 1) * 512],
                start=(kc == 0), stop=(kc == 15)),
                wait=w, inc="tP" if kc == 15 else None)
        s.do("vector", lambda e, half=half, pscb=pscb: e.tensor_copy(
            cb_row[0:1, half * 512:(half + 1) * 512], pscb[0:1, 0:512]),
            wait=[("tP", s.cnt["tP"])], inc="tV")
    tr_v = s.cnt["tV"]
    for c8 in range(8):
        s.do("tensor", lambda e, c8=c8: e.transpose(
            C["ps_a"][:, 0:1], cb_row[0:1, c8 * 128:(c8 + 1) * 128], id_f[0:1, 0:1]),
            wait=[("tV", tr_v)], inc="tP")
        s.do("vector", lambda e, c8=c8: e.tensor_copy(cb[:, c8:c8 + 1], C["ps_a"][:, 0:1]),
             wait=[("tP", s.cnt["tP"])], inc="tV")
        tr_v = s.cnt["tV"]
    for kc in range(16):
        s.do("vector", lambda e, kc=kc: e.tensor_scalar_mul(
            win3_bf[:, kc * D:(kc + 1) * D], win3_bf[:, kc * D:(kc + 1) * D],
            lng_pc[:, kc:kc + 1]), wait=[("tV", s.cnt["tV"])], inc="tV")
    init_v = s.cnt["tV"]

    x_fm = x_in.rearrange("(fc fp) t -> fp fc t", fp=128)
    xc_fm = xc_in.rearrange("(cc cp) t -> cp cc t", cp=128)
    xln_fm = io["xln_in"].rearrange("(fc fp) t -> fp fc t", fp=128)
    out_fm = out.rearrange("(fc fp) t -> fp fc t", fp=128)

    H = {}

    def head(g):
        ti, c = divmod(g, 4)
        i = g % 2
        i3 = g % 3
        h = {"i3": i3}
        w = list(s.bank_free.get(("xcb", i3), []))
        s.do("sync", lambda e, ti=ti, c=c, i3=i3: e.dma_start(
            out=xcb[i3].ap().rearrange("p (cc t) -> p cc t", t=128),
            in_=xc_fm[:, :, ti * TILE + c * 128: ti * TILE + (c + 1) * 128]),
            wait=w, inc=f"dXC{i3}", inc_k=16)
        h["dxc"] = s.cnt[f"dXC{i3}"]
        dxl = H[("dxl", ti)]
        for piece in range(4):
            pb = p[piece % 2]
            pbn = f"m2p{piece % 2}"
            for kc in range(8):
                w = []
                if kc == 0:
                    w = [("dXL", dxl)]
                    if ti == 0:
                        w.append(("tV", init_v))
                    if pbn in s.bank_free:
                        w.extend(s.bank_free[pbn])
                s.do("tensor", lambda e, c=c, piece=piece, kc=kc, pb=pb: e.matmul(
                    pb[:, :], xln[:, kc * TILE + c * 128: kc * TILE + (c + 1) * 128],
                    win2_bf[:, kc * 2048 + piece * 512: kc * 2048 + (piece + 1) * 512],
                    start=(kc == 0), stop=(kc == 7)),
                    wait=w, inc="tP" if kc == 7 else None)
            pe = s.cnt["tP"]
            sw = [("tP", pe)] + list(s.bank_free.get(("u2", i3), []) if piece == 0 else [])
            s.do("scalar", lambda e, piece=piece, pb=pb, i3=i3: e.activation(
                u2b[i3][:, piece * 512:(piece + 1) * 512], pb[:, :], AF.Gelu),
                wait=sw, inc="tS")
            s.bank_free[pbn] = [("tS", s.cnt["tS"])]
        h["u2_v"] = s.cnt["tS"]

        prs = [(xcb[i3][:, ch * 128:(ch + 1) * 128],
                x_tmb[i][:, ch * 128:(ch + 1) * 128]) for ch in range(16)]
        prs.append((xcb[i3][:, 16 * 128:17 * 128], bc_tmb[i][:, 0:128]))
        tpw = [(f"dXC{i3}", h["dxc"])] + list(s.bank_free.get(("xtm", i), []))
        h["tp_v"] = emit_tpN(s, pT, "m2pT", id_bf, prs, tpw)
        w = list(s.bank_free.get(("m2pT", 0), []))
        s.do("tensor", lambda e, i=i: e.transpose(pT[0][0:64, 0:128], bc_tmb[i][:, 64:128],
                                                  id_bf[:, :]),
             wait=w + [("tV", h["tp_v"])], inc="tP")
        s.do("vector", lambda e, i=i: e.tensor_copy(c0b[i][:, :], pT[0][0:64, 0:128]),
             wait=[("tP", s.cnt["tP"])], inc="tV")
        s.bank_free[("m2pT", 0)] = [("tV", s.cnt["tV"])]
        h["c0_v"] = s.cnt["tV"]

        s.do("vector", lambda e, c=c, ti=ti: e.tensor_tensor_scan(
            acum_c[:, :], at_t[:, c * 128:(c + 1) * 128], at_t[:, c * 128:(c + 1) * 128],
            0.0, ALU.add, ALU.bypass),
            wait=[(f"dAT", H[("dat", ti)]), ("tV", s.cnt["tV"])], inc="tV")
        scan_v = s.cnt["tV"]
        s.do("vector", lambda e, c=c: e.reduce_sum(
            asum[:, 0:1], at_t[:, c * 128:(c + 1) * 128], mybir.AxisListType.X),
            wait=[("tV", scan_v)], inc="tV")
        sum_v = s.cnt["tV"]
        w = [("tV", scan_v)] + s.bank_free.get("m2p2", [])
        s.do("tensor", lambda e: e.transpose(p[2][:, 0:32], acum_c[:, :], id_f[0:32, 0:32]),
             wait=w, inc="tP")
        s.do("vector", lambda e, i=i: e.tensor_copy(acumb[i][:, :], p[2][:, 0:32]),
             wait=[("tP", s.cnt["tP"])], inc="tV")
        s.bank_free["m2p2"] = [("tV", s.cnt["tV"])]
        h["acum_v"] = s.cnt["tV"]
        s.do("scalar", lambda e, i=i: e.activation(doutb[i][:, :], acumb[i][:, :], AF.Exp),
             wait=[("tV", h["acum_v"])], inc="tS")
        h["dout_s"] = s.cnt["tS"]
        w = s.bank_free.get("m2p2", [])
        s.do("tensor", lambda e: e.transpose(p[2][0:1, 0:32], r_sb[:, 0:1], id_f[0:32, 0:32]),
             wait=w + [("tV", s.cnt["tV"])], inc="tP")
        s.do("vector", lambda e: e.tensor_copy(asum_row[0:1, :], p[2][0:1, 0:32]),
             wait=[("tP", s.cnt["tP"])], inc="tV")
        s.bank_free["m2p2"] = [("tV", s.cnt["tV"])]
        s.do("scalar", lambda e: e.activation(g_row[0:1, :], asum_row[0:1, :], AF.Exp),
             wait=[("tV", s.cnt["tV"])], inc="tS")
        grow_s = s.cnt["tS"]
        s.do("vector", lambda e: e.tensor_tensor(r_sb[:, 0:1], r_sb[:, 0:1], asum[:, 0:1],
                                                 ALU.add),
             wait=[("tV", sum_v), ("tS", grow_s)], inc="tV")

        stw = list(s.bank_free.get("stb", []))
        s.do("gpsimd", lambda e, g=g: e.dma_start(out=stb[:, :], in_=st_in[g]),
             wait=stw, inc="dST", inc_k=16)
        dst_c = s.cnt["dST"]
        w = s.bank_free.get("m2p2", [])
        s.do("tensor", lambda e: e.matmul(p[2][0:64, 0:32], C["ones_f"][0:1, 0:64],
                                          g_row[0:1, :], start=True, stop=True),
             wait=w + [("tS", grow_s)], inc="tP")
        gbc_pe = s.cnt["tP"]
        hbw = [("tP", gbc_pe), ("dHin", m2_dhin)] + list(s.bank_free.get(("hsb", i), []))
        s.do("vector", lambda e, i=i: e.tensor_tensor(
            hsb[i][:, :], hin_sb[:, :],
            bass.AP(p[2].ap().tensor, p[2].ap().offset,
                    [[p[2].ap().ap[0][0], 64], [1, 32], [0, 64]]),
            ALU.mult), wait=hbw, inc="tV")
        s.bank_free["m2p2"] = [("tV", s.cnt["tV"])]
        s.do("vector", lambda e, i=i: e.tensor_tensor(hsb[i][:, :], hsb[i][:, :], stb[:, :],
                                                      ALU.add),
             wait=[("tV", s.cnt["tV"]), ("dST", dst_c)], inc="tV")
        s.bank_free["stb"] = [("tV", s.cnt["tV"])]
        h["st_v"] = s.cnt["tV"]

        w = [("tV", h["c0_v"]), (f"dXC{i3}", h["dxc"])] + s.bank_free.get("m2p2", [])
        s.do("tensor", lambda e, i=i, i3=i3: e.matmul(
            p[2][:, 0:128], xcb[i3][0:64, 16 * 128:17 * 128],
            c0b[i][:, :], start=True, stop=True), wait=w, inc="tP")
        sc_pe = s.cnt["tP"]
        s.bank_free[("xcb", i3)] = [("tP", sc_pe), ("tV", h["tp_v"])]
        smw = [("tP", sc_pe)] + list(s.bank_free.get(("smb", i), []))
        s.do("vector", lambda e, i=i: e.tensor_tensor(smb[i][:, :], p[2][:, 0:128],
                                                      triu_sb[:, :], ALU.mult),
             wait=smw, inc="tV")
        s.bank_free["m2p2"] = [("tV", s.cnt["tV"])]
        h["sm_v"] = s.cnt["tV"]
        H[g] = h

    def tail_sl(g):
        ti, c = divmod(g, 4)
        i = g % 2
        h = H[g]
        y_all = y_allb[i]
        nb = 8
        st = {}

        def e_dg(b):
            bi = b % 2
            h0 = b * 4
            w = [("tV", h["sm_v"])]
            if ("dg", bi) in s.bank_free:
                w.extend(s.bank_free[("dg", bi)])
            s.do("vector", lambda e, h0=h0, i=i, bi=bi: e.tensor_tensor(
                dg4b[bi][:, :],
                bass.AP(id_f.ap().tensor, id_f.ap().offset,
                        [id_f.ap().ap[0], [0, 4], [1, 128]]),
                bass.AP(acumb[i].ap().tensor, acumb[i].ap().offset + h0,
                        [acumb[i].ap().ap[0], [1, 4], [0, 128]]),
                ALU.mult), wait=w, inc="tV")
            st[("dg_v", b)] = s.cnt["tV"]

        def e_p(b):
            bi = b % 2
            w = list(s.bank_free.get(("psA", bi), []))
            pband = C["ps_a"] if bi == 0 else C["ps_b"]
            s.do("tensor", lambda e, bi=bi, pband=pband: e.matmul(
                pband[:, :], ones128[:, :], dg4b[bi][:, :], start=True, stop=True),
                wait=w + [("tV", st[("dg_v", b)])], inc="tP")
            st[("p_pe", b)] = s.cnt["tP"]

        def e_d4(b):
            bi = b % 2
            h0 = b * 4
            pband = C["ps_a"] if bi == 0 else C["ps_b"]
            s.do("vector", lambda e, h0=h0, i=i, bi=bi, pband=pband: e.tensor_tensor(
                d4b[bi][:, :], pband[:, :],
                bass.AP(acumb[i].ap().tensor, acumb[i].ap().offset + h0,
                        [acumb[i].ap().ap[0], [1, 4], [0, 128]]),
                ALU.subtract), wait=[("tP", st[("p_pe", b)])], inc="tV")
            s.bank_free[("psA", bi)] = [("tV", s.cnt["tV"])]
            s.do("vector", lambda e, bi=bi: e.tensor_scalar_min(
                d4b[bi][:, :], d4b[bi][:, :], 0.0),
                wait=[("tV", s.cnt["tV"])], inc="tV")
            st[("min_v", b)] = s.cnt["tV"]

        def e_exp(b):
            bi = b % 2
            s.do("scalar", lambda e, bi=bi: e.activation(d4b[bi][:, :], d4b[bi][:, :], AF.Exp),
                 wait=[("tV", st[("min_v", b)])], inc="tS")
            st[("exp_s", b)] = s.cnt["tS"]

        def e_sl(b):
            bi = b % 2
            b4 = b % 2
            w = [("tS", st[("exp_s", b)])]
            grp_prev = b // 2 - 1
            if ("slrd", b % 2) in s.bank_free:
                w.extend(s.bank_free[("slrd", b % 2)])
            s.do("vector", lambda e, b=b, i=i, bi=bi: e.tensor_tensor(
                sl_sb[:, (b % 2) * 512:(b % 2) * 512 + 512], d4b[bi][:, :],
                bass.AP(smb[i].ap().tensor, smb[i].ap().offset,
                        [smb[i].ap().ap[0], [0, 4], [1, 128]]),
                ALU.mult), wait=w, inc="tV")
            s.bank_free[("dg", bi)] = [("tV", s.cnt["tV"])]
            st[("sl_v", b)] = s.cnt["tV"]

        def e_y(grp):
            sl_v = st[("sl_v", 2 * grp + 1)]
            w1 = [("tV", sl_v), ("tV", h["tp_v"])] + s.bank_free.get("m2p1", [])
            w2 = [("tV", h["st_v"]), ("tV", h["c0_v"])] + s.bank_free.get("m2p2", [])
            for hh in range(8):
                hd = grp * 8 + hh
                s.do("tensor", lambda e, hd=hd, hh=hh, i=i: e.matmul(
                    p[1][:, hh * 64:(hh + 1) * 64],
                    sl_sb[:, (hh // 4) * 512 + (hh % 4) * 128:
                             (hh // 4) * 512 + (hh % 4) * 128 + 128],
                    x_tmb[i][:, hd * 64:(hd + 1) * 64], start=True, stop=True),
                    wait=w1 if hh == 0 else (), inc="tP" if hh == 7 else None)
            yd_pe = s.cnt["tP"]
            for hh in range(8):
                hd = grp * 8 + hh
                s.do("tensor", lambda e, hd=hd, hh=hh, i=i: e.matmul(
                    p[2][0:128, hh * 64:(hh + 1) * 64], c0b[i][:, :],
                    hsb[i][:, hd * 64:(hd + 1) * 64], start=True, stop=True),
                    wait=w2 if hh == 0 else (), inc="tP" if hh == 7 else None)
            yo_pe = s.cnt["tP"]
            s.bank_free[("slrd", 0)] = [("tP", yd_pe)]
            s.bank_free[("slrd", 1)] = [("tP", yd_pe)]
            s.do("vector", lambda e, grp=grp, i=i: e.tensor_tensor(
                yo_sc[:, :], p[2][:, 0:512],
                bass.AP(doutb[i].ap().tensor, doutb[i].ap().offset + grp * 8,
                        [doutb[i].ap().ap[0], [1, 8], [0, 64]]),
                ALU.mult), wait=[("tP", yo_pe), ("tS", h["dout_s"])], inc="tV")
            s.bank_free["m2p2"] = [("tV", s.cnt["tV"])]
            yw = [("tV", s.cnt["tV"]), ("tP", yd_pe)]
            if grp == 0:
                yw.extend(s.bank_free.get(("yall", i), []))
            s.do("vector", lambda e, grp=grp, i=i, y_all=y_all: e.tensor_tensor(
                y_all[:, grp * 512:(grp + 1) * 512], yo_sc[:, :], p[1][:, 0:512],
                ALU.add), wait=yw, inc="tV")
            s.bank_free["m2p1"] = [("tV", s.cnt["tV"])]
            if grp == 3:
                s.bank_free[("hsb", i)] = [("tP", yo_pe)]
                h["y_v"] = s.cnt["tV"]

        e_dg(0); e_p(0); e_dg(1); e_d4(0); e_p(1); e_exp(0); e_d4(1); e_sl(0)
        e_exp(1); e_sl(1); e_y(0)
        for grp in range(1, 4):
            b0, b1 = 2 * grp, 2 * grp + 1
            e_dg(b0); e_p(b0); e_dg(b1); e_d4(b0); e_p(b1); e_exp(b0)
            e_d4(b1); e_sl(b0); e_exp(b1); e_sl(b1); e_y(grp)

    def tail_ln(g):
        ti, c = divmod(g, 4)
        i = g % 2
        h = H.pop(g)
        i3 = h["i3"]
        y_all = y_allb[i]
        hn = y_all
        hga = xcb[i3]
        s.do("vector", lambda e, i3=i3, hga=hga, y_all=y_all: e.tensor_tensor(
            hga[:, 0:2048], y_all[:, :], u2b[i3][:, :], ALU.mult),
             wait=[("tV", h["y_v"]), ("tS", h["u2_v"])], inc="tV")
        hga_v = s.cnt["tV"]
        s.bank_free[("u2", i3)] = [("tV", hga_v)]
        s.do("vector", lambda e, hga=hga: e.reduce_sum(
            mrow[:, 0:1], hga[:, 0:2048], mybir.AxisListType.X),
             wait=[("tV", hga_v)], inc="tV")
        s.do("scalar", lambda e, i3=i3, hga=hga: e.activation(
            u2b[i3][:, :], hga[:, 0:2048], AF.Square),
             wait=[("tV", hga_v)], inc="tS")
        s.do("vector", lambda e, i3=i3: e.reduce_sum(vrow[:, 0:1], u2b[i3][:, :],
                                                     mybir.AxisListType.X),
             wait=[("tS", s.cnt["tS"])], inc="tV")
        sq_v = s.cnt["tV"]
        s.bank_free[("u2", i3)] = [("tV", sq_v)]
        s.do("vector", lambda e: e.tensor_scalar_mul(mrow[:, 0:1], mrow[:, 0:1], 1.0 / 2048),
             wait=[("tV", sq_v)], inc="tV")
        s.do("vector", lambda e: e.tensor_tensor(rrow[:, 0:1], mrow[:, 0:1], mrow[:, 0:1],
                                                 ALU.mult),
             wait=[("tV", s.cnt["tV"])], inc="tV")
        s.do("vector", lambda e: e.scalar_tensor_tensor(
            vrow[:, 0:1], vrow[:, 0:1], 1.0 / 2048, rrow[:, 0:1], ALU.mult, ALU.subtract),
            wait=[("tV", s.cnt["tV"])], inc="tV")
        s.do("scalar", lambda e: e.activation(rrow[:, 0:1], vrow[:, 0:1], AF.Ln,
                                              bias=eps_col[:, 0:1]),
             wait=[("tV", s.cnt["tV"])], inc="tS")
        s.do("scalar", lambda e: e.activation(rrow[:, 0:1], rrow[:, 0:1], AF.Exp, scale=-0.5),
             wait=[("tS", s.cnt["tS"])], inc="tS")
        rstd_s = s.cnt["tS"]
        s.do("vector", lambda e, hga=hga, hn=hn: e.tensor_scalar(
            hn[:, :], hga[:, 0:2048], mrow[:, 0:1], rrow[:, 0:1], ALU.subtract, ALU.mult),
            wait=[("tS", rstd_s)], inc="tV")
        s.bank_free[("xcb", i3)] = [("tV", s.cnt["tV"])]
        hn_v = s.cnt["tV"]
        prs = [(hn[:, ch * 128:(ch + 1) * 128],
                hn_fm[:, ch * 256 + (c % 2) * 128: ch * 256 + (c % 2) * 128 + 128])
               for ch in range(16)]
        emit_tpN(s, pT, "m2pT", id_bf, prs,
                 [("tV", hn_v)] + list(s.bank_free.get("hnfm", [])))
        s.bank_free[("yall", i)] = [("tP", s.cnt["tP"])]
        if c % 2 == 1:
            hnfm_v = s.cnt["tV"]
            tcol = (c - 1) * 128
            for oc in range(8):
                pb = p[oc % 2]
                pbn = f"m2p{oc % 2}"
                for kc in range(16):
                    w = []
                    if kc == 0:
                        w = [("tV", hnfm_v)]
                        if pbn in s.bank_free:
                            w.extend(s.bank_free[pbn])
                    s.do("tensor", lambda e, oc=oc, kc=kc, pb=pb: e.matmul(
                        pb[:, 0:256], win3_bf[:, kc * D + oc * 128: kc * D + oc * 128 + 128],
                        hn_fm[:, kc * 256:(kc + 1) * 256], start=(kc == 0), stop=(kc == 15)),
                        wait=w, inc="tP" if kc == 15 else None)
                pe = s.cnt["tP"]
                s.do("vector", lambda e, oc=oc, pb=pb, tcol=tcol: e.scalar_tensor_tensor(
                    x_t[:, oc * TILE + tcol: oc * TILE + tcol + 256],
                    pb[:, 0:256], cb[:, oc:oc + 1],
                    x_t[:, oc * TILE + tcol: oc * TILE + tcol + 256],
                    ALU.add, ALU.add), wait=[("tP", pe)], inc="tV")
                s.bank_free[pbn] = [("tV", s.cnt["tV"])]
            s.bank_free["hnfm"] = [("tP", s.cnt["tP"])]

    sl_sb = sb("sl_sb", [128, 1024], BF16)

    for ti in range(NT):
        xw = [("tV", s.cnt["tV"])]
        if s.cnt.get("sO", 0) > 0:
            xw.append(("sO", s.cnt["sO"]))
        s.do("sync", lambda e, ti=ti: e.dma_start(
            out=x_t.ap().rearrange("p (fc t) -> p fc t", t=TILE),
            in_=x_fm[:, :, ti * TILE:(ti + 1) * TILE]), wait=xw, inc="dX0", inc_k=16)
        dx = s.cnt["dX0"]
        s.do("sync", lambda e, ti=ti: e.dma_start(
            out=at_t[:, :], in_=at_in[:, ti * TILE:(ti + 1) * TILE]),
            wait=[("tV", s.cnt["tV"])], inc="dAT", inc_k=16)
        H[("dat", ti)] = s.cnt["dAT"]
        xlw = [("tP", H[("xlnfree", ti - 1)])] if ti > 0 else []
        s.do("sync", lambda e, ti=ti: e.dma_start(
            out=xln.ap().rearrange("p (fc t) -> p fc t", t=TILE),
            in_=xln_fm[:, :, ti * TILE:(ti + 1) * TILE]),
            wait=xlw, inc="dXL", inc_k=16)
        H[("dxl", ti)] = s.cnt["dXL"]

        g0 = ti * 4
        head(g0)
        head(g0 + 1)
        tail_sl(g0)
        head(g0 + 2)
        tail_sl(g0 + 1)
        tail_ln(g0)
        head(g0 + 3)
        H[("xlnfree", ti)] = s.cnt["tP"]
        tail_sl(g0 + 2)
        tail_ln(g0 + 1)
        tail_sl(g0 + 3)
        tail_ln(g0 + 2)
        tail_ln(g0 + 3)
        s.do("scalar", lambda e, ti=ti: e.dma_start(
            out=out_fm[:, :, ti * TILE:(ti + 1) * TILE],
            in_=x_t.ap().rearrange("p (fc t) -> p fc t", t=TILE)),
            wait=[("tV", s.cnt["tV"])], inc="sO", inc_k=16)

    s.wait("scalar", "sO", s.cnt["sO"])
    stack.close()


def build_m2():
    nc = bass.Bass("TRN2", target_bir_lowering=False)
    io = {}
    for nm, shape, dt, kind in (
            ("x", [D, TOK], F32, "ExternalInput"), ("xc", [2176, TOK], BF16, "ExternalInput"),
            ("at", [32, TOK], F32, "ExternalInput"),
            ("states", [TOK // 128 + 1, 64, 2048], F32, "ExternalInput"),
            ("hin", [64, 2048], F32, "ExternalInput"), ("w2T", [D, 2048], F32, "ExternalInput"),
            ("w3T", [2048, D], F32, "ExternalInput"), ("lng", [128, 2048], F32, "ExternalInput"),
            ("lnb", [128, 2048], F32, "ExternalInput"), ("triu", [128, 128], F32, "ExternalInput"),
            ("ident", [128, 128], F32, "ExternalInput"), ("epsc", [1, 1], F32, "ExternalInput"),
            ("out", [D, TOK], F32, "ExternalOutput")):
        io[nm] = nc.dram_tensor(nm, shape, dt, kind=kind).ap()
    s = Sched(nc)
    C = common_tiles(s, nc)
    emit_m2(s, nc, io, C)
    s.run()
    s.stack.close()
    return nc


def m_host(x2_fm_percore, W_in1, A_log_, conv_w, conv_b_, prelu_a_, W_in2, W_in3, ln_g_, ln_b_):
    CH = 2176
    epsc = np.full((1, 1), 1e-5, np.float32)
    triu_ = np.triu(np.ones((128, 128), np.float32))
    ident_ = np.eye(128, dtype=np.float32)
    w1T = wT(np.asarray(W_in1)[2048:4256, :])
    convT_ = np.ascontiguousarray(np.asarray(conv_w).transpose(1, 2, 0).reshape(CH, 4 * CH))
    nc1 = build_m1()
    in1 = []
    for c in range(N_CORES):
        if c % 2 == 1:
            halo = x2_fm_percore[c - 1][:, 4093:4096]
        else:
            halo = np.zeros((D, 3), np.float32)
        x2e = np.ascontiguousarray(np.concatenate([halo, x2_fm_percore[c]], axis=1))
        in1.append({
            "x": x2e, "w1T": w1T, "convT": convT_,
            "convb": pc(conv_b_), "prelu": np.full((1, 1), prelu_a_, np.float32),
            "alog": np.asarray(A_log_, np.float32).reshape(32, 1),
            "ident": ident_, "epsc": epsc,
        })
    r1 = run_bass_kernel_spmd(nc1, in1, core_ids=list(range(N_CORES))).results

    nc2 = build_m2()
    zeros_h = np.zeros((64, 2048), np.float32)
    in2 = []
    for c in range(N_CORES):
        in2.append({
            "x": x2_fm_percore[c],
            "xc": r1[c]["xc"], "at": r1[c]["at"], "states": r1[c]["states"],
            "hin": r1[c - 1]["states"][TOK // 128] if c % 2 == 1 else zeros_h,
            "w2T": wT(W_in2), "w3T": wT(W_in3),
            "lng": np.ascontiguousarray(np.broadcast_to(np.asarray(ln_g_, np.float32)[None, :], (128, 2048))),
            "lnb": np.ascontiguousarray(np.broadcast_to(np.asarray(ln_b_, np.float32)[None, :], (128, 2048))),
            "triu": triu_, "ident": ident_, "epsc": epsc,
        })
    r2 = run_bass_kernel_spmd(nc2, in2, core_ids=list(range(N_CORES))).results
    return [r2[c]["out"] for c in range(N_CORES)]


CC_GROUPS = [[0, 1], [2, 3], [4, 5], [6, 7]]


def build_fused():
    nc = bass.Bass("TRN2", target_bir_lowering=False)
    EI = lambda nm, shape, dt=F32: nc.dram_tensor(nm, shape, dt, kind="ExternalInput").ap()
    IN = lambda nm, shape, dt=F32: nc.dram_tensor(nm, shape, dt).ap()

    io = {
        "x": EI("x", [D, TOK]),
        "mask": EI("mask", [128, 1]),
        "WqT": EI("WqT", [D, D]), "WkT": EI("WkT", [D, D]), "WvT": EI("WvT", [D, D]),
        "WoT": EI("WoT", [D, D]),
        "bq": EI("bq", [128, 8]), "bk": EI("bk", [128, 8]), "bo": EI("bo", [128, 8]),
        "bv_bc": EI("bv_bc", [128, 1024]),
        "wA": EI("wA", [32, 128, 2048]),
        "h1P": EI("h1P", [32, 128, 4096]), "h2P": EI("h2P", [32, 128, 4096]),
        "w3P": EI("w3P", [8, 128, 4096]),
        "s12": EI("s12", [128, 64]), "sh12": EI("sh12", [128, 64]), "s3": EI("s3", [128, 8]),
        "m_w1T": EI("m_w1T", [D, DIN_P]), "convP": EI("convP", [NCH, 128, 8704]),
        "convb": EI("convb", [128, NCH]), "prelu": EI("prelu", [1, 1]),
        "alog": EI("alog", [32, 1]),
        "m_w2T": EI("m_w2T", [D, 2048]), "m_w3T": EI("m_w3T", [2048, D]),
        "lng": EI("lng", [128, 16]), "lnb": EI("lnb", [128, 16]),
        "fw0": EI("fw0", [D, D]), "fw1": EI("fw1", [D, D]), "fw2": EI("fw2", [D, D]),
        "fbias": EI("fbias", [128, 8]),
        "triu": EI("triu", [128, 128]), "ident": EI("ident", [128, 128]),
        "out": nc.dram_tensor("out", [D, TOK], F32, kind="ExternalOutput").ap(),
    }
    x1 = IN("ix1", [D, TOK])
    x2 = IN("ix2", [D, TOK])
    x3 = IN("ix3", [D, TOK])
    qp = IN("iqp", [D, TOK], BF16)
    num = IN("inum", [TOK, D], BF16)
    den = IN("iden", [128, 32])
    kv = IN("ikv", [D, D], BF16)
    z = IN("iz", [128, 8])
    kvg = IN("ikvg", [2 * D, D], BF16)
    zg = IN("izg", [256, 8])
    xc = IN("ixc", [2176, TOK], BF16)
    at = IN("iat", [32, TOK])
    states = IN("istates", [TOK // 128 + 1, 64, 2048])
    hxl = IN("ihxl", [D, 3])
    hxg = IN("ihxg", [2 * D, 3])
    stg = IN("istg", [128, 2048])
    xlns = IN("ixln", [D, TOK], BF16)

    scr = {
        "A": IN("scrA", [32, 128, 2048], BF16),
        "H1": IN("scrH1", [32, 128, 4096], BF16),
        "H2": IN("scrH2", [32, 128, 4096], BF16),
        "W3": IN("scrW3", [8, 128, 4096], BF16),
        "C": IN("scrC", [NCH, 128, 8704], BF16),
    }
    s = Sched(nc)
    C = common_tiles(s, nc)
    fence = (C["m_row"], C["v_row"])

    emit_a1(s, nc, {
        "x": io["x"], "WqT": io["WqT"], "WkT": io["WkT"], "WvT": io["WvT"],
        "bq": io["bq"], "bk": io["bk"], "bv_bc": io["bv_bc"],
        "triu": io["triu"], "ident": io["ident"],
        "qp": qp, "num": num, "den": den, "kv": kv, "z": z,
        "kv_defer": True}, C)
    tp_a1 = s.cnt.get("tP", 0)
    pv = emit_preconv(
        s, {"A": io["wA"], "H1": io["h1P"], "H2": io["h2P"], "W3": io["w3P"]},
        scr, groups=[("A", 32), ("H1", 32), ("H2", 32), ("W3", 8)],
        pace=(0, (tp_a1 * 9) // 10))
    kv_src, kv_vfin = s.kv_info
    s.do("gpsimd", lambda e: e.dma_start(
        out=kv.rearrange("(dc dp) m -> dp dc m", dp=128), in_=kv_src),
        wait=[("tV", kv_vfin)], inc="sO", inc_k=16)
    s.barrier(fence)
    s.do("gpsimd", lambda e: e.collective_compute(
        "AllGather", ALU.bypass, replica_groups=CC_GROUPS, ins=[kv], outs=[kvg]),
        inc="cc", inc_k=1)
    s.do("gpsimd", lambda e: e.collective_compute(
        "AllGather", ALU.bypass, replica_groups=CC_GROUPS, ins=[z], outs=[zg]),
        inc="cc", inc_k=1)
    s.barrier(fence)
    emit_a2(s, nc, {
        "x": io["x"], "qp": qp, "num": num, "den": den,
        "kvp": kvg[0:D, :], "zp": zg[0:128, :], "mask": io["mask"],
        "WoT": io["WoT"], "bo": io["bo"], "ident": io["ident"], "out": x1}, C)
    s.barrier(fence)
    tp_a2 = s.cnt.get("tP", 0)
    emit_f0(s, nc, {
        "x": x1, "s12": io["s12"], "sh12": io["sh12"],
        "s3": io["s3"], "out": x2, "scr": scr, "pv": pv,
        "tile_order": [NT - 1] + list(range(NT - 1)),
        "halo_cb": lambda w: s.do(
            "sync", lambda e: e.dma_start(out=hxl, in_=x2[:, TOK - 3:TOK]),
            wait=w, inc="dHx", inc_k=16)}, C)
    pv.update(emit_preconv(
        s, {"C": io["convP"]}, scr, groups=[("C", NCH)],
        pace=(tp_a2, tp_a2 + ((s.cnt.get("tP", 0) - tp_a2) * 8) // 10)))
    s.do("gpsimd", lambda e: e.collective_compute(
        "AllGather", ALU.bypass, replica_groups=CC_GROUPS, ins=[hxl], outs=[hxg]),
        wait=[("dHx", s.cnt.get("dHx", 16))], inc="cc", inc_k=1)
    s.barrier(fence)
    emit_m1(s, nc, {
        "x": x2, "xh": hxg[0:D, :], "hmask": io["mask"],
        "w1T": io["m_w1T"], "convb": io["convb"],
        "prelu": io["prelu"], "alog": io["alog"], "ident": io["ident"],
        "xc": xc, "at": at, "states": states, "scr": scr, "pv": pv,
        "xlns": xlns}, C)
    s.barrier(fence)
    s.do("gpsimd", lambda e: e.collective_compute(
        "AllGather", ALU.bypass, replica_groups=CC_GROUPS,
        ins=[states[TOK // 128]], outs=[stg]), inc="cc", inc_k=1)
    s.barrier(fence)
    emit_m2(s, nc, {
        "x": x2, "xc": xc, "at": at, "states": states,
        "hin": stg[0:64, :], "hmask": io["mask"],
        "w2T": io["m_w2T"], "w3T": io["m_w3T"], "lng": io["lng"], "lnb": io["lnb"],
        "triu": io["triu"], "ident": io["ident"], "out": x3,
        "xln_in": xlns}, C)
    scrF1 = IN("scrF1", [3, 128, 8 * D], BF16)
    for i, wnm in enumerate(("fw0", "fw1", "fw2")):
        s.do("gpsimd", lambda e, i=i, wnm=wnm: e.dma_start(
            out=scrF1[i].rearrange("p (kc m) -> p kc m", m=D),
            in_=io[wnm].rearrange("(kc kp) m -> kp kc m", kp=128)),
            inc="dF1p", inc_k=16)
    f1_pre = [("dF1p", s.cnt["dF1p"])]
    s.barrier(fence)
    emit_f1(s, nc, {
        "x": x3, "w0": io["fw0"], "w1": io["fw1"], "w2": io["fw2"],
        "bias": io["fbias"], "out": io["out"],
        "preW": (scrF1, f1_pre)}, C)
    s.run()
    s.stack.close()
    return nc


def kernel_fused(**inp):
    CH = 2176
    triu_ = np.triu(np.ones((128, 128), np.float32))
    ident_ = np.eye(128, dtype=np.float32)
    x = np.asarray(inp["L0_attn_x"], np.float32)
    nc = build_fused()
    s12_ = np.concatenate([pc(inp["L0_ffn_w1_s"]), pc(inp["L0_ffn_w2_s"])], axis=1)
    sh12_ = np.concatenate([pc(inp["L0_ffn_h1_s"]), pc(inp["L0_ffn_h2_s"])], axis=1)
    common = {
        "WqT": wT(inp["L0_attn_Wq"]), "WkT": wT(inp["L0_attn_Wk"]),
        "WvT": wT(inp["L0_attn_Wv"]), "WoT": wT(inp["L0_attn_Wo"]),
        "bq": pc(inp["L0_attn_bq"]), "bk": pc(inp["L0_attn_bk"]),
        "bo": pc(inp["L0_attn_bo"]),
        "bv_bc": np.ascontiguousarray(np.broadcast_to(
            np.asarray(inp["L0_attn_bv"], np.float32)[None, :], (128, 1024))),
        "wA": np.ascontiguousarray(np.concatenate(
            [panP(wT(inp["L0_ffn_w1_w"]), 8, 32), panP(wT(inp["L0_ffn_w2_w"]), 8, 32)],
            axis=2)),
        "h1P": panP(wT(inp["L0_ffn_h1_w"]), 32, 32),
        "h2P": panP(wT(inp["L0_ffn_h2_w"]), 32, 32),
        "w3P": panP(wT(inp["L0_ffn_w3_w"]), 32, 8),
        "s12": s12_, "sh12": sh12_, "s3": pc(inp["L0_ffn_w3_s"]),
        "m_w1T": wT(np.asarray(inp["L1_attn_W_in1"], np.float32)[2048:4256, :]),
        "convP": np.ascontiguousarray(
            np.asarray(inp["L1_attn_conv_w"], np.float32).transpose(1, 2, 0).reshape(CH, 4 * CH)
            .reshape(17, 128, 4, 17, 128).transpose(3, 1, 0, 2, 4).reshape(17, 128, 8704)),
        "convb": pc(inp["L1_attn_conv_b"]),
        "prelu": np.asarray(inp["L1_attn_prelu_a"], np.float32).reshape(1, 1),
        "alog": np.asarray(inp["L1_attn_A_log"], np.float32).reshape(32, 1),
        "m_w2T": wT(inp["L1_attn_W_in2"]), "m_w3T": wT(inp["L1_attn_W_in3"]),
        "lng": pc(inp["L1_attn_ln_g"]),
        "lnb": pc(inp["L1_attn_ln_b"]),
        "fw0": np.ascontiguousarray(np.asarray(inp["L1_ffn_weight"], np.float32)),
        "fw1": np.ascontiguousarray(np.asarray(inp["L1_ffn_aux_weight1"], np.float32)),
        "fw2": np.ascontiguousarray(np.asarray(inp["L1_ffn_aux_weight2"], np.float32)),
        "fbias": pc(inp["L1_ffn_bias"]),
        "triu": triu_, "ident": ident_,
    }
    in_maps = []
    for c in range(N_CORES):
        b, h = c // 2, c % 2
        m = dict(common)
        m["x"] = fm(x[b, h * TOK:(h + 1) * TOK, :])
        m["mask"] = np.full((128, 1), 1.0 if h == 1 else 0.0, np.float32)
        in_maps.append(m)
    res = run_bass_kernel_spmd(nc, in_maps, core_ids=list(range(N_CORES))).results
    out = np.empty((4, 8192, 1024), np.float32)
    for c in range(N_CORES):
        b, h = c // 2, c % 2
        out[b, h * TOK:(h + 1) * TOK, :] = np.asarray(res[c]["out"]).T
    return out


def kernel(**inp):
    CH = 2176
    epsc = np.full((1, 1), 1e-5, np.float32)
    triu_ = np.triu(np.ones((128, 128), np.float32))
    ident_ = np.eye(128, dtype=np.float32)

    x = np.asarray(inp["L0_attn_x"], np.float32)
    x_pc = []
    for c in range(N_CORES):
        b, h = c // 2, c % 2
        x_pc.append(fm(x[b, h * TOK:(h + 1) * TOK, :]))

    cores = list(range(N_CORES))

    nc = build_a1()
    bv_row_ = np.ascontiguousarray(np.broadcast_to(
        np.asarray(inp["L0_attn_bv"], np.float32)[None, :], (128, 1024)))
    in1 = [{
        "x": x_pc[c],
        "WqT": wT(inp["L0_attn_Wq"]), "WkT": wT(inp["L0_attn_Wk"]),
        "WvT": wT(inp["L0_attn_Wv"]),
        "bq": pc(inp["L0_attn_bq"]), "bk": pc(inp["L0_attn_bk"]), "bv_bc": bv_row_,
        "triu": triu_, "ident": ident_, "epsc": epsc,
    } for c in cores]
    r1 = run_bass_kernel_spmd(nc, in1, core_ids=cores).results

    nc = build_a2()
    zeros_kv = np.zeros((D, D), np.float32)
    zeros_z = np.zeros((128, 8), np.float32)
    in2 = [{
        "x": x_pc[c],
        "qp": r1[c]["qp"], "num": r1[c]["num"], "den": r1[c]["den"],
        "kvp": r1[c - 1]["kv"] if c % 2 == 1 else zeros_kv,
        "zp": r1[c - 1]["z"] if c % 2 == 1 else zeros_z,
        "WoT": wT(inp["L0_attn_Wo"]), "bo": pc(inp["L0_attn_bo"]), "ident": ident_,
    } for c in cores]
    r2 = run_bass_kernel_spmd(nc, in2, core_ids=cores).results
    x1_pc = [np.ascontiguousarray(r2[c]["out"]) for c in cores]

    nc = build_f0()
    s12_ = np.concatenate([pc(inp["L0_ffn_w1_s"]), pc(inp["L0_ffn_w2_s"])], axis=1)
    sh12_ = np.concatenate([pc(inp["L0_ffn_h1_s"]), pc(inp["L0_ffn_h2_s"])], axis=1)
    in3 = [{
        "x": x1_pc[c],
        "w1T": wT(inp["L0_ffn_w1_w"]), "w2T": wT(inp["L0_ffn_w2_w"]),
        "h1T": wT(inp["L0_ffn_h1_w"]), "h2T": wT(inp["L0_ffn_h2_w"]),
        "w3T": wT(inp["L0_ffn_w3_w"]),
        "s12": s12_, "sh12": sh12_, "s3": pc(inp["L0_ffn_w3_s"]), "epsc": epsc,
    } for c in cores]
    r3 = run_bass_kernel_spmd(nc, in3, core_ids=cores).results
    x2_pc = [np.ascontiguousarray(r3[c]["out"]) for c in cores]

    w1T_ = wT(np.asarray(inp["L1_attn_W_in1"], np.float32)[2048:4256, :])
    convT_ = np.ascontiguousarray(
        np.asarray(inp["L1_attn_conv_w"], np.float32).transpose(1, 2, 0).reshape(CH, 4 * CH))
    nc = build_m1()
    in4 = []
    for c in cores:
        halo = x2_pc[c - 1][:, TOK - 3:TOK] if c % 2 == 1 else np.zeros((D, 3), np.float32)
        in4.append({
            "x": np.ascontiguousarray(np.concatenate([halo, x2_pc[c]], axis=1)),
            "w1T": w1T_, "convT": convT_,
            "convb": pc(inp["L1_attn_conv_b"]),
            "prelu": np.asarray(inp["L1_attn_prelu_a"], np.float32).reshape(1, 1),
            "alog": np.asarray(inp["L1_attn_A_log"], np.float32).reshape(32, 1),
            "ident": ident_, "epsc": epsc,
        })
    r4 = run_bass_kernel_spmd(nc, in4, core_ids=cores).results

    nc = build_m2()
    zeros_h = np.zeros((64, 2048), np.float32)
    lng_ = np.ascontiguousarray(np.broadcast_to(
        np.asarray(inp["L1_attn_ln_g"], np.float32)[None, :], (128, 2048)))
    lnb_ = np.ascontiguousarray(np.broadcast_to(
        np.asarray(inp["L1_attn_ln_b"], np.float32)[None, :], (128, 2048)))
    in5 = [{
        "x": x2_pc[c],
        "xc": r4[c]["xc"], "at": r4[c]["at"], "states": r4[c]["states"],
        "hin": r4[c - 1]["states"][TOK // 128] if c % 2 == 1 else zeros_h,
        "w2T": wT(inp["L1_attn_W_in2"]), "w3T": wT(inp["L1_attn_W_in3"]),
        "lng": lng_, "lnb": lnb_, "triu": triu_, "ident": ident_, "epsc": epsc,
    } for c in cores]
    r5 = run_bass_kernel_spmd(nc, in5, core_ids=cores).results
    x3_pc = [np.ascontiguousarray(r5[c]["out"]) for c in cores]

    nc = build_f1()
    in6 = [{
        "x": x3_pc[c],
        "w0": np.ascontiguousarray(np.asarray(inp["L1_ffn_weight"], np.float32)),
        "w1": np.ascontiguousarray(np.asarray(inp["L1_ffn_aux_weight1"], np.float32)),
        "w2": np.ascontiguousarray(np.asarray(inp["L1_ffn_aux_weight2"], np.float32)),
        "bias": pc(inp["L1_ffn_bias"]), "epsc": epsc,
    } for c in cores]
    r6 = run_bass_kernel_spmd(nc, in6, core_ids=cores).results

    out = np.empty((4, 8192, 1024), np.float32)
    for c in cores:
        b, h = c // 2, c % 2
        out[b, h * TOK:(h + 1) * TOK, :] = np.asarray(r6[c]["out"]).T
    return out





kernel_unfused = kernel
kernel = kernel_fused
